# version 04
# baseline (speedup 1.0000x reference)
import numpy as np
import ml_dtypes
from contextlib import ExitStack

import concourse.bass as bass
import concourse.mybir as mybir
from concourse.alu_op_type import AluOpType as OP
from concourse.bass_utils import run_bass_kernel_spmd

F32 = mybir.dt.float32
F32R = mybir.dt.float32r
BF16 = mybir.dt.bfloat16
I8 = mybir.dt.int8
AF = mybir.ActivationFunctionType
AX = mybir.AxisListType

D = 1024
DH = 4096
SEQ = 4096
TOK = 2048
W = 2050
WP = 2176
NT = 17
SPW = 256
NSP = 8
WR_SPANS = [(0, 384), (384, 384), (768, 384), (1152, 384), (1536, 384), (1920, 130)]
ENGS = ("sync", "tensor", "vector", "scalar", "gpsimd")


class Sem:
    def __init__(self, h):
        self.h = h
        self.n = 0


class B:
    def __init__(self):
        self.ev = {e: [] for e in ENGS}

    def em(self, eng, fn, sem=None, inc=1, selfwait=False):
        if sem is not None:
            sem.n += inc
            def wrapped(q, fn=fn, s=sem.h, k=inc):
                fn(q).then_inc(s, k)
            self.ev[eng].append(wrapped)
            if selfwait:
                v = sem.n
                self.ev[eng].append(lambda q, s=sem.h, v=v: q.wait_ge(s, v))
        else:
            self.ev[eng].append(lambda q, fn=fn: fn(q))

    def wait(self, eng, sem, val=None):
        v = sem.n if val is None else val
        if v > 0:
            self.ev[eng].append(lambda q, s=sem.h, v=v: q.wait_ge(s, v))

    def dma(self, eng, out, in_, sem):
        self.em(eng, lambda q, o=out, i=in_: q.dma_start(out=o, in_=i), sem, 16)


def build_program(upto="full", nheads=8, attn_level=3, fpart=4):
    nc = bass.Bass("TRN2", target_bir_lowering=False)

    xin = nc.dram_tensor("xin", [WP, D], F32, kind="ExternalInput")
    w1t = nc.dram_tensor("w1t", [32, 128, 1024], F32, kind="ExternalInput")
    w2t = nc.dram_tensor("w2t", [8, 128, 4096], F32, kind="ExternalInput")
    cwi = nc.dram_tensor("cwi", [128, 128], F32, kind="ExternalInput")
    cbi = nc.dram_tensor("cbi", [128, 32], F32, kind="ExternalInput")
    wqkvi = nc.dram_tensor("wqkvi", [128, 192], F32, kind="ExternalInput")
    bqkvi = nc.dram_tensor("bqkvi", [128, 3], F32, kind="ExternalInput")
    bvbi = nc.dram_tensor("bvbi", [128, 65], F32, kind="ExternalInput")
    wri = nc.dram_tensor("wri", [1024, 1024], F32, kind="ExternalInput")
    bri = nc.dram_tensor("bri", [1, 1024], F32, kind="ExternalInput")
    ideni = nc.dram_tensor("ideni", [128, 128], F32, kind="ExternalInput")
    mski = nc.dram_tensor("mski", [128, 4], F32, kind="ExternalInput")

    xout = nc.dram_tensor("xout", [TOK, D], F32, kind="ExternalOutput")

    m1d = nc.dram_tensor("m1d", [32, 128, 1024], BF16)
    m2d = nc.dram_tensor("m2d", [8, 128, 4096], BF16)
    x1d = nc.dram_tensor("x1d", [WP, D], F32)
    x2d = nc.dram_tensor("x2d", [WP, D], F32)
    aod = nc.dram_tensor("aod", [512, 4096], BF16)
    rowd = nc.dram_tensor("rowd", [17, 128], F32)
    y2ci = nc.dram_tensor("y2ci", [512, TOK], BF16)
    y2go = nc.dram_tensor("y2go", [1024, TOK], BF16)
    aoci = nc.dram_tensor("aoci", [514, 2050], BF16)
    aogo = nc.dram_tensor("aogo", [1028, 2050], BF16)

    bld = B()

    with ExitStack() as ctx:
        e = ctx.enter_context

        identf = e(nc.sbuf_tensor("identf", [128, 128], F32))
        identb = e(nc.sbuf_tensor("identb", [128, 128], BF16))
        cw = e(nc.sbuf_tensor("cw", [128, 128], F32))
        cb = e(nc.sbuf_tensor("cb", [128, 32], F32))
        bqk = e(nc.sbuf_tensor("bqk", [128, 3], F32))
        bq8 = e(nc.sbuf_tensor("bq8", [128, 2], F32))
        wqkf = e(nc.sbuf_tensor("wqkf", [128, 192], F32))
        wqkb = e(nc.sbuf_tensor("wqkb", [128, 192], BF16))
        wv65 = e(nc.sbuf_tensor("wv65", [128, 65], BF16))
        bvb = e(nc.sbuf_tensor("bvb", [128, 65], F32))
        brrow = e(nc.sbuf_tensor("brrow", [1, 1024], F32))
        brb = e(nc.sbuf_tensor("brb", [1, 1024], BF16))
        msk = e(nc.sbuf_tensor("msk", [128, 4], F32))
        onesf = e(nc.sbuf_tensor("onesf", [1, 128], F32))
        onescol = e(nc.sbuf_tensor("onescol", [128, 1], F32))
        onesb = e(nc.sbuf_tensor("onesb", [128, 512], BF16))
        epsv = e(nc.sbuf_tensor("epsv", [128, 1], F32))
        wsc = e(nc.sbuf_tensor("wsc", [128, 4], F32))
        scr1 = e(nc.sbuf_tensor("scr1", [1, 8], F32))
        stA = e(nc.sbuf_tensor("stA", [128, 8 * NT], F32))
        o17 = e(nc.sbuf_tensor("o17", [17, 128], F32))
        oscrow = e(nc.sbuf_tensor("oscrow", [1, WP], F32))
        s2row = e(nc.sbuf_tensor("s2row", [1, 256], F32))
        o2c = e(nc.sbuf_tensor("o2c", [128, 4], F32))
        xmsc = e(nc.sbuf_tensor("xmsc", [128, 1024], F32))
        sqsc = e(nc.sbuf_tensor("sqsc", [128, 1024], BF16))
        ntmb = e(nc.sbuf_tensor("ntmb", [128, 1024], BF16))
        oscb = e(nc.sbuf_tensor("oscb", [128, 288], F32))
        s2b = e(nc.sbuf_tensor("s2b", [128, 256], F32))
        vsufA = e(nc.sbuf_tensor("vsufA", [1, 80], F32))
        vsufB = e(nc.sbuf_tensor("vsufB", [1, 80], BF16))
        rcpz = e(nc.sbuf_tensor("rcpz", [1, 512], F32))
        rcpb = e(nc.sbuf_tensor("rcpb", [1, 1024], BF16))
        wabs = e(nc.sbuf_tensor("wabs", [128, 1024], BF16))
        halo = e(nc.sbuf_tensor("halo", [1, 1024], F32))
        hbf = e(nc.sbuf_tensor("hbf", [1, 4096], BF16))
        wrB = e(nc.sbuf_tensor("wrB", [128, 8 * 1024], BF16))
        vext2 = wrB[:, 0:32 * 65]
        Ebuf2 = wrB[:, 2080:2080 + 2 * 1536]

        AR = 132 * 1024
        MAN_TOP = 229344
        MAN_BASE = MAN_TOP - AR
        _vn = [0]

        def V(off, nelem, dt, parts=128):
            esz = {F32: 4, F32R: 4, BF16: 2, I8: 1}[dt]
            _vn[0] += 1
            h = nc.alloc_sbuf_tensor_at(f"ov{_vn[0]}", [128, nelem], dt,
                                        offset=MAN_BASE + off)
            ap = h[:, :] if parts == 128 else h[0:parts, :]
            return ap, off + ((nelem * esz + 31) // 32) * 32

        off = 0
        nfm, off = V(off, 8 * WP, I8)
        PERM_END = off
        LIM = AR

        off = PERM_END
        silub, off = V(off, 32 * SPW, F32)
        n2b, off = V(off, 32 * SPW, BF16)
        fsb, off = V(off, 8 * SPW, F32)
        m1sb, off = V(off, 4 * 1024, BF16)
        m2w, off = V(off, 2 * 4096, BF16)
        xt, off = V(off, 2 * 1024, F32)
        xtm, off = V(off, 2 * 1024, F32)
        nbr, off = V(off, 8 * 258, BF16)
        gb, off = V(off, 2 * 258, F32)
        c2b, off = V(off, 2 * 256, F32)
        c3b, off = V(off, 2 * 256, F32)
        rmx4, off = V(off, 4 * 256, F32)
        rmxf, off = V(off, 256, F32)
        n28, off = V(off, 8 * 256, I8)
        assert off <= LIM, off

        off = PERM_END
        y2hh, off = V(off, 4 * SEQ, BF16)
        ATT2 = off
        y2fm, off = V(off, 4 * TOK, BF16)
        y2own, off = V(off, 4 * TOK, BF16)
        mrg, off = V(off, 4 * TOK, BF16)
        mgs, off = V(off, 2048, BF16)
        assert off <= LIM, off
        off = ATT2
        qk2, off = V(off, 2 * SEQ, BF16)
        vext, off = V(off, 32 * 65, BF16)
        Ebuf, off = V(off, 2 * 1536, BF16)
        aosb, off = V(off, 2 * 512, BF16)
        assert off <= LIM, off

        off = PERM_END
        candA, off = V(off, 4 * 2050, BF16)
        candB, off = V(off, 4 * 2050, BF16)
        ownw, off = V(off, 4 * 2050, BF16)
        awin, off = V(off, 8 * 2050, BF16)
        wfs, off = V(off, 8 * 384, F32)
        wxt, off = V(off, 1024, F32)
        wxm, off = V(off, 1024, F32)
        assert off <= LIM, off

        off = PERM_END
        wqa, off = V(off, 4096, F32)
        wqb, off = V(off, 4096, F32)
        wq8, off = V(off, 4096, I8)
        wqm, off = V(off, 4096, BF16)
        wrstg, off = V(off, 1024, F32)
        assert off <= LIM, off

        ps = e(nc.psum_tensor("ps", [128, 4096], F32))

        def bank(i, w=512):
            return ps[:, 512 * i:512 * i + w]

        S = {}
        for name in ["sLD", "sPE", "sDV", "sAC", "sPL", "sST", "sCC",
                     "sB1", "sB2", "sB3", "sB4", "sB5",
                     "sW0", "sW1", "sX0", "sX1", "sXG", "sBC", "sT0", "sT1",
                     "sM1a", "sM1b", "sM1c", "sM1d", "sM2a", "sM2b"]:
            S[name] = Sem(e(nc.semaphore(name)))
        SW = [S["sW0"], S["sW1"]]
        SX = [S["sX0"], S["sX1"]]
        SM1 = [S["sM1a"], S["sM1b"], S["sM1c"], S["sM1d"]]
        SM2 = [S["sM2a"], S["sM2b"]]

        block = e(nc.Block())
        fillreg = {}
        bld.ev["gpsimd"].append(lambda q: fillreg.__setitem__("one", q.to_reg(1.0)))

        def barrier():
            bs = [S["sB1"], S["sB2"], S["sB3"], S["sB4"], S["sB5"]]
            for eng, s in zip(ENGS, bs):
                if eng in ("vector", "gpsimd"):
                    bld.em(eng, lambda q: q.engine_nop(), s, 1)
                else:
                    bld.em(eng, lambda q: q.nop(), s, 1)
            for eng in ENGS:
                for s in bs:
                    bld.wait(eng, s)

        def mm(out, lhsT, rhs, start, stop, sem=None):
            bld.em("tensor", lambda q, o=out, l=lhsT, r=rhs, a=start, b=stop:
                   q.matmul(o, l, r, start=a, stop=b), sem, 1)

        def tr(out, in_, ident, sem=None):
            bld.em("tensor", lambda q, o=out, i=in_, d=ident:
                   q.transpose(o, i, d), sem, 1)

        def _narrow(out):
            try:
                return out.free_size < 256
            except Exception:
                return True

        def act(out, in_, func, bias=None, scale=None, accum=None, sem=None):
            def f(q, o=out, i=in_, fn=func, b=bias, s=scale, a=accum):
                kw = {}
                if b is not None:
                    kw["bias"] = b
                if s is not None:
                    kw["scale"] = s
                if a is not None:
                    kw["accum_out"] = a
                return q.activation(out=o, in_=i, func=fn, **kw)
            bld.em("scalar", f, sem, 1, selfwait=_narrow(out) or accum is not None)

        def vop(kind, *a, sem=None, eng="vector", **kw):
            out = a[0] if a else None
            bld.em(eng, lambda q, k=kind, aa=a, kk=kw: getattr(q, k)(*aa, **kk), sem, 1,
                   selfwait=_narrow(out))

        def sdma(out, in_, eng="scalar", sem=None):
            bld.em(eng, lambda q, o=out, i=in_: q.dma_start(out=o, in_=i),
                   S["sST"] if sem is None else sem, 16)

        for out, src in [(identf[:], ideni[:, :]), (cw[:], cwi[:, :]), (cb[:], cbi[:, :]),
                         (bvb[:], bvbi[:, :]), (brrow[:], bri[:, :]), (msk[:], mski[:, :]),
                         (bqk[:], bqkvi[:, :]), (wqkf[:], wqkvi[:, :])]:
            bld.dma("sync", out, src, S["sLD"])
        bld.wait("vector", S["sLD"])
        vop("tensor_copy", identb[:], identf[:], sem=S["sDV"])
        vop("memset", onesf[:], 1.0, sem=S["sDV"])
        vop("memset", onescol[:], 1.0, sem=S["sDV"])


        vop("memset", onesb[:], 1.0, sem=S["sDV"])
        vop("memset", epsv[:], 1e-5, sem=S["sDV"])
        vop("tensor_scalar", bq8[:, 0:1], bqk[:, 0:1], 0.125, 1.0,
            op0=OP.mult, op1=OP.mult, sem=S["sDV"])
        vop("tensor_copy", bq8[:, 1:2], bqk[:, 1:2], sem=S["sDV"])
        vop("tensor_copy", wqkb[:], wqkf[:], sem=S["sDV"])
        vop("memset", wv65[:], 0.0, sem=S["sDV"])
        vop("tensor_copy", brb[:], brrow[:], sem=S["sDV"])
        bld.wait("vector", S["sDV"])
        vop("tensor_copy", wv65[:, 0:64], wqkb[:, 128:192], sem=S["sDV"])

        for kc in range(8):
            bld.wait("sync", S["sDV"])
            bld.dma("sync", wrstg[:, :], wri[128 * kc:128 * (kc + 1), :], S["sLD"])
            bld.wait("vector", S["sLD"])
            vop("tensor_copy", wrB[:, 1024 * kc:1024 * (kc + 1)], wrstg[:, :], sem=S["sDV"])

        def wmean(wt, ntiles, wd, bank_i):
            absdone = []
            for qn in ("sync",):
                bld.wait(qn, S["sAC"])
                bld.wait(qn, S["sDV"])
            for i in range(ntiles):
                buf = wqa if i % 2 == 0 else wqb
                qn = "sync" if i % 2 == 0 else "scalar"
                if i >= 2:
                    bld.wait(qn, S["sAC"], absdone[i - 2])
                bld.dma(qn, buf[:, 0:wd], wt[i, :, :], SW[i % 2])
                bld.wait("scalar", SW[i % 2])
                for g in range(0, wd, 1024):
                    bld.wait("scalar", S["sPE"])
                    act(wabs[:, 0:1024], buf[:, g:g + 1024], AF.Abs, sem=S["sAC"])
                    bld.wait("tensor", S["sAC"])
                    for g2 in range(0, 1024, 512):
                        first = (i == 0 and g == 0 and g2 == 0)
                        last = (i == ntiles - 1 and g + 1024 == wd and g2 == 512)
                        mm(bank(bank_i)[0:1, :], onesb[:, 0:1],
                           wabs[:, g2:g2 + 512], first, last,
                           sem=S["sPE"])
                absdone.append(S["sAC"].n)
        wmean(w1t, 32, 1024, 0)
        wmean(w2t, 8, 4096, 1)

        bld.wait("vector", S["sPE"])
        vop("reduce_sum", scr1[0:1, 0:1], bank(0)[0:1, :], AX.X, sem=S["sDV"])
        vop("reduce_sum", scr1[0:1, 1:2], bank(1)[0:1, :], AX.X, sem=S["sDV"])
        vop("tensor_scalar", scr1[0:1, 2:3], scr1[0:1, 0:1], 1.0 / (DH * D), 1e-5,
            op0=OP.mult, op1=OP.max, sem=S["sDV"])
        vop("tensor_scalar", scr1[0:1, 3:4], scr1[0:1, 1:2], 1.0 / (DH * D), 1e-5,
            op0=OP.mult, op1=OP.max, sem=S["sDV"])
        vop("reciprocal", scr1[0:1, 4:5], scr1[0:1, 2:3], sem=S["sDV"])
        vop("reciprocal", scr1[0:1, 5:6], scr1[0:1, 3:4], sem=S["sDV"])
        bld.wait("tensor", S["sDV"])
        for j, col in enumerate([4, 2, 5, 3]):
            mm(bank(2)[:, j:j + 1], onesf[:, :], scr1[0:1, col:col + 1], True, True,
               sem=(S["sPE"] if j == 3 else None))
        bld.wait("vector", S["sPE"])
        vop("tensor_copy", wsc[:, 0:4], bank(2)[:, 0:4], sem=S["sDV"])

        def wquant(wt, md, ntiles, wd, wscol):
            qdone = []
            for qn in ("sync",):
                bld.wait(qn, S["sAC"])
                bld.wait(qn, S["sDV"])
                bld.wait(qn, S["sPL"])
            for i in range(ntiles):
                buf = wqa if i % 2 == 0 else wqb
                qn = "sync" if i % 2 == 0 else "scalar"
                if i >= 2:
                    bld.wait(qn, S["sDV"], qdone[i - 2])
                bld.dma(qn, buf[:, 0:wd], wt[i, :, :], SW[i % 2])
                bld.wait("vector", SW[i % 2])
                vop("tensor_scalar", wq8[:, 0:wd], buf[:, 0:wd], wsc[:, wscol:wscol + 1],
                    1.0, op0=OP.mult, op1=OP.mult, sem=S["sDV"])
                qdone.append(S["sDV"].n)
                bld.wait("vector", S["sST"])
                vop("tensor_scalar", wqm[:, 0:wd], wq8[:, 0:wd], -1.0, 1.0,
                    op0=OP.max, op1=OP.min, sem=S["sDV"])
                bld.wait("scalar", S["sDV"])
                sdma(md[i, :, :], wqm[:, 0:wd])
        bld.wait("vector", S["sDV"])
        wquant(w1t, m1d, 32, 1024, 0)
        wquant(w2t, m2d, 8, 4096, 2)

        vop("memset", xmsc[:, 0:1024], 0.0, sem=S["sDV"])
        bld.wait("scalar", S["sDV"])
        r = 2050
        while r < WP:
            nr = min(126, WP - r)
            sdma(x2d[r:r + nr, :], xmsc[0:nr, 0:1024])
            r += nr
        sdma(x1d[0:1, :], xmsc[0:1, 0:1024])
        sdma(x1d[2049:2050, :], xmsc[0:1, 0:1024])
        bld.wait("scalar", S["sST"])
        barrier()

        def ln_quant_entry(xsrc):
            P = NT
            SXE = [SX[0], S["sXG"]]
            a_dv = []
            a_ac = []
            a_pl = []
            for t in range(NT):
                slot = t % 2
                qn = "sync" if slot == 0 else "scalar"
                if t >= 2:
                    bld.wait(qn, S["sDV"], a_dv[t - 2])
                    bld.wait(qn, S["sAC"], a_ac[t - 2])
                    bld.wait(qn, S["sPL"], a_pl[t - 2])
                bld.dma(qn, xt[:, 1024 * slot:1024 * (slot + 1)],
                        xsrc[128 * t:128 * (t + 1), :], SXE[slot])
                xtile = xt[:, 1024 * slot:1024 * (slot + 1)]
                bld.wait("vector", SXE[slot])
                vop("reduce_sum", stA[:, 0 * P + t:0 * P + t + 1], xtile, AX.X, sem=S["sDV"])
                vop("tensor_scalar", stA[:, 2 * P + t:2 * P + t + 1],
                    stA[:, 0 * P + t:0 * P + t + 1], 1.0 / D, 1.0,
                    op0=OP.mult, op1=OP.mult, sem=S["sDV"])
                bld.wait("scalar", SXE[slot])
                act(sqsc[:, 0:1024], xtile, AF.Square,
                    accum=stA[:, 1 * P + t:1 * P + t + 1], sem=S["sAC"])
                a_ac.append(S["sAC"].n)
                bld.wait("gpsimd", SXE[slot])
                bld.wait("gpsimd", S["sDV"])
                vop("tensor_scalar", silub[:, 1024 * slot:1024 * (slot + 1)], xtile,
                    stA[:, 2 * P + t:2 * P + t + 1], 1.0,
                    op0=OP.subtract, op1=OP.mult, sem=S["sPL"], eng="gpsimd")
                a_pl.append(S["sPL"].n)
                bld.wait("vector", S["sPL"], a_pl[-1])
                vop("reduce_max", stA[:, 5 * P + t:5 * P + t + 1],
                    silub[:, 1024 * slot:1024 * (slot + 1)], AX.X,
                    apply_absolute_value=True, sem=S["sDV"])
                a_dv.append(S["sDV"].n)
            bld.wait("vector", S["sAC"])
            pl = lambda i: stA[:, i * P:(i + 1) * P]
            vop("tensor_tensor", pl(3), pl(2), pl(2), OP.mult, sem=S["sDV"])
            vop("tensor_scalar", pl(4), pl(1), 1.0 / D, 1.0, op0=OP.mult, op1=OP.mult, sem=S["sDV"])
            vop("tensor_tensor", pl(3), pl(4), pl(3), OP.subtract, sem=S["sDV"])
            bld.wait("scalar", S["sDV"])
            act(xmsc[:, 0:P], pl(3), AF.Ln, bias=epsv[:, 0:1], sem=S["sAC"])
            act(pl(4), xmsc[:, 0:P], AF.Exp, scale=-0.5, sem=S["sAC"])
            bld.wait("vector", S["sAC"])
            vop("tensor_tensor", pl(6), pl(4), pl(5), OP.mult, sem=S["sDV"])
            vop("tensor_scalar", pl(5), pl(6), 1e-5, 1.0, op0=OP.max, op1=OP.mult, sem=S["sDV"])
            vop("reciprocal", pl(6), pl(5), sem=S["sDV"])
            vop("tensor_tensor", pl(6), pl(6), pl(4), OP.mult, sem=S["sDV"])
            vop("tensor_scalar", pl(6), pl(6), 127.0, 1.0, op0=OP.mult, op1=OP.mult, sem=S["sDV"])
            vop("tensor_scalar", pl(7), pl(5), wsc[:, 1:2], 1.0 / 127.0,
                op0=OP.mult, op1=OP.mult, sem=S["sDV"])
            bld.wait("tensor", S["sDV"])
            bld.wait("tensor", S["sAC"])
            tr(ps[0:NT, 3584:3584 + 128], pl(7), identf[:], sem=S["sPE"])
            bld.wait("scalar", S["sPE"])
            act(o17[:, :], ps[0:NT, 3584:3584 + 128], AF.Copy, sem=S["sAC"])
            bld.wait("gpsimd", S["sAC"])
            bld.em("gpsimd", lambda q: q.dma_start(out=rowd[:, :], in_=o17[:, :]), S["sBC"], 16)
            bld.wait("gpsimd", S["sBC"])
            bld.em("gpsimd", lambda q: q.dma_start(
                out=oscrow[0:1, :], in_=rowd[:, :].rearrange("(o p) n -> o (p n)", o=1)),
                S["sBC"], 16)
            oscrow_ready = S["sBC"].n
            bld.wait("tensor", S["sAC"])
            trd = []
            b_pl = []
            b_cast = []
            b_trl = []
            for t in range(NT):
                slot = t % 2
                qn = "sync" if slot == 0 else "scalar"
                if t >= 2:
                    bld.wait(qn, S["sPL"], b_pl[t - 2])
                else:
                    bld.wait(qn, S["sDV"], a_dv[16 - t])
                    bld.wait(qn, S["sAC"], a_ac[16 - t])
                    bld.wait(qn, S["sPL"], a_pl[16 - t])
                bld.dma(qn, xt[:, 1024 * slot:1024 * (slot + 1)],
                        xsrc[128 * t:128 * (t + 1), :], SXE[slot])
                xtile = xt[:, 1024 * slot:1024 * (slot + 1)]
                bld.wait("gpsimd", SXE[slot])
                if t == 0:
                    bld.wait("gpsimd", S["sDV"])
                if t >= 2:
                    bld.wait("gpsimd", S["sAC"], b_cast[t - 2])
                vop("tensor_scalar", n28[:, 1024 * slot:1024 * (slot + 1)], xtile,
                    stA[:, 2 * P + t:2 * P + t + 1], stA[:, 6 * P + t:6 * P + t + 1],
                    op0=OP.subtract, op1=OP.mult, sem=S["sPL"], eng="gpsimd")
                b_pl.append(S["sPL"].n)
                bld.wait("scalar", S["sPL"], b_pl[-1])
                if t >= 2:
                    bld.wait("scalar", S["sPE"], b_trl[t - 2])
                act(n2b[:, 1024 * slot:1024 * (slot + 1)],
                    n28[:, 1024 * slot:1024 * (slot + 1)], AF.Copy, sem=S["sAC"])
                b_cast.append(S["sAC"].n)
                bld.wait("tensor", S["sAC"], b_cast[-1])
                for k in range(8):
                    pslot = (ps[:, 3072:3072 + 256] if k % 2 == 0
                             else ps[:, 3584:3584 + 256]).bitcast(BF16)[:, 0:128]
                    if len(trd) >= 2:
                        bld.wait("tensor", trd[-2][0], trd[-2][1])
                    tr(pslot, n2b[:, 1024 * slot + 128 * k:1024 * slot + 128 * (k + 1)],
                       identb[:], sem=S["sPE"])
                    dst = nfm[:, WP * k + 128 * t:WP * k + 128 * (t + 1)]
                    if k % 2 == 0:
                        bld.wait("vector", S["sPE"])
                        vop("tensor_copy", dst, pslot, sem=S["sDV"])
                        trd.append((S["sDV"], S["sDV"].n))
                    else:
                        bld.wait("scalar", S["sPE"])
                        act(dst, pslot, AF.Copy, sem=S["sAC"])
                        trd.append((S["sAC"], S["sAC"].n))
                b_trl.append(S["sPE"].n)
            bld.wait("tensor", S["sDV"])
            bld.wait("tensor", S["sAC"])
            return oscrow_ready

        m1done = []
        m2done = []
        FPS = {"fd": [0] * 8, "resid": 0, "wave": [0] * 8, "hlast": 0, "xtst": [0, 0]}

        def ffn_call(xsrc, resdst, res_off):
            oscrow_ready = ln_quant_entry(xsrc)
            for si in range(NSP):
                o = 1 + SPW * si
                hw = SPW + 2
                bld.wait("tensor", S["sBC"], oscrow_ready)
                bld.wait("tensor", S["sAC"], FPS["fd"][6])
                mm(bank(6)[:, 0:hw], onesf[:, :], oscrow[0:1, o - 1:o - 1 + hw],
                   True, True, sem=S["sPE"])
                bld.wait("scalar", S["sPE"])
                act(oscb[:, 0:hw], bank(6)[:, 0:hw], AF.Copy, sem=S["sAC"])
                oscb_ready = S["sAC"].n
                bld.wait("gpsimd", S["sPE"], FPS["hlast"])
                for kk in range(8):
                    vop("tensor_copy", nbr[:, 258 * kk:258 * kk + hw],
                        nfm[:, WP * kk + o - 1:WP * kk + o - 1 + hw],
                        sem=S["sPL"], eng="gpsimd")
                nbr_ready = S["sPL"].n
                h_sem = []
                evg = []
                evA = []
                evC = []
                evS = []

                def emit_BC_silu(hc):
                    slot = hc % 2
                    gs = gb[:, 258 * slot:258 * slot + hw]
                    cs = c2b[:, 256 * slot:256 * (slot + 1)]
                    c3s = c3b[:, 256 * slot:256 * (slot + 1)]
                    bld.wait("vector", S["sPL"], evA[hc])
                    vop("scalar_tensor_tensor", cs, gs[:, 0:SPW],
                        cw[:, 4 * hc:4 * hc + 1], cs,
                        op0=OP.mult, op1=OP.add, sem=S["sDV"])
                    if hc >= 2:
                        bld.wait("vector", S["sAC"], evS[hc - 2])
                    vop("scalar_tensor_tensor", c3s, gs[:, 2:2 + SPW],
                        cw[:, 4 * hc + 2:4 * hc + 3], cs,
                        op0=OP.mult, op1=OP.add, sem=S["sDV"])
                    evC.append(S["sDV"].n)
                    bld.wait("scalar", S["sDV"], evC[hc])
                    act(silub[:, SPW * hc:SPW * hc + SPW], c3s, AF.Silu,
                        bias=cb[:, hc:hc + 1], sem=S["sAC"])
                    evS.append(S["sAC"].n)

                def emit_rmx(p):
                    bld.wait("vector", S["sAC"], evS[8 * p + 7])
                    vop("tensor_reduce", rmx4[:, 256 * p:256 * (p + 1)],
                        silub[:, 2048 * p:2048 * (p + 1)].rearrange(
                            "p (hc t) -> p t hc", hc=8),
                        AX.X, OP.max, apply_absolute_value=True, sem=S["sDV"])

                for hc in range(32):
                    mslot = len(m1done) % 4
                    if len(m1done) >= 4:
                        bld.wait("sync", S["sPE"], m1done[-4])
                    bld.dma("sync", m1sb[:, 1024 * mslot:1024 * (mslot + 1)],
                            m1d[hc, :, :], SM1[mslot])
                    hps = bank(hc % 2)[:, 0:hw]
                    bld.wait("tensor", SM1[mslot])
                    if hc == 0:
                        bld.wait("tensor", S["sPL"], nbr_ready)
                        bld.wait("tensor", S["sAC"], FPS["fd"][1])
                    if hc >= 2:
                        bld.wait("tensor", S["sDV"], evg[hc - 2])
                    for k in range(8):
                        mm(hps, m1sb[:, 1024 * mslot + 128 * k:1024 * mslot + 128 * (k + 1)],
                           nbr[:, 258 * k:258 * k + hw],
                           start=(k == 0), stop=(k == 7), sem=(S["sPE"] if k == 7 else None))
                    m1done.append(S["sPE"].n)
                    h_sem.append(S["sPE"].n)
                    bld.wait("vector", S["sPE"], h_sem[hc])
                    if hc == 0:
                        bld.wait("vector", S["sAC"], oscb_ready)
                    if hc >= 2:
                        bld.wait("vector", S["sPL"], evA[hc - 2])
                    vop("tensor_tensor", gb[:, 258 * (hc % 2):258 * (hc % 2) + hw],
                        hps, oscb[:, 0:hw], OP.mult, sem=S["sDV"])
                    evg.append(S["sDV"].n)
                    bld.wait("gpsimd", S["sDV"], evg[hc])
                    if hc >= 2:
                        bld.wait("gpsimd", S["sDV"], evC[hc - 2])
                    vop("tensor_scalar", c2b[:, 256 * (hc % 2):256 * (hc % 2 + 1)],
                        gb[:, 258 * (hc % 2) + 1:258 * (hc % 2) + 1 + SPW],
                        cw[:, 4 * hc + 1:4 * hc + 2], 1.0,
                        op0=OP.mult, op1=OP.mult, sem=S["sPL"], eng="gpsimd")
                    evA.append(S["sPL"].n)
                    if hc >= 1:
                        emit_BC_silu(hc - 1)
                    if hc >= 2 and (hc - 2) % 8 == 7:
                        emit_rmx((hc - 2) // 8)
                emit_BC_silu(31)
                emit_rmx(3)
                FPS["hlast"] = h_sem[-1]
                if fpart < 2:
                    if fpart < 1.2:
                        continue
                    pass
                vop("tensor_reduce", rmxf[:, 0:SPW],
                    rmx4[:, :].rearrange("p (g t) -> p t g", g=4),
                    AX.X, OP.max, sem=S["sDV"])
                rmx_dv = S["sDV"].n
                if fpart < 1.3:
                    continue
                for u in range(2):
                    cb0 = 16 * u
                    bld.wait("tensor", S["sDV"])
                    bld.wait("tensor", S["sAC"])
                    tr(ps[0:128, 3584:3584 + 128], rmxf[:, 128 * u:128 * (u + 1)],
                       identf[:, :], sem=S["sPE"])
                    bld.wait("vector", S["sPE"])
                    vop("reduce_max", xmsc[0:128, cb0:cb0 + 1],
                        ps[0:128, 3584:3584 + 128], AX.X, sem=S["sDV"])
                    if fpart < 1.5:
                        continue
                    vop("tensor_scalar", xmsc[0:128, cb0 + 1:cb0 + 2],
                        xmsc[0:128, cb0:cb0 + 1], 1e-5, 1.0,
                        op0=OP.max, op1=OP.mult, sem=S["sDV"])
                    vop("reciprocal", xmsc[0:128, cb0 + 2:cb0 + 3],
                        xmsc[0:128, cb0 + 1:cb0 + 2], sem=S["sDV"])
                    vop("tensor_scalar", xmsc[0:128, cb0 + 3:cb0 + 4],
                        xmsc[0:128, cb0 + 2:cb0 + 3], 127.0, 1.0,
                        op0=OP.mult, op1=OP.mult, sem=S["sDV"])
                    vop("tensor_scalar", o2c[0:128, u:u + 1],
                        xmsc[0:128, cb0 + 1:cb0 + 2], wsc[0:128, 3:4], 0.5 / 127.0,
                        op0=OP.mult, op1=OP.mult, sem=S["sDV"])
                    if fpart < 1.6:
                        continue
                    bld.wait("tensor", S["sDV"])
                    tr(ps[0:1, 3840:3840 + 128],
                       xmsc[0:128, cb0 + 3:cb0 + 4], identf[:], sem=S["sPE"])
                    bld.wait("scalar", S["sPE"])
                    act(s2row[0:1, 128 * u:128 * (u + 1)],
                        ps[0:1, 3840:3840 + 128], AF.Copy, sem=S["sAC"])
                bld.wait("tensor", S["sAC"])
                mm(bank(6)[:, 0:SPW], onesf[:, :], s2row[0:1, 0:SPW], True, True,
                   sem=S["sPE"])
                bld.wait("scalar", S["sPE"])
                act(s2b[:, 0:SPW], bank(6)[:, 0:SPW], AF.Copy, sem=S["sAC"])
                s2b_ready = S["sAC"].n
                if fpart < 3:
                    continue
                qv = []
                castv = []
                ocstop = [0] * 8
                for w in range(8):
                    m2slot = len(m2done) % 2
                    if len(m2done) >= 2:
                        bld.wait("scalar", S["sPE"], m2done[-2])
                    for oc in range(8):
                        bld.dma("scalar",
                                m2w[:, 4096 * m2slot + 512 * oc:4096 * m2slot + 512 * (oc + 1)],
                                m2d[oc, :, 512 * w:512 * (w + 1)], SM2[m2slot])
                    for j in range(4):
                        hc = 4 * w + j
                        if hc == 0:
                            bld.wait("vector", S["sAC"], s2b_ready)
                        if w >= 2 and j == 0:
                            bld.wait("vector", S["sAC"], castv[w - 2])
                        vop("tensor_tensor", n28[:, 256 * (hc % 8):256 * (hc % 8) + 256],
                            silub[:, SPW * hc:SPW * (hc + 1)], s2b[:, 0:SPW], OP.mult,
                            sem=S["sDV"])
                        qv.append(S["sDV"].n)
                    bld.wait("scalar", S["sDV"], qv[-1])
                    bld.wait("scalar", S["sPE"], FPS["wave"][w])
                    act(n2b[:, 1024 * w:1024 * (w + 1)],
                        n28[:, 1024 * (w % 2):1024 * (w % 2 + 1)], AF.Copy, sem=S["sAC"])
                    castv.append(S["sAC"].n)
                    if fpart < 4:
                        m2done.append(S["sPE"].n)
                        continue
                    bld.wait("tensor", SM2[m2slot])
                    bld.wait("tensor", S["sAC"], castv[w])
                    if w == 0:
                        bld.wait("tensor", S["sDV"], FPS["resid"])
                    if w < 7:
                        for j in range(4):
                            for oc in range(8):
                                mm(bank(oc)[:, 0:SPW],
                                   m2w[:, 4096 * m2slot + 512 * oc + 128 * j:
                                       4096 * m2slot + 512 * oc + 128 * (j + 1)],
                                   n2b[:, 256 * (4 * w + j):256 * (4 * w + j + 1)],
                                   start=(w == 0 and j == 0), stop=False,
                                   sem=(S["sPE"] if (j == 3 and oc == 7) else None))
                    else:
                        for oc in range(8):
                            for j in range(4):
                                mm(bank(oc)[:, 0:SPW],
                                   m2w[:, 4096 * m2slot + 512 * oc + 128 * j:
                                       4096 * m2slot + 512 * oc + 128 * (j + 1)],
                                   n2b[:, 256 * (28 + j):256 * (29 + j)],
                                   start=False, stop=(j == 3),
                                   sem=(S["sPE"] if j == 3 else None))
                            ocstop[oc] = S["sPE"].n
                    m2done.append(S["sPE"].n)
                    FPS["wave"][w] = S["sPE"].n
                if fpart < 4:
                    continue
                for oc in range(8):
                    bld.wait("scalar", S["sPE"], ocstop[oc])
                    act(fsb[:, 256 * oc:256 * (oc + 1)], bank(oc)[:, 0:SPW],
                        AF.Copy, sem=S["sAC"])
                    FPS["fd"][oc] = S["sAC"].n
                bld.wait("tensor", S["sAC"], FPS["fd"][7])
                for u in range(2):
                    xslot = u % 2
                    bld.wait("sync", S["sDV"])
                    bld.dma("sync", xt[0:128, 1024 * xslot:1024 * (xslot + 1)],
                            xsrc[o + 128 * u:o + 128 * u + 128, :], SX[xslot])
                    if u == 1:
                        bld.wait("tensor", S["sDV"], FPS["resid"])
                    for oc in range(8):
                        tr(ps[0:128, 2048 + 128 * oc:2048 + 128 * (oc + 1)],
                           fsb[:, 256 * oc + 128 * u:256 * oc + 128 * u + 128],
                           identf[:], sem=(S["sPE"] if oc == 7 else None))
                    bld.wait("vector", S["sPE"])
                    bld.wait("vector", SX[xslot])
                    bld.wait("vector", S["sT%d" % u], FPS["xtst"][u])
                    vop("scalar_tensor_tensor", xtm[0:128, 1024 * u:1024 * (u + 1)],
                        ps[0:128, 2048:3072], o2c[0:128, u:u + 1],
                        xt[0:128, 1024 * xslot:1024 * (xslot + 1)],
                        op0=OP.mult, op1=OP.add, sem=S["sDV"])
                    FPS["resid"] = S["sDV"].n
                    bld.wait("scalar", S["sDV"], FPS["resid"])
                    sdma(resdst[o + res_off + 128 * u:o + res_off + 128 * u + 128, :],
                         xtm[0:128, 1024 * u:1024 * (u + 1)], sem=S["sT%d" % u])
                    FPS["xtst"][u] = S["sT%d" % u].n
            bld.wait("scalar", S["sST"])
            bld.wait("scalar", S["sT0"])
            bld.wait("scalar", S["sT1"])

        def finish_debug(srcap):
            bld.em("gpsimd", lambda q, s=srcap: q.dma_start(out=xout[:, :], in_=s),
                   S["sBC"], 16)
            bld.wait("gpsimd", S["sBC"])

        if upto == "pre":
            finish_debug(x2d[0:2048, :])
        if upto in ("full", "ffn1", "ln2", "attn", "wr"):
            ffn_call(xin, x1d, 0)
            barrier()
        if upto == "ffn1":
            finish_debug(x1d[1:2049, :])

        RUN_LN2 = upto in ("full", "ln2", "attn", "wr")
        if RUN_LN2:
            P = NT
            for t in range(16):
                slot = t % 2
                if t >= 2:
                    bld.wait("sync", S["sDV"])
                    bld.wait("sync", S["sAC"])
                bld.dma("sync", xt[:, 1024 * slot:1024 * (slot + 1)],
                        x1d[1 + 128 * t:1 + 128 * (t + 1), :], SX[slot])
                xtile = xt[:, 1024 * slot:1024 * (slot + 1)]
                bld.wait("vector", SX[slot])
                vop("reduce_sum", stA[:, t:t + 1], xtile, AX.X, sem=S["sDV"])
                bld.wait("scalar", SX[slot])
                act(sqsc[:, 0:1024], xtile, AF.Square, accum=stA[:, P + t:P + t + 1], sem=S["sAC"])
                vop("tensor_scalar", stA[:, 2 * P + t:2 * P + t + 1], stA[:, t:t + 1],
                    1.0 / D, 1.0, op0=OP.mult, op1=OP.mult, sem=S["sDV"])
            bld.wait("vector", S["sAC"])
            pl = lambda i: stA[:, i * P:i * P + 16]
            vop("tensor_tensor", pl(3), pl(2), pl(2), OP.mult, sem=S["sDV"])
            vop("tensor_scalar", pl(4), pl(1), 1.0 / D, 1.0, op0=OP.mult, op1=OP.mult, sem=S["sDV"])
            vop("tensor_tensor", pl(3), pl(4), pl(3), OP.subtract, sem=S["sDV"])
            bld.wait("scalar", S["sDV"])
            act(xmsc[:, 0:16], pl(3), AF.Ln, bias=epsv[:, 0:1], sem=S["sAC"])
            act(pl(4), xmsc[:, 0:16], AF.Exp, scale=-0.5, sem=S["sAC"])
            bld.wait("vector", S["sAC"])
            y2done = []
            for t in range(16):
                slot = t % 2
                bld.wait("sync", S["sDV"])
                bld.wait("sync", S["sAC"])
                bld.dma("sync", xt[:, 1024 * slot:1024 * (slot + 1)],
                        x1d[1 + 128 * t:1 + 128 * (t + 1), :], SX[slot])
                xtile = xt[:, 1024 * slot:1024 * (slot + 1)]
                bld.wait("vector", SX[slot])
                if t >= 1:
                    bld.wait("vector", S["sPE"])
                vop("tensor_scalar", ntmb[:, 0:1024], xtile, stA[:, 2 * P + t:2 * P + t + 1],
                    stA[:, 4 * P + t:4 * P + t + 1], op0=OP.subtract, op1=OP.mult, sem=S["sDV"])
                bld.wait("tensor", S["sDV"])
                for k in range(8):
                    pslot = (ps[:, 3072:3072 + 256] if k % 2 == 0
                             else ps[:, 3584:3584 + 256]).bitcast(BF16)[:, 0:128]
                    if len(y2done) >= 2:
                        bld.wait("tensor", S["sDV"], y2done[-2])
                    tr(pslot, ntmb[:, 128 * k:128 * (k + 1)], identb[:], sem=S["sPE"])
                    bld.wait("vector", S["sPE"])
                    dst = y2fm if k < 4 else y2own
                    kk = k % 4
                    vop("tensor_copy", dst[:, TOK * kk + 128 * t:TOK * kk + 128 * (t + 1)],
                        pslot, sem=S["sDV"])
                    y2done.append(S["sDV"].n)
            bld.wait("vector", S["sDV"])
            for ch in range(4):
                lo = y2fm[:, TOK * ch:TOK * (ch + 1)]
                hi = y2own[:, TOK * ch:TOK * (ch + 1)]
                vop("tensor_scalar", mrg[:, 0:TOK], hi, msk[:, 1:2], 1.0,
                    op0=OP.mult, op1=OP.mult, sem=S["sDV"])
                vop("scalar_tensor_tensor", mrg[:, 0:TOK], lo, msk[:, 0:1], mrg[:, 0:TOK],
                    op0=OP.mult, op1=OP.add, sem=S["sDV"])
                bld.wait("scalar", S["sDV"])
                sdma(y2ci[128 * ch:128 * (ch + 1), :], mrg[:, 0:TOK])
                vop("tensor_scalar", mrg[:, TOK:2 * TOK], lo, msk[:, 1:2], 1.0,
                    op0=OP.mult, op1=OP.mult, sem=S["sDV"])
                vop("scalar_tensor_tensor", mrg[:, TOK:2 * TOK], hi, msk[:, 0:1],
                    mrg[:, TOK:2 * TOK], op0=OP.mult, op1=OP.add, sem=S["sDV"])
            bld.wait("vector", S["sST"])
            for ch in range(4):
                lo = y2fm[:, TOK * ch:TOK * (ch + 1)]
                hi = y2own[:, TOK * ch:TOK * (ch + 1)]
                vop("tensor_scalar", sqsc[:, 0:1024].rearrange("p n -> p n") if False else mrg[:, 0:TOK],
                    lo, msk[:, 1:2], 1.0, op0=OP.mult, op1=OP.mult, sem=S["sDV"])
                vop("scalar_tensor_tensor", y2own[:, TOK * ch:TOK * (ch + 1)], hi, msk[:, 0:1],
                    mrg[:, 0:TOK], op0=OP.mult, op1=OP.add, sem=S["sDV"])
            bld.wait("gpsimd", S["sST"])
            bld.wait("gpsimd", S["sDV"])
            bld.em("gpsimd", lambda q: q.collective_compute(
                "AllGather", OP.bypass, replica_groups=[[0, 1], [2, 3], [4, 5], [6, 7]],
                ins=[y2ci[:, :]], outs=[y2go[:, :]]), S["sCC"], 1)
            bld.wait("sync", S["sCC"])
            for half in range(2):
                row0 = 0 if half == 0 else 512
                for ch in range(4):
                    bld.wait("sync", S["sDV"])
                    bld.dma("sync", mrg[:, 0:TOK],
                            y2go[row0 + 128 * ch:row0 + 128 * (ch + 1), :], SX[0])
                    own = y2own[:, TOK * ch:TOK * (ch + 1)]
                    dst = y2hh[:, SEQ * ch + TOK * half:SEQ * ch + TOK * (half + 1)]
                    bld.wait("vector", SX[0])
                    if half == 0:
                        vop("tensor_scalar", mgs[:, 0:TOK], own, msk[:, 1:2], 1.0,
                            op0=OP.mult, op1=OP.mult, sem=S["sDV"])
                        vop("scalar_tensor_tensor", dst, mrg[:, 0:TOK], msk[:, 0:1],
                            mgs[:, 0:TOK], op0=OP.mult, op1=OP.add, sem=S["sDV"])
                    else:
                        vop("tensor_scalar", mgs[:, 0:TOK], mrg[:, 0:TOK], msk[:, 1:2], 1.0,
                            op0=OP.mult, op1=OP.mult, sem=S["sDV"])
                        vop("scalar_tensor_tensor", dst, own, msk[:, 0:1],
                            mgs[:, 0:TOK], op0=OP.mult, op1=OP.add, sem=S["sDV"])
            barrier()

        if upto == "ln2":
            bld.em("gpsimd", lambda q: q.dma_start(
                out=xout[0:1024, :].bitcast(BF16), in_=y2go[:, :]), S["sBC"], 16)
            bld.wait("gpsimd", S["sBC"])
        if upto in ("full", "attn", "wr"):
            pav = ps[0:65, 3072:3584]
            pzb = ps[0:64, 3584:3584 + 512]
            ptmp = ps[0:1, 3584:3584 + 65]
            rot_expdone = [0, 0]
            fills_done = [0, 0]
            qkdone = [0, 0]
            av_guard = 0
            for k in range(nheads):
                ch4 = k // 2
                pbase = 64 * (k % 2)

                def yh(c0, w_, ch4=ch4, pbase=pbase):
                    return y2hh[pbase:pbase + 64, SEQ * ch4 + c0:SEQ * ch4 + c0 + w_]

                for which, (wsl0, wsl1, bcol, scl) in enumerate(
                        [(0, 64, 0, 0.125), (64, 128, 1, None)]):
                    for cq in range(8):
                        s = (which * 8 + cq) % 2
                        pqk = ps[0:64, 1536 * s:1536 * s + 512]
                        bld.wait("tensor", S["sAC"], qkdone[s])
                        if which == 0 and cq == 0:
                            bld.wait("tensor", S["sDV"], av_guard)
                            bld.wait("tensor", S["sAC"], max(rot_expdone))
                        mm(pqk, wqkb[pbase:pbase + 64, wsl0:wsl1], yh(512 * cq, 512),
                           True, True, sem=S["sPE"])
                        bld.wait("scalar", S["sPE"])
                        act(qk2[0:64, SEQ * which + 512 * cq:SEQ * which + 512 * (cq + 1)], pqk,
                            AF.Identity, bias=bq8[pbase:pbase + 64, bcol:bcol + 1], scale=scl,
                            sem=S["sAC"])
                        qkdone[s] = S["sAC"].n
                vext_u = vext
                Ebuf_u = Ebuf
                vdone = [0, 0]
                for kb in range(32):
                    s = kb % 2
                    pv = ps[:, 512 + 512 * s:512 + 512 * s + 65]
                    bld.wait("tensor", S["sDV"], vdone[s])
                    mm(pv, yh(128 * kb, 128), wv65[pbase:pbase + 64, :], True, True,
                       sem=S["sPE"])
                    bld.wait("vector", S["sPE"])
                    vop("tensor_tensor", vext_u[:, 65 * kb:65 * (kb + 1)], pv, bvb[:, :],
                        OP.add, sem=S["sDV"])
                    vdone[s] = S["sDV"].n
                bld.wait("tensor", S["sDV"])
                bld.wait("tensor", S["sAC"])
                chunk_range = [] if attn_level == 0 else list(range(7, -1, -1))
                for ci, c in enumerate(chunk_range):
                    nblk = 4 * c + 4
                    glist = [(g0, min(3, nblk - g0)) for g0 in range(0, nblk, 3)]
                    first_av = True
                    for gi, (g0, gn) in enumerate(glist):
                        r = gi % 2
                        rot = r * 1536
                        bld.wait("tensor", S["sAC"], rot_expdone[r])
                        for j in range(gn):
                            b = g0 + j
                            mm(ps[0:128, rot + 512 * j:rot + 512 * j + 512],
                               qk2[0:64, SEQ + 128 * b:SEQ + 128 * (b + 1)],
                               qk2[0:64, 512 * c:512 * (c + 1)], True, True,
                               sem=(S["sPE"] if j == gn - 1 else None))
                        bld.wait("scalar", S["sPE"])
                        bld.wait("scalar", S["sPL"], fills_done[r])
                        act(Ebuf_u[:, 1536 * r:1536 * r + 512 * gn],
                            ps[:, rot:rot + 512 * gn], AF.Exp, sem=S["sAC"])
                        rot_expdone[r] = S["sAC"].n
                        nfills = 0
                        for j in range(gn):
                            b = g0 + j
                            if b >= 4 * c:
                                if nfills == 0:
                                    bld.wait("gpsimd", S["sAC"], rot_expdone[r])
                                eap = Ebuf_u[:, 1536 * r + 512 * j:1536 * r + 512 * (j + 1)]
                                bld.em("gpsimd", lambda q, eap=eap, bb=b, cc=c:
                                       q.affine_select(eap, eap, [[1, 512]], OP.is_ge,
                                                       fillreg["one"],
                                                       base=512 * cc - 128 * bb - 1,
                                                       channel_multiplier=-1),
                                       S["sPL"], 1)
                                nfills += 1
                        if nfills:
                            fills_done[r] = S["sPL"].n
                            bld.wait("tensor", S["sPL"], fills_done[r])
                        if attn_level < 2:
                            continue
                        if first_av:
                            bld.wait("tensor", S["sDV"], av_guard)
                            first_av = False
                        for j in range(gn):
                            b = g0 + j
                            stop = (c == 7 and b == nblk - 1)
                            mm(pav, vext_u[:, 65 * b:65 * (b + 1)],
                               Ebuf_u[:, 1536 * r + 512 * j:1536 * r + 512 * (j + 1)],
                               start=(b == 0), stop=stop,
                               sem=(S["sPE"] if stop else None))
                    if attn_level < 2:
                        av_guard = S["sDV"].n
                        continue
                    if c < 7:
                        mm(pav, vsufB[0:1, 0:65], onesb[0:1, 0:512], start=False, stop=True,
                           sem=S["sPE"])
                    if attn_level < 3:
                        bld.wait("vector", S["sPE"])
                        vop("tensor_copy", aosb[0:64, 0:512], ps[0:64, 3072:3584], sem=S["sDV"])
                        av_guard = S["sDV"].n
                        continue
                    bld.wait("vector", S["sPE"])
                    vop("reciprocal", rcpz[0:1, 0:512], ps[64:65, 3072:3584], sem=S["sDV"])
                    vop("tensor_copy", rcpb[0:1, 0:512], rcpz[0:1, 0:512], sem=S["sDV"])
                    vop("scalar_tensor_tensor", rcpb[0:1, 512:1024], rcpb[0:1, 0:512], -1.0,
                        rcpz[0:1, 0:512], op0=OP.mult, op1=OP.add, sem=S["sDV"])
                    bld.wait("tensor", S["sDV"])
                    mm(pzb, onesb[0:1, 0:64], rcpb[0:1, 0:512], True, False)
                    mm(pzb, onesb[0:1, 0:64], rcpb[0:1, 512:1024], False, True, sem=S["sPE"])
                    bld.wait("vector", S["sPE"])
                    aslot = aosb[0:64, 512 * (ci % 2):512 * (ci % 2) + 512]
                    bld.wait("vector", S["sST"])
                    vop("tensor_copy", aslot, ps[0:64, 3072:3584], sem=S["sDV"])
                    bld.wait("vector", S["sDV"])
                    vop("tensor_tensor", aslot, aslot, pzb, OP.mult, sem=S["sDV"])
                    zb_read = S["sDV"].n
                    bld.wait("scalar", S["sDV"])
                    sdma(aod[64 * k:64 * (k + 1), 512 * c:512 * (c + 1)], aslot)
                    if c > 0:
                        bld.wait("tensor", S["sDV"], zb_read)
                        for j in range(4):
                            b = 4 * c + j
                            mm(ptmp, onesb[:, 0:1], vext_u[:, 65 * b:65 * (b + 1)],
                               start=(j == 0), stop=(j == 3), sem=(S["sPE"] if j == 3 else None))
                        bld.wait("vector", S["sPE"])
                        if c == 7:
                            vop("tensor_copy", vsufA[0:1, 0:65], ptmp, sem=S["sDV"])
                        else:
                            vop("tensor_tensor", vsufA[0:1, 0:65], vsufA[0:1, 0:65], ptmp,
                                OP.add, sem=S["sDV"])
                        vop("tensor_copy", vsufB[0:1, 0:65], vsufA[0:1, 0:65], sem=S["sDV"])
                    av_guard = S["sDV"].n
            bld.wait("tensor", S["sDV"], av_guard)
            bld.wait("scalar", S["sST"])
            barrier()

        if upto == "attn":
            bld.em("gpsimd", lambda q: q.dma_start(
                out=xout[0:1024, :].bitcast(BF16),
                in_=aod[:, :].rearrange("p (a n) -> (p a) n", a=2)), S["sBC"], 16)
            bld.wait("gpsimd", S["sBC"])
        if upto in ("full", "wr"):
            cA = lambda ch: candA[:, 2050 * ch:2050 * (ch + 1)]
            cB = lambda ch: candB[:, 2050 * ch:2050 * (ch + 1)]
            ow = lambda ch: ownw[:, 2050 * ch:2050 * (ch + 1)]
            for ch in range(4):
                vop("memset", cA(ch)[:, 0:1], 0.0, sem=S["sDV"])
                vop("memset", cB(ch)[:, 2049:2050], 0.0, sem=S["sDV"])
            bld.wait("sync", S["sDV"])
            for ch in range(4):
                bld.dma("sync", cA(ch)[:, 1:2050], aod[128 * ch:128 * (ch + 1), 0:2049], SX[0])
                bld.dma("sync", cB(ch)[:, 0:2049], aod[128 * ch:128 * (ch + 1), 2047:4096], SX[0])
            bld.wait("vector", SX[0])
            for ch in range(4):
                vop("tensor_scalar", ow(ch), cB(ch), msk[:, 1:2], 1.0,
                    op0=OP.mult, op1=OP.mult, sem=S["sDV"])
                vop("scalar_tensor_tensor", ow(ch), cA(ch), msk[:, 0:1], ow(ch),
                    op0=OP.mult, op1=OP.add, sem=S["sDV"])
                bld.wait("scalar", S["sDV"])
                sdma(aoci[128 * ch:128 * (ch + 1), :], ow(ch))
            bld.dma("sync", hbf[0:1, 0:2048].bitcast(F32), x1d[1:2, :], SX[0])
            bld.dma("sync", hbf[0:1, 2048:4096].bitcast(F32), x1d[2048:2049, :], SX[0])
            bld.wait("scalar", SX[0])
            sdma(aoci[512:513, 0:2048], hbf[0:1, 0:2048])
            sdma(aoci[513:514, 0:2048], hbf[0:1, 2048:4096])
            bld.wait("gpsimd", S["sST"])
            bld.em("gpsimd", lambda q: q.collective_compute(
                "AllGather", OP.bypass, replica_groups=[[0, 1], [2, 3], [4, 5], [6, 7]],
                ins=[aoci[:, :]], outs=[aogo[:, :]]), S["sCC"], 1)
            bld.wait("vector", S["sST"])
            for ch in range(4):
                vop("tensor_scalar", ow(ch), cA(ch), msk[:, 1:2], 1.0,
                    op0=OP.mult, op1=OP.mult, sem=S["sDV"])
                vop("scalar_tensor_tensor", ow(ch), cB(ch), msk[:, 0:1], ow(ch),
                    op0=OP.mult, op1=OP.add, sem=S["sDV"])
            bld.wait("sync", S["sCC"])
            bld.wait("sync", S["sDV"])
            for ch in range(4):
                bld.dma("sync", cA(ch), aogo[128 * ch:128 * (ch + 1), 0:2050], SX[0])
                bld.dma("sync", cB(ch), aogo[514 + 128 * ch:514 + 128 * (ch + 1), 0:2050], SX[0])
            bld.wait("vector", SX[0])
            for ch in range(4):
                vop("tensor_scalar", cB(ch), cB(ch), msk[:, 1:2], 1.0,
                    op0=OP.mult, op1=OP.mult, sem=S["sDV"])
                vop("scalar_tensor_tensor", cB(ch), cA(ch), msk[:, 0:1], cB(ch),
                    op0=OP.mult, op1=OP.add, sem=S["sDV"])
            for ch in range(4):
                vop("tensor_scalar", cA(ch), ow(ch), msk[:, 1:2], 1.0,
                    op0=OP.mult, op1=OP.mult, sem=S["sDV"])
                vop("scalar_tensor_tensor", awin[:, 2050 * ch:2050 * (ch + 1)], cB(ch),
                    msk[:, 0:1], cA(ch), op0=OP.mult, op1=OP.add, sem=S["sDV"])
            for ch in range(4):
                vop("tensor_scalar", cA(ch), cB(ch), msk[:, 1:2], 1.0,
                    op0=OP.mult, op1=OP.mult, sem=S["sDV"])
                vop("scalar_tensor_tensor", awin[:, 2050 * (ch + 4):2050 * (ch + 5)],
                    ow(ch), msk[:, 0:1], cA(ch), op0=OP.mult, op1=OP.add, sem=S["sDV"])
            bld.wait("sync", S["sDV"])
            bld.dma("sync", hbf[0:1, 0:2048], aogo[1026:1027, 0:2048], SX[0])
            bld.dma("sync", hbf[0:1, 2048:4096], aogo[513:514, 0:2048], SX[0])
            bld.wait("vector", SX[0])
            vop("tensor_scalar", halo[0:1, 0:1024], hbf[0:1, 0:2048].bitcast(F32),
                msk[0:1, 1:2], 1.0, op0=OP.mult, op1=OP.mult, sem=S["sDV"])
            vop("scalar_tensor_tensor", halo[0:1, 0:1024], hbf[0:1, 2048:4096].bitcast(F32),
                msk[0:1, 0:1], halo[0:1, 0:1024], op0=OP.mult, op1=OP.add, sem=S["sDV"])
            wrdone = []
            for si, (o, width) in enumerate(WR_SPANS):
                nu = (width + 127) // 128
                for oc in range(8):
                    fps = bank(2 + oc % 2)[:, 0:width]
                    if len(wrdone) >= 2:
                        bld.wait("tensor", S["sAC"], wrdone[-2])
                    if si == 0 and oc == 0:
                        bld.wait("tensor", S["sDV"])
                    for kc in range(8):
                        mm(fps, wrB[:, 1024 * kc + 128 * oc:1024 * kc + 128 * (oc + 1)],
                           awin[:, 2050 * kc + o:2050 * kc + o + width],
                           start=(kc == 0), stop=False)
                    mm(fps, brb[0:1, 128 * oc:128 * (oc + 1)], onesb[0:1, 0:width],
                       start=False, stop=True, sem=S["sPE"])
                    bld.wait("scalar", S["sPE"])
                    act(wfs[:, 384 * oc:384 * oc + width], fps, AF.Copy, sem=S["sAC"])
                    wrdone.append(S["sAC"].n)
                bld.wait("tensor", S["sAC"])
                for u in range(nu):
                    blk = min(128, width - 128 * u)
                    w0 = o + 128 * u
                    bld.wait("sync", S["sDV"])
                    bld.dma("sync", wxt[0:blk, 0:1024], x1d[w0:w0 + blk, :], SX[0])
                    pT = ps[:, 2048:3072]
                    bld.wait("tensor", S["sDV"])
                    for oc in range(8):
                        tr(pT[0:blk, 128 * oc:128 * (oc + 1)],
                           wfs[:, 384 * oc + 128 * u:384 * oc + 128 * u + blk],
                           identf[:], sem=(S["sPE"] if oc == 7 else None))
                    bld.wait("vector", S["sPE"])
                    bld.wait("vector", SX[0])
                    bld.wait("vector", S["sST"])
                    vop("tensor_tensor", wxm[0:blk, 0:1024], pT[0:blk, 0:1024],
                        wxt[0:blk, 0:1024], OP.add, sem=S["sDV"])
                    bld.wait("scalar", S["sDV"])
                    sdma(x2d[w0:w0 + blk, :], wxm[0:blk, 0:1024])
            for wrow, mcol in [(0, 0), (2049, 1)]:
                bld.wait("sync", S["sST"])
                bld.dma("sync", wxm[0:1, 0:1024], x2d[wrow:wrow + 1, :], SX[0])
                bld.wait("vector", SX[0])
                bld.wait("vector", S["sST"])
                vop("tensor_tensor", wxm[0:1, 0:1024], wxm[0:1, 0:1024], halo[0:1, :],
                    OP.add, sem=S["sDV"])
                vop("tensor_scalar", wxm[0:1, 0:1024], wxm[0:1, 0:1024], msk[0:1, mcol:mcol + 1],
                    1.0, op0=OP.mult, op1=OP.mult, sem=S["sDV"])
                bld.wait("scalar", S["sDV"])
                sdma(x2d[wrow:wrow + 1, :], wxm[0:1, 0:1024])
            bld.wait("scalar", S["sST"])
            barrier()

        if upto == "wr":
            finish_debug(x2d[0:2048, :])
        if upto == "full":
            ffn_call(x2d, xout, -1)
        bld.wait("gpsimd", S["sST"])

        @block.sync
        def _(q):
            for f in bld.ev["sync"]:
                f(q)

        @block.tensor
        def _(q):
            for f in bld.ev["tensor"]:
                f(q)

        @block.vector
        def _(q):
            for f in bld.ev["vector"]:
                f(q)

        @block.scalar
        def _(q):
            for f in bld.ev["scalar"]:
                f(q)

        @block.gpsimd
        def _(q):
            for f in bld.ev["gpsimd"]:
                f(q)

    return nc


_NC_CACHE = None


def _get_nc():
    global _NC_CACHE
    if _NC_CACHE is None:
        _NC_CACHE = build_program()
    return _NC_CACHE


def kernel(L0_attn_x, L0_attn_Wqkv, L0_attn_bqkv, L0_attn_Wr, L0_attn_br,
           L0_ffn_w1, L0_ffn_conv_w, L0_ffn_conv_b, L0_ffn_w2):
    x = np.asarray(L0_attn_x, dtype=np.float32)
    w1 = np.asarray(L0_ffn_w1, dtype=np.float32)
    w2 = np.asarray(L0_ffn_w2, dtype=np.float32)
    conv_w = np.asarray(L0_ffn_conv_w, dtype=np.float32)
    conv_b = np.asarray(L0_ffn_conv_b, dtype=np.float32)
    wqkv = np.asarray(L0_attn_Wqkv, dtype=np.float32)
    bqkv = np.asarray(L0_attn_bqkv, dtype=np.float32)
    wr = np.asarray(L0_attn_Wr, dtype=np.float32)
    br = np.asarray(L0_attn_br, dtype=np.float32)

    w1tiles = np.ascontiguousarray(
        w1.reshape(32, 128, 8, 128).transpose(0, 3, 2, 1).reshape(32, 128, 1024))
    w2tiles = np.ascontiguousarray(
        w2.reshape(8, 128, 32, 128).transpose(0, 3, 2, 1).reshape(8, 128, 4096))
    cwh = np.zeros((128, 128), np.float32)
    cwh.reshape(128, 32, 4)[:, :, 0:3] = conv_w[:, 0, :].reshape(32, 128, 3).transpose(1, 0, 2)
    cbh = np.ascontiguousarray(conv_b.reshape(32, 128).T)
    bqkvh = np.ascontiguousarray(np.tile(bqkv.reshape(3, 64).T, (2, 1)))
    wqkvh = np.ascontiguousarray(np.tile(wqkv, (2, 1)))
    bvbh = np.ones((128, 65), np.float32)
    bvbh[:, 0:64] = bqkv[128:192][None, :]
    brh = np.ascontiguousarray(br.reshape(1, 1024))
    iden = np.eye(128, dtype=np.float32)

    in_maps = []
    for c in range(8):
        b, p = c // 2, c % 2
        xw = np.zeros((WP, D), np.float32)
        xw[1:2049] = x[b, 2048 * p:2048 * p + 2048]
        if p == 0:
            xw[2049] = x[b, 2048]
        else:
            xw[0] = x[b, 2047]
        mskh = np.zeros((128, 4), np.float32)
        mskh[:, 0] = float(p)
        mskh[:, 1] = 1.0 - float(p)
        in_maps.append({
            "xin": xw, "w1t": w1tiles, "w2t": w2tiles, "cwi": cwh, "cbi": cbh,
            "wqkvi": wqkvh, "bqkvi": bqkvh, "bvbi": bvbh, "wri": wr, "bri": brh,
            "ideni": iden, "mski": mskh,
        })

    nc = _get_nc()
    res = run_bass_kernel_spmd(nc, in_maps, core_ids=list(range(8)))
    out = np.empty((4, 4096, 1024), np.float32)
    for c in range(8):
        b, p = c // 2, c % 2
        out[b, 2048 * p:2048 * p + 2048] = res.results[c]["xout"]
    return out


# version 05
# speedup vs baseline: 1.0575x; 1.0575x over previous
import numpy as np
import ml_dtypes
from contextlib import ExitStack

import concourse.bass as bass
import concourse.mybir as mybir
from concourse.alu_op_type import AluOpType as OP
from concourse.bass_utils import run_bass_kernel_spmd

F32 = mybir.dt.float32
F32R = mybir.dt.float32r
BF16 = mybir.dt.bfloat16
I8 = mybir.dt.int8
AF = mybir.ActivationFunctionType
AX = mybir.AxisListType

D = 1024
DH = 4096
SEQ = 4096
TOK = 2048
W = 2050
WP = 2176
NT = 17
SPW = 256
NSP = 8
WR_SPANS = [(0, 384), (384, 384), (768, 384), (1152, 384), (1536, 384), (1920, 130)]
ENGS = ("sync", "tensor", "vector", "scalar", "gpsimd")


class Sem:
    def __init__(self, h):
        self.h = h
        self.n = 0


class B:
    def __init__(self):
        self.ev = {e: [] for e in ENGS}

    def em(self, eng, fn, sem=None, inc=1, selfwait=False):
        if sem is not None:
            sem.n += inc
            def wrapped(q, fn=fn, s=sem.h, k=inc):
                fn(q).then_inc(s, k)
            self.ev[eng].append(wrapped)
            if selfwait:
                v = sem.n
                self.ev[eng].append(lambda q, s=sem.h, v=v: q.wait_ge(s, v))
        else:
            self.ev[eng].append(lambda q, fn=fn: fn(q))

    def wait(self, eng, sem, val=None):
        v = sem.n if val is None else val
        if v > 0:
            self.ev[eng].append(lambda q, s=sem.h, v=v: q.wait_ge(s, v))

    def dma(self, eng, out, in_, sem):
        self.em(eng, lambda q, o=out, i=in_: q.dma_start(out=o, in_=i), sem, 16)


def build_program(upto="full", nheads=8, attn_level=3, fpart=4):
    nc = bass.Bass("TRN2", target_bir_lowering=False)

    xin = nc.dram_tensor("xin", [WP, D], F32, kind="ExternalInput")
    w1t = nc.dram_tensor("w1t", [32, 128, 1024], F32, kind="ExternalInput")
    w2t = nc.dram_tensor("w2t", [8, 128, 4096], F32, kind="ExternalInput")
    cwi = nc.dram_tensor("cwi", [128, 128], F32, kind="ExternalInput")
    cbi = nc.dram_tensor("cbi", [128, 32], F32, kind="ExternalInput")
    wqkvi = nc.dram_tensor("wqkvi", [128, 192], F32, kind="ExternalInput")
    bqkvi = nc.dram_tensor("bqkvi", [128, 3], F32, kind="ExternalInput")
    bvbi = nc.dram_tensor("bvbi", [128, 65], F32, kind="ExternalInput")
    wri = nc.dram_tensor("wri", [1024, 1024], F32, kind="ExternalInput")
    bri = nc.dram_tensor("bri", [1, 1024], F32, kind="ExternalInput")
    ideni = nc.dram_tensor("ideni", [128, 128], F32, kind="ExternalInput")
    mski = nc.dram_tensor("mski", [128, 4], F32, kind="ExternalInput")

    xout = nc.dram_tensor("xout", [TOK, D], F32, kind="ExternalOutput")

    m1d = nc.dram_tensor("m1d", [32, 128, 1024], BF16)
    m2d = nc.dram_tensor("m2d", [8, 128, 4096], BF16)
    x1d = nc.dram_tensor("x1d", [WP, D], F32)
    x2d = nc.dram_tensor("x2d", [WP, D], F32)
    aod = nc.dram_tensor("aod", [512, 4096], BF16)
    rowd = nc.dram_tensor("rowd", [17, 128], F32)
    y2ci = nc.dram_tensor("y2ci", [512, TOK], BF16)
    y2go = nc.dram_tensor("y2go", [1024, TOK], BF16)
    aoci = nc.dram_tensor("aoci", [514, 2050], BF16)
    aogo = nc.dram_tensor("aogo", [1028, 2050], BF16)

    bld = B()

    with ExitStack() as ctx:
        e = ctx.enter_context

        identf = e(nc.sbuf_tensor("identf", [128, 128], F32))
        identb = e(nc.sbuf_tensor("identb", [128, 128], BF16))
        cw = e(nc.sbuf_tensor("cw", [128, 128], F32))
        cb = e(nc.sbuf_tensor("cb", [128, 32], F32))
        bqk = e(nc.sbuf_tensor("bqk", [128, 3], F32))
        bq8 = e(nc.sbuf_tensor("bq8", [128, 2], F32))
        wqkf = e(nc.sbuf_tensor("wqkf", [128, 192], F32))
        wqkb = e(nc.sbuf_tensor("wqkb", [128, 192], BF16))
        wv65 = e(nc.sbuf_tensor("wv65", [128, 65], BF16))
        bvb = e(nc.sbuf_tensor("bvb", [128, 65], F32))
        brrow = e(nc.sbuf_tensor("brrow", [1, 1024], F32))
        brb = e(nc.sbuf_tensor("brb", [1, 1024], BF16))
        msk = e(nc.sbuf_tensor("msk", [128, 4], F32))
        onesf = e(nc.sbuf_tensor("onesf", [1, 128], F32))
        onescol = e(nc.sbuf_tensor("onescol", [128, 1], F32))
        onesb = e(nc.sbuf_tensor("onesb", [128, 512], BF16))
        epsv = e(nc.sbuf_tensor("epsv", [128, 1], F32))
        wsc = e(nc.sbuf_tensor("wsc", [128, 4], F32))
        scr1 = e(nc.sbuf_tensor("scr1", [1, 8], F32))
        stA = e(nc.sbuf_tensor("stA", [128, 8 * NT], F32))
        o17 = e(nc.sbuf_tensor("o17", [17, 128], F32))
        oscrow = e(nc.sbuf_tensor("oscrow", [1, WP], F32))
        s2row = e(nc.sbuf_tensor("s2row", [1, 256], F32))
        o2c = e(nc.sbuf_tensor("o2c", [128, 4], F32))
        xmsc = e(nc.sbuf_tensor("xmsc", [128, 1024], F32))
        sqsc = e(nc.sbuf_tensor("sqsc", [128, 1024], BF16))
        ntmb = e(nc.sbuf_tensor("ntmb", [128, 1024], BF16))
        oscb = e(nc.sbuf_tensor("oscb", [128, 288], F32))
        s2b = e(nc.sbuf_tensor("s2b", [128, 256], F32))
        vsufA = e(nc.sbuf_tensor("vsufA", [1, 80], F32))
        vsufB = e(nc.sbuf_tensor("vsufB", [1, 80], BF16))
        rcpz = e(nc.sbuf_tensor("rcpz", [1, 512], F32))
        rcpb = e(nc.sbuf_tensor("rcpb", [1, 1024], BF16))
        wabs = e(nc.sbuf_tensor("wabs", [128, 1024], BF16))
        halo = e(nc.sbuf_tensor("halo", [1, 1024], F32))
        hbf = e(nc.sbuf_tensor("hbf", [1, 4096], BF16))
        wrB = e(nc.sbuf_tensor("wrB", [128, 8 * 1024], BF16))
        vext2 = wrB[:, 0:32 * 65]
        Ebuf2 = wrB[:, 2080:2080 + 2 * 1536]

        AR = 132 * 1024
        MAN_TOP = 229344
        MAN_BASE = MAN_TOP - AR
        _vn = [0]

        def V(off, nelem, dt, parts=128):
            esz = {F32: 4, F32R: 4, BF16: 2, I8: 1}[dt]
            _vn[0] += 1
            h = nc.alloc_sbuf_tensor_at(f"ov{_vn[0]}", [128, nelem], dt,
                                        offset=MAN_BASE + off)
            ap = h[:, :] if parts == 128 else h[0:parts, :]
            return ap, off + ((nelem * esz + 31) // 32) * 32

        off = 0
        nfm, off = V(off, 8 * WP, I8)
        PERM_END = off
        LIM = AR

        off = PERM_END
        silub, off = V(off, 32 * SPW, F32)
        n2b, off = V(off, 32 * SPW, BF16)
        fsb, off = V(off, 8 * SPW, F32)
        m1sb, off = V(off, 4 * 1024, BF16)
        m2w, off = V(off, 2 * 4096, BF16)
        xt, off = V(off, 2 * 1024, F32)
        xtm, off = V(off, 2 * 1024, F32)
        nbr, off = V(off, 8 * 258, BF16)
        gb, off = V(off, 2 * 258, F32)
        c2b, off = V(off, 2 * 256, F32)
        c3b, off = V(off, 2 * 256, F32)
        rmx4, off = V(off, 4 * 256, F32)
        rmxf, off = V(off, 256, F32)
        n28, off = V(off, 8 * 256, I8)
        assert off <= LIM, off

        off = PERM_END
        y2hh, off = V(off, 4 * SEQ, BF16)
        ATT2 = off
        y2fm, off = V(off, 4 * TOK, BF16)
        y2own, off = V(off, 4 * TOK, BF16)
        mrg, off = V(off, 4 * TOK, BF16)
        mgs, off = V(off, 2048, BF16)
        assert off <= LIM, off
        off = ATT2
        qk2, off = V(off, 2 * SEQ, BF16)
        vext, off = V(off, 32 * 65, BF16)
        Ebuf, off = V(off, 2 * 1536, BF16)
        aosb, off = V(off, 2 * 512, BF16)
        assert off <= LIM, off

        off = PERM_END
        candA, off = V(off, 4 * 2050, BF16)
        candB, off = V(off, 4 * 2050, BF16)
        ownw, off = V(off, 4 * 2050, BF16)
        awin, off = V(off, 8 * 2050, BF16)
        wfs, off = V(off, 8 * 384, F32)
        wxt, off = V(off, 1024, F32)
        wxm, off = V(off, 1024, F32)
        assert off <= LIM, off

        off = PERM_END
        wqa, off = V(off, 4096, F32)
        wqb, off = V(off, 4096, F32)
        wq8, off = V(off, 4096, I8)
        wqm, off = V(off, 4096, BF16)
        wrstg, off = V(off, 1024, F32)
        assert off <= LIM, off

        ps = e(nc.psum_tensor("ps", [128, 4096], F32))

        def bank(i, w=512):
            return ps[:, 512 * i:512 * i + w]

        S = {}
        for name in ["sLD", "sPE", "sDV", "sAC", "sPL", "sST", "sCC",
                     "sB1", "sB2", "sB3", "sB4", "sB5",
                     "sW0", "sW1", "sX0", "sX1", "sXG", "sBC", "sT0", "sT1",
                     "sM1a", "sM1b", "sM1c", "sM1d", "sM2a", "sM2b"]:
            S[name] = Sem(e(nc.semaphore(name)))
        SW = [S["sW0"], S["sW1"]]
        SX = [S["sX0"], S["sX1"]]
        SM1 = [S["sM1a"], S["sM1b"], S["sM1c"], S["sM1d"]]
        SM2 = [S["sM2a"], S["sM2b"]]

        block = e(nc.Block())
        fillreg = {}
        bld.ev["gpsimd"].append(lambda q: fillreg.__setitem__("one", q.to_reg(1.0)))

        def barrier():
            bs = [S["sB1"], S["sB2"], S["sB3"], S["sB4"], S["sB5"]]
            for eng, s in zip(ENGS, bs):
                if eng in ("vector", "gpsimd"):
                    bld.em(eng, lambda q: q.engine_nop(), s, 1)
                else:
                    bld.em(eng, lambda q: q.nop(), s, 1)
            for eng in ENGS:
                for s in bs:
                    bld.wait(eng, s)

        def mm(out, lhsT, rhs, start, stop, sem=None):
            bld.em("tensor", lambda q, o=out, l=lhsT, r=rhs, a=start, b=stop:
                   q.matmul(o, l, r, start=a, stop=b), sem, 1)

        def tr(out, in_, ident, sem=None):
            bld.em("tensor", lambda q, o=out, i=in_, d=ident:
                   q.transpose(o, i, d), sem, 1)

        def _narrow(out):
            try:
                return out.free_size < 256
            except Exception:
                return True

        def act(out, in_, func, bias=None, scale=None, accum=None, sem=None):
            def f(q, o=out, i=in_, fn=func, b=bias, s=scale, a=accum):
                kw = {}
                if b is not None:
                    kw["bias"] = b
                if s is not None:
                    kw["scale"] = s
                if a is not None:
                    kw["accum_out"] = a
                return q.activation(out=o, in_=i, func=fn, **kw)
            bld.em("scalar", f, sem, 1, selfwait=_narrow(out) or accum is not None)

        def vop(kind, *a, sem=None, eng="vector", **kw):
            out = a[0] if a else None
            bld.em(eng, lambda q, k=kind, aa=a, kk=kw: getattr(q, k)(*aa, **kk), sem, 1,
                   selfwait=_narrow(out))

        def sdma(out, in_, eng="scalar", sem=None):
            bld.em(eng, lambda q, o=out, i=in_: q.dma_start(out=o, in_=i),
                   S["sST"] if sem is None else sem, 16)

        for out, src in [(identf[:], ideni[:, :]), (cw[:], cwi[:, :]), (cb[:], cbi[:, :]),
                         (bvb[:], bvbi[:, :]), (brrow[:], bri[:, :]), (msk[:], mski[:, :]),
                         (bqk[:], bqkvi[:, :]), (wqkf[:], wqkvi[:, :])]:
            bld.dma("sync", out, src, S["sLD"])
        bld.wait("vector", S["sLD"])
        vop("tensor_copy", identb[:], identf[:], sem=S["sDV"])
        vop("memset", onesf[:], 1.0, sem=S["sDV"])
        vop("memset", onescol[:], 1.0, sem=S["sDV"])


        vop("memset", onesb[:], 1.0, sem=S["sDV"])
        vop("memset", epsv[:], 1e-5, sem=S["sDV"])
        vop("tensor_scalar", bq8[:, 0:1], bqk[:, 0:1], 0.125, 1.0,
            op0=OP.mult, op1=OP.mult, sem=S["sDV"])
        vop("tensor_copy", bq8[:, 1:2], bqk[:, 1:2], sem=S["sDV"])
        vop("tensor_copy", wqkb[:], wqkf[:], sem=S["sDV"])
        vop("memset", wv65[:], 0.0, sem=S["sDV"])
        vop("tensor_copy", brb[:], brrow[:], sem=S["sDV"])
        bld.wait("vector", S["sDV"])
        vop("tensor_copy", wv65[:, 0:64], wqkb[:, 128:192], sem=S["sDV"])

        for kc in range(8):
            bld.wait("sync", S["sDV"])
            bld.dma("sync", wrstg[:, :], wri[128 * kc:128 * (kc + 1), :], S["sLD"])
            bld.wait("vector", S["sLD"])
            vop("tensor_copy", wrB[:, 1024 * kc:1024 * (kc + 1)], wrstg[:, :], sem=S["sDV"])

        def wmean(wt, ntiles, wd, bank_i):
            absdone = []
            for qn in ("sync",):
                bld.wait(qn, S["sAC"])
                bld.wait(qn, S["sDV"])
            for i in range(ntiles):
                buf = wqa if i % 2 == 0 else wqb
                qn = "sync"
                if i >= 2:
                    bld.wait(qn, S["sAC"], absdone[i - 2])
                bld.dma(qn, buf[:, 0:wd], wt[i, :, :], SW[i % 2])
                bld.wait("scalar", SW[i % 2])
                for g in range(0, wd, 1024):
                    bld.wait("scalar", S["sPE"])
                    act(wabs[:, 0:1024], buf[:, g:g + 1024], AF.Abs, sem=S["sAC"])
                    bld.wait("tensor", S["sAC"])
                    for g2 in range(0, 1024, 512):
                        first = (i == 0 and g == 0 and g2 == 0)
                        last = (i == ntiles - 1 and g + 1024 == wd and g2 == 512)
                        mm(bank(bank_i)[0:1, :], onesb[:, 0:1],
                           wabs[:, g2:g2 + 512], first, last,
                           sem=S["sPE"])
                absdone.append(S["sAC"].n)
        wmean(w1t, 32, 1024, 0)
        wmean(w2t, 8, 4096, 1)

        bld.wait("vector", S["sPE"])
        vop("reduce_sum", scr1[0:1, 0:1], bank(0)[0:1, :], AX.X, sem=S["sDV"])
        vop("reduce_sum", scr1[0:1, 1:2], bank(1)[0:1, :], AX.X, sem=S["sDV"])
        vop("tensor_scalar", scr1[0:1, 2:3], scr1[0:1, 0:1], 1.0 / (DH * D), 1e-5,
            op0=OP.mult, op1=OP.max, sem=S["sDV"])
        vop("tensor_scalar", scr1[0:1, 3:4], scr1[0:1, 1:2], 1.0 / (DH * D), 1e-5,
            op0=OP.mult, op1=OP.max, sem=S["sDV"])
        vop("reciprocal", scr1[0:1, 4:5], scr1[0:1, 2:3], sem=S["sDV"])
        vop("reciprocal", scr1[0:1, 5:6], scr1[0:1, 3:4], sem=S["sDV"])
        bld.wait("tensor", S["sDV"])
        for j, col in enumerate([4, 2, 5, 3]):
            mm(bank(2)[:, j:j + 1], onesf[:, :], scr1[0:1, col:col + 1], True, True,
               sem=(S["sPE"] if j == 3 else None))
        bld.wait("vector", S["sPE"])
        vop("tensor_copy", wsc[:, 0:4], bank(2)[:, 0:4], sem=S["sDV"])

        def wquant(wt, md, ntiles, wd, wscol):
            qdone = []
            for qn in ("sync",):
                bld.wait(qn, S["sAC"])
                bld.wait(qn, S["sDV"])
                bld.wait(qn, S["sPL"])
            for i in range(ntiles):
                buf = wqa if i % 2 == 0 else wqb
                qn = "sync"
                if i >= 2:
                    bld.wait(qn, S["sDV"], qdone[i - 2])
                bld.dma(qn, buf[:, 0:wd], wt[i, :, :], SW[i % 2])
                bld.wait("vector", SW[i % 2])
                vop("tensor_scalar", wq8[:, 0:wd], buf[:, 0:wd], wsc[:, wscol:wscol + 1],
                    1.0, op0=OP.mult, op1=OP.mult, sem=S["sDV"])
                qdone.append(S["sDV"].n)
                bld.wait("vector", S["sST"])
                vop("tensor_scalar", wqm[:, 0:wd], wq8[:, 0:wd], -1.0, 1.0,
                    op0=OP.max, op1=OP.min, sem=S["sDV"])
                bld.wait("scalar", S["sDV"])
                sdma(md[i, :, :], wqm[:, 0:wd])
        bld.wait("vector", S["sDV"])
        wquant(w1t, m1d, 32, 1024, 0)
        wquant(w2t, m2d, 8, 4096, 2)

        vop("memset", xmsc[:, 0:1024], 0.0, sem=S["sDV"])
        bld.wait("scalar", S["sDV"])
        r = 2050
        while r < WP:
            nr = min(126, WP - r)
            sdma(x2d[r:r + nr, :], xmsc[0:nr, 0:1024])
            r += nr
        sdma(x1d[0:1, :], xmsc[0:1, 0:1024])
        sdma(x1d[2049:2050, :], xmsc[0:1, 0:1024])
        bld.wait("scalar", S["sST"])
        barrier()

        def ln_quant_entry(xsrc):
            P = NT
            SXE = [SX[0], S["sXG"]]
            a_dv = []
            a_ac = []
            a_pl = []
            for t in range(NT):
                slot = t % 2
                qn = "sync"
                if t >= 2:
                    bld.wait(qn, S["sDV"], a_dv[t - 2])
                    bld.wait(qn, S["sAC"], a_ac[t - 2])
                    bld.wait(qn, S["sPL"], a_pl[t - 2])
                bld.dma(qn, xt[:, 1024 * slot:1024 * (slot + 1)],
                        xsrc[128 * t:128 * (t + 1), :], SXE[slot])
                xtile = xt[:, 1024 * slot:1024 * (slot + 1)]
                bld.wait("vector", SXE[slot])
                vop("reduce_sum", stA[:, 0 * P + t:0 * P + t + 1], xtile, AX.X, sem=S["sDV"])
                vop("tensor_scalar", stA[:, 2 * P + t:2 * P + t + 1],
                    stA[:, 0 * P + t:0 * P + t + 1], 1.0 / D, 1.0,
                    op0=OP.mult, op1=OP.mult, sem=S["sDV"])
                bld.wait("scalar", SXE[slot])
                act(sqsc[:, 0:1024], xtile, AF.Square,
                    accum=stA[:, 1 * P + t:1 * P + t + 1], sem=S["sAC"])
                a_ac.append(S["sAC"].n)
                bld.wait("gpsimd", SXE[slot])
                bld.wait("gpsimd", S["sDV"])
                vop("tensor_scalar", silub[:, 1024 * slot:1024 * (slot + 1)], xtile,
                    stA[:, 2 * P + t:2 * P + t + 1], 1.0,
                    op0=OP.subtract, op1=OP.mult, sem=S["sPL"], eng="gpsimd")
                a_pl.append(S["sPL"].n)
                bld.wait("vector", S["sPL"], a_pl[-1])
                vop("reduce_max", stA[:, 5 * P + t:5 * P + t + 1],
                    silub[:, 1024 * slot:1024 * (slot + 1)], AX.X,
                    apply_absolute_value=True, sem=S["sDV"])
                a_dv.append(S["sDV"].n)
            bld.wait("vector", S["sAC"])
            pl = lambda i: stA[:, i * P:(i + 1) * P]
            vop("tensor_tensor", pl(3), pl(2), pl(2), OP.mult, sem=S["sDV"])
            vop("tensor_scalar", pl(4), pl(1), 1.0 / D, 1.0, op0=OP.mult, op1=OP.mult, sem=S["sDV"])
            vop("tensor_tensor", pl(3), pl(4), pl(3), OP.subtract, sem=S["sDV"])
            bld.wait("scalar", S["sDV"])
            act(xmsc[:, 0:P], pl(3), AF.Ln, bias=epsv[:, 0:1], sem=S["sAC"])
            act(pl(4), xmsc[:, 0:P], AF.Exp, scale=-0.5, sem=S["sAC"])
            bld.wait("vector", S["sAC"])
            vop("tensor_tensor", pl(6), pl(4), pl(5), OP.mult, sem=S["sDV"])
            vop("tensor_scalar", pl(5), pl(6), 1e-5, 1.0, op0=OP.max, op1=OP.mult, sem=S["sDV"])
            vop("reciprocal", pl(6), pl(5), sem=S["sDV"])
            vop("tensor_tensor", pl(6), pl(6), pl(4), OP.mult, sem=S["sDV"])
            vop("tensor_scalar", pl(6), pl(6), 127.0, 1.0, op0=OP.mult, op1=OP.mult, sem=S["sDV"])
            vop("tensor_scalar", pl(7), pl(5), wsc[:, 1:2], 1.0 / 127.0,
                op0=OP.mult, op1=OP.mult, sem=S["sDV"])
            bld.wait("tensor", S["sDV"])
            bld.wait("tensor", S["sAC"])
            tr(ps[0:NT, 3584:3584 + 128], pl(7), identf[:], sem=S["sPE"])
            bld.wait("scalar", S["sPE"])
            act(o17[:, :], ps[0:NT, 3584:3584 + 128], AF.Copy, sem=S["sAC"])
            bld.wait("gpsimd", S["sAC"])
            bld.em("gpsimd", lambda q: q.dma_start(out=rowd[:, :], in_=o17[:, :]), S["sBC"], 16)
            bld.wait("gpsimd", S["sBC"])
            bld.em("gpsimd", lambda q: q.dma_start(
                out=oscrow[0:1, :], in_=rowd[:, :].rearrange("(o p) n -> o (p n)", o=1)),
                S["sBC"], 16)
            oscrow_ready = S["sBC"].n
            bld.wait("tensor", S["sAC"])
            trd = []
            b_pl = []
            b_cast = []
            b_trl = []
            for t in range(NT):
                slot = t % 2
                qn = "sync"
                if t >= 2:
                    bld.wait(qn, S["sPL"], b_pl[t - 2])
                else:
                    bld.wait(qn, S["sDV"], a_dv[16 - t])
                    bld.wait(qn, S["sAC"], a_ac[16 - t])
                    bld.wait(qn, S["sPL"], a_pl[16 - t])
                bld.dma(qn, xt[:, 1024 * slot:1024 * (slot + 1)],
                        xsrc[128 * t:128 * (t + 1), :], SXE[slot])
                xtile = xt[:, 1024 * slot:1024 * (slot + 1)]
                bld.wait("gpsimd", SXE[slot])
                if t == 0:
                    bld.wait("gpsimd", S["sDV"])
                if t >= 2:
                    bld.wait("gpsimd", S["sAC"], b_cast[t - 2])
                vop("tensor_scalar", n28[:, 1024 * slot:1024 * (slot + 1)], xtile,
                    stA[:, 2 * P + t:2 * P + t + 1], stA[:, 6 * P + t:6 * P + t + 1],
                    op0=OP.subtract, op1=OP.mult, sem=S["sPL"], eng="gpsimd")
                b_pl.append(S["sPL"].n)
                bld.wait("scalar", S["sPL"], b_pl[-1])
                if t >= 2:
                    bld.wait("scalar", S["sPE"], b_trl[t - 2])
                act(n2b[:, 1024 * slot:1024 * (slot + 1)],
                    n28[:, 1024 * slot:1024 * (slot + 1)], AF.Copy, sem=S["sAC"])
                b_cast.append(S["sAC"].n)
                bld.wait("tensor", S["sAC"], b_cast[-1])
                for k in range(8):
                    pslot = (ps[:, 3072:3072 + 256] if k % 2 == 0
                             else ps[:, 3584:3584 + 256]).bitcast(BF16)[:, 0:128]
                    if len(trd) >= 2:
                        bld.wait("tensor", trd[-2][0], trd[-2][1])
                    tr(pslot, n2b[:, 1024 * slot + 128 * k:1024 * slot + 128 * (k + 1)],
                       identb[:], sem=S["sPE"])
                    dst = nfm[:, WP * k + 128 * t:WP * k + 128 * (t + 1)]
                    if k % 2 == 0:
                        bld.wait("vector", S["sPE"])
                        vop("tensor_copy", dst, pslot, sem=S["sDV"])
                        trd.append((S["sDV"], S["sDV"].n))
                    else:
                        bld.wait("scalar", S["sPE"])
                        act(dst, pslot, AF.Copy, sem=S["sAC"])
                        trd.append((S["sAC"], S["sAC"].n))
                b_trl.append(S["sPE"].n)
            bld.wait("tensor", S["sDV"])
            bld.wait("tensor", S["sAC"])
            return oscrow_ready

        m1done = []
        m2done = []
        FPS = {"fd": [0] * 8, "resid": 0, "wave": [0] * 8, "hlast": 0, "xtst": [0, 0]}

        def ffn_call(xsrc, resdst, res_off):
            oscrow_ready = ln_quant_entry(xsrc)
            for si in range(NSP):
                o = 1 + SPW * si
                hw = SPW + 2
                bld.wait("tensor", S["sBC"], oscrow_ready)
                bld.wait("tensor", S["sAC"], FPS["fd"][6])
                mm(bank(6)[:, 0:hw], onesf[:, :], oscrow[0:1, o - 1:o - 1 + hw],
                   True, True, sem=S["sPE"])
                bld.wait("scalar", S["sPE"])
                act(oscb[:, 0:hw], bank(6)[:, 0:hw], AF.Copy, sem=S["sAC"])
                oscb_ready = S["sAC"].n
                bld.wait("gpsimd", S["sPE"], FPS["hlast"])
                for kk in range(8):
                    vop("tensor_copy", nbr[:, 258 * kk:258 * kk + hw],
                        nfm[:, WP * kk + o - 1:WP * kk + o - 1 + hw],
                        sem=S["sPL"], eng="gpsimd")
                nbr_ready = S["sPL"].n
                h_sem = []
                evg = []
                evA = []
                evC = []
                evS = []

                def emit_BC_silu(hc):
                    slot = hc % 2
                    gs = gb[:, 258 * slot:258 * slot + hw]
                    cs = c2b[:, 256 * slot:256 * (slot + 1)]
                    c3s = c3b[:, 256 * slot:256 * (slot + 1)]
                    bld.wait("vector", S["sPL"], evA[hc])
                    vop("scalar_tensor_tensor", cs, gs[:, 0:SPW],
                        cw[:, 4 * hc:4 * hc + 1], cs,
                        op0=OP.mult, op1=OP.add, sem=S["sDV"])
                    if hc >= 2:
                        bld.wait("vector", S["sAC"], evS[hc - 2])
                    vop("scalar_tensor_tensor", c3s, gs[:, 2:2 + SPW],
                        cw[:, 4 * hc + 2:4 * hc + 3], cs,
                        op0=OP.mult, op1=OP.add, sem=S["sDV"])
                    evC.append(S["sDV"].n)
                    bld.wait("scalar", S["sDV"], evC[hc])
                    act(silub[:, SPW * hc:SPW * hc + SPW], c3s, AF.Silu,
                        bias=cb[:, hc:hc + 1], sem=S["sAC"])
                    evS.append(S["sAC"].n)

                def emit_rmx(p):
                    bld.wait("vector", S["sAC"], evS[8 * p + 7])
                    vop("tensor_reduce", rmx4[:, 256 * p:256 * (p + 1)],
                        silub[:, 2048 * p:2048 * (p + 1)].rearrange(
                            "p (hc t) -> p t hc", hc=8),
                        AX.X, OP.max, apply_absolute_value=True, sem=S["sDV"])

                for hc in range(32):
                    mslot = len(m1done) % 4
                    if len(m1done) >= 4:
                        bld.wait("sync", S["sPE"], m1done[-4])
                    bld.dma("sync", m1sb[:, 1024 * mslot:1024 * (mslot + 1)],
                            m1d[hc, :, :], SM1[mslot])
                    hps = bank(hc % 2)[:, 0:hw]
                    bld.wait("tensor", SM1[mslot])
                    if hc == 0:
                        bld.wait("tensor", S["sPL"], nbr_ready)
                        bld.wait("tensor", S["sAC"], FPS["fd"][1])
                    if hc >= 2:
                        bld.wait("tensor", S["sDV"], evg[hc - 2])
                    for k in range(8):
                        mm(hps, m1sb[:, 1024 * mslot + 128 * k:1024 * mslot + 128 * (k + 1)],
                           nbr[:, 258 * k:258 * k + hw],
                           start=(k == 0), stop=(k == 7), sem=(S["sPE"] if k == 7 else None))
                    m1done.append(S["sPE"].n)
                    h_sem.append(S["sPE"].n)
                    bld.wait("vector", S["sPE"], h_sem[hc])
                    if hc == 0:
                        bld.wait("vector", S["sAC"], oscb_ready)
                    if hc >= 2:
                        bld.wait("vector", S["sPL"], evA[hc - 2])
                    vop("tensor_tensor", gb[:, 258 * (hc % 2):258 * (hc % 2) + hw],
                        hps, oscb[:, 0:hw], OP.mult, sem=S["sDV"])
                    evg.append(S["sDV"].n)
                    bld.wait("gpsimd", S["sDV"], evg[hc])
                    if hc >= 2:
                        bld.wait("gpsimd", S["sDV"], evC[hc - 2])
                    vop("tensor_scalar", c2b[:, 256 * (hc % 2):256 * (hc % 2 + 1)],
                        gb[:, 258 * (hc % 2) + 1:258 * (hc % 2) + 1 + SPW],
                        cw[:, 4 * hc + 1:4 * hc + 2], 1.0,
                        op0=OP.mult, op1=OP.mult, sem=S["sPL"], eng="gpsimd")
                    evA.append(S["sPL"].n)
                    if hc >= 1:
                        emit_BC_silu(hc - 1)
                    if hc >= 2 and (hc - 2) % 8 == 7:
                        emit_rmx((hc - 2) // 8)
                emit_BC_silu(31)
                emit_rmx(3)
                FPS["hlast"] = h_sem[-1]
                if fpart < 2:
                    if fpart < 1.2:
                        continue
                    pass
                vop("tensor_reduce", rmxf[:, 0:SPW],
                    rmx4[:, :].rearrange("p (g t) -> p t g", g=4),
                    AX.X, OP.max, sem=S["sDV"])
                rmx_dv = S["sDV"].n
                if fpart < 1.3:
                    continue
                for u in range(2):
                    cb0 = 16 * u
                    bld.wait("tensor", S["sDV"])
                    bld.wait("tensor", S["sAC"])
                    tr(ps[0:128, 3584:3584 + 128], rmxf[:, 128 * u:128 * (u + 1)],
                       identf[:, :], sem=S["sPE"])
                    bld.wait("vector", S["sPE"])
                    vop("reduce_max", xmsc[0:128, cb0:cb0 + 1],
                        ps[0:128, 3584:3584 + 128], AX.X, sem=S["sDV"])
                    if fpart < 1.5:
                        continue
                    vop("tensor_scalar", xmsc[0:128, cb0 + 1:cb0 + 2],
                        xmsc[0:128, cb0:cb0 + 1], 1e-5, 1.0,
                        op0=OP.max, op1=OP.mult, sem=S["sDV"])
                    vop("reciprocal", xmsc[0:128, cb0 + 2:cb0 + 3],
                        xmsc[0:128, cb0 + 1:cb0 + 2], sem=S["sDV"])
                    vop("tensor_scalar", xmsc[0:128, cb0 + 3:cb0 + 4],
                        xmsc[0:128, cb0 + 2:cb0 + 3], 127.0, 1.0,
                        op0=OP.mult, op1=OP.mult, sem=S["sDV"])
                    vop("tensor_scalar", o2c[0:128, u:u + 1],
                        xmsc[0:128, cb0 + 1:cb0 + 2], wsc[0:128, 3:4], 0.5 / 127.0,
                        op0=OP.mult, op1=OP.mult, sem=S["sDV"])
                    if fpart < 1.6:
                        continue
                    bld.wait("tensor", S["sDV"])
                    tr(ps[0:1, 3840:3840 + 128],
                       xmsc[0:128, cb0 + 3:cb0 + 4], identf[:], sem=S["sPE"])
                    bld.wait("scalar", S["sPE"])
                    act(s2row[0:1, 128 * u:128 * (u + 1)],
                        ps[0:1, 3840:3840 + 128], AF.Copy, sem=S["sAC"])
                bld.wait("tensor", S["sAC"])
                mm(bank(6)[:, 0:SPW], onesf[:, :], s2row[0:1, 0:SPW], True, True,
                   sem=S["sPE"])
                bld.wait("scalar", S["sPE"])
                act(s2b[:, 0:SPW], bank(6)[:, 0:SPW], AF.Copy, sem=S["sAC"])
                s2b_ready = S["sAC"].n
                if fpart < 3:
                    continue
                qv = []
                castv = []
                ocstop = [0] * 8
                for w in range(8):
                    m2slot = len(m2done) % 2
                    if len(m2done) >= 2:
                        bld.wait("gpsimd", S["sPE"], m2done[-2])
                    for oc in range(8):
                        bld.dma("gpsimd",
                                m2w[:, 4096 * m2slot + 512 * oc:4096 * m2slot + 512 * (oc + 1)],
                                m2d[oc, :, 512 * w:512 * (w + 1)], SM2[m2slot])
                    for j in range(4):
                        hc = 4 * w + j
                        if hc == 0:
                            bld.wait("vector", S["sAC"], s2b_ready)
                        if w >= 2 and j == 0:
                            bld.wait("vector", S["sAC"], castv[w - 2])
                        vop("tensor_tensor", n28[:, 256 * (hc % 8):256 * (hc % 8) + 256],
                            silub[:, SPW * hc:SPW * (hc + 1)], s2b[:, 0:SPW], OP.mult,
                            sem=S["sDV"])
                        qv.append(S["sDV"].n)
                    bld.wait("scalar", S["sDV"], qv[-1])
                    bld.wait("scalar", S["sPE"], FPS["wave"][w])
                    act(n2b[:, 1024 * w:1024 * (w + 1)],
                        n28[:, 1024 * (w % 2):1024 * (w % 2 + 1)], AF.Copy, sem=S["sAC"])
                    castv.append(S["sAC"].n)
                    if fpart < 4:
                        m2done.append(S["sPE"].n)
                        continue
                    bld.wait("tensor", SM2[m2slot])
                    bld.wait("tensor", S["sAC"], castv[w])
                    if w == 0:
                        bld.wait("tensor", S["sDV"], FPS["resid"])
                    if w < 7:
                        for j in range(4):
                            for oc in range(8):
                                mm(bank(oc)[:, 0:SPW],
                                   m2w[:, 4096 * m2slot + 512 * oc + 128 * j:
                                       4096 * m2slot + 512 * oc + 128 * (j + 1)],
                                   n2b[:, 256 * (4 * w + j):256 * (4 * w + j + 1)],
                                   start=(w == 0 and j == 0), stop=False,
                                   sem=(S["sPE"] if (j == 3 and oc == 7) else None))
                    else:
                        for oc in range(8):
                            for j in range(4):
                                mm(bank(oc)[:, 0:SPW],
                                   m2w[:, 4096 * m2slot + 512 * oc + 128 * j:
                                       4096 * m2slot + 512 * oc + 128 * (j + 1)],
                                   n2b[:, 256 * (28 + j):256 * (29 + j)],
                                   start=False, stop=(j == 3),
                                   sem=(S["sPE"] if j == 3 else None))
                            ocstop[oc] = S["sPE"].n
                    m2done.append(S["sPE"].n)
                    FPS["wave"][w] = S["sPE"].n
                if fpart < 4:
                    continue
                for oc in range(8):
                    bld.wait("scalar", S["sPE"], ocstop[oc])
                    act(fsb[:, 256 * oc:256 * (oc + 1)], bank(oc)[:, 0:SPW],
                        AF.Copy, sem=S["sAC"])
                    FPS["fd"][oc] = S["sAC"].n
                bld.wait("tensor", S["sAC"], FPS["fd"][7])
                for u in range(2):
                    xslot = u % 2
                    bld.wait("sync", S["sDV"])
                    bld.dma("sync", xt[0:128, 1024 * xslot:1024 * (xslot + 1)],
                            xsrc[o + 128 * u:o + 128 * u + 128, :], SX[xslot])
                    if u == 1:
                        bld.wait("tensor", S["sDV"], FPS["resid"])
                    for oc in range(8):
                        tr(ps[0:128, 2048 + 128 * oc:2048 + 128 * (oc + 1)],
                           fsb[:, 256 * oc + 128 * u:256 * oc + 128 * u + 128],
                           identf[:], sem=(S["sPE"] if oc == 7 else None))
                    bld.wait("vector", S["sPE"])
                    bld.wait("vector", SX[xslot])
                    bld.wait("vector", S["sT%d" % u], FPS["xtst"][u])
                    vop("scalar_tensor_tensor", xtm[0:128, 1024 * u:1024 * (u + 1)],
                        ps[0:128, 2048:3072], o2c[0:128, u:u + 1],
                        xt[0:128, 1024 * xslot:1024 * (xslot + 1)],
                        op0=OP.mult, op1=OP.add, sem=S["sDV"])
                    FPS["resid"] = S["sDV"].n
                    bld.wait("scalar", S["sDV"], FPS["resid"])
                    sdma(resdst[o + res_off + 128 * u:o + res_off + 128 * u + 128, :],
                         xtm[0:128, 1024 * u:1024 * (u + 1)], sem=S["sT%d" % u])
                    FPS["xtst"][u] = S["sT%d" % u].n
            bld.wait("scalar", S["sST"])
            bld.wait("scalar", S["sT0"])
            bld.wait("scalar", S["sT1"])

        def finish_debug(srcap):
            bld.em("gpsimd", lambda q, s=srcap: q.dma_start(out=xout[:, :], in_=s),
                   S["sBC"], 16)
            bld.wait("gpsimd", S["sBC"])

        if upto == "pre":
            finish_debug(x2d[0:2048, :])
        if upto in ("full", "ffn1", "ln2", "attn", "wr"):
            ffn_call(xin, x1d, 0)
            barrier()
        if upto == "ffn1":
            finish_debug(x1d[1:2049, :])

        RUN_LN2 = upto in ("full", "ln2", "attn", "wr")
        if RUN_LN2:
            P = NT
            for t in range(16):
                slot = t % 2
                if t >= 2:
                    bld.wait("sync", S["sDV"])
                    bld.wait("sync", S["sAC"])
                bld.dma("sync", xt[:, 1024 * slot:1024 * (slot + 1)],
                        x1d[1 + 128 * t:1 + 128 * (t + 1), :], SX[slot])
                xtile = xt[:, 1024 * slot:1024 * (slot + 1)]
                bld.wait("vector", SX[slot])
                vop("reduce_sum", stA[:, t:t + 1], xtile, AX.X, sem=S["sDV"])
                bld.wait("scalar", SX[slot])
                act(sqsc[:, 0:1024], xtile, AF.Square, accum=stA[:, P + t:P + t + 1], sem=S["sAC"])
                vop("tensor_scalar", stA[:, 2 * P + t:2 * P + t + 1], stA[:, t:t + 1],
                    1.0 / D, 1.0, op0=OP.mult, op1=OP.mult, sem=S["sDV"])
            bld.wait("vector", S["sAC"])
            pl = lambda i: stA[:, i * P:i * P + 16]
            vop("tensor_tensor", pl(3), pl(2), pl(2), OP.mult, sem=S["sDV"])
            vop("tensor_scalar", pl(4), pl(1), 1.0 / D, 1.0, op0=OP.mult, op1=OP.mult, sem=S["sDV"])
            vop("tensor_tensor", pl(3), pl(4), pl(3), OP.subtract, sem=S["sDV"])
            bld.wait("scalar", S["sDV"])
            act(xmsc[:, 0:16], pl(3), AF.Ln, bias=epsv[:, 0:1], sem=S["sAC"])
            act(pl(4), xmsc[:, 0:16], AF.Exp, scale=-0.5, sem=S["sAC"])
            bld.wait("vector", S["sAC"])
            y2done = []
            for t in range(16):
                slot = t % 2
                bld.wait("sync", S["sDV"])
                bld.wait("sync", S["sAC"])
                bld.dma("sync", xt[:, 1024 * slot:1024 * (slot + 1)],
                        x1d[1 + 128 * t:1 + 128 * (t + 1), :], SX[slot])
                xtile = xt[:, 1024 * slot:1024 * (slot + 1)]
                bld.wait("vector", SX[slot])
                if t >= 1:
                    bld.wait("vector", S["sPE"])
                vop("tensor_scalar", ntmb[:, 0:1024], xtile, stA[:, 2 * P + t:2 * P + t + 1],
                    stA[:, 4 * P + t:4 * P + t + 1], op0=OP.subtract, op1=OP.mult, sem=S["sDV"])
                bld.wait("tensor", S["sDV"])
                for k in range(8):
                    pslot = (ps[:, 3072:3072 + 256] if k % 2 == 0
                             else ps[:, 3584:3584 + 256]).bitcast(BF16)[:, 0:128]
                    if len(y2done) >= 2:
                        bld.wait("tensor", S["sDV"], y2done[-2])
                    tr(pslot, ntmb[:, 128 * k:128 * (k + 1)], identb[:], sem=S["sPE"])
                    bld.wait("vector", S["sPE"])
                    dst = y2fm if k < 4 else y2own
                    kk = k % 4
                    vop("tensor_copy", dst[:, TOK * kk + 128 * t:TOK * kk + 128 * (t + 1)],
                        pslot, sem=S["sDV"])
                    y2done.append(S["sDV"].n)
            bld.wait("vector", S["sDV"])
            for ch in range(4):
                lo = y2fm[:, TOK * ch:TOK * (ch + 1)]
                hi = y2own[:, TOK * ch:TOK * (ch + 1)]
                vop("tensor_scalar", mrg[:, 0:TOK], hi, msk[:, 1:2], 1.0,
                    op0=OP.mult, op1=OP.mult, sem=S["sDV"])
                vop("scalar_tensor_tensor", mrg[:, 0:TOK], lo, msk[:, 0:1], mrg[:, 0:TOK],
                    op0=OP.mult, op1=OP.add, sem=S["sDV"])
                bld.wait("scalar", S["sDV"])
                sdma(y2ci[128 * ch:128 * (ch + 1), :], mrg[:, 0:TOK])
                vop("tensor_scalar", mrg[:, TOK:2 * TOK], lo, msk[:, 1:2], 1.0,
                    op0=OP.mult, op1=OP.mult, sem=S["sDV"])
                vop("scalar_tensor_tensor", mrg[:, TOK:2 * TOK], hi, msk[:, 0:1],
                    mrg[:, TOK:2 * TOK], op0=OP.mult, op1=OP.add, sem=S["sDV"])
            bld.wait("vector", S["sST"])
            for ch in range(4):
                lo = y2fm[:, TOK * ch:TOK * (ch + 1)]
                hi = y2own[:, TOK * ch:TOK * (ch + 1)]
                vop("tensor_scalar", sqsc[:, 0:1024].rearrange("p n -> p n") if False else mrg[:, 0:TOK],
                    lo, msk[:, 1:2], 1.0, op0=OP.mult, op1=OP.mult, sem=S["sDV"])
                vop("scalar_tensor_tensor", y2own[:, TOK * ch:TOK * (ch + 1)], hi, msk[:, 0:1],
                    mrg[:, 0:TOK], op0=OP.mult, op1=OP.add, sem=S["sDV"])
            bld.wait("gpsimd", S["sST"])
            bld.wait("gpsimd", S["sDV"])
            bld.em("gpsimd", lambda q: q.collective_compute(
                "AllGather", OP.bypass, replica_groups=[[0, 1], [2, 3], [4, 5], [6, 7]],
                ins=[y2ci[:, :]], outs=[y2go[:, :]]), S["sCC"], 1)
            bld.wait("sync", S["sCC"])
            for half in range(2):
                row0 = 0 if half == 0 else 512
                for ch in range(4):
                    bld.wait("sync", S["sDV"])
                    bld.dma("sync", mrg[:, 0:TOK],
                            y2go[row0 + 128 * ch:row0 + 128 * (ch + 1), :], SX[0])
                    own = y2own[:, TOK * ch:TOK * (ch + 1)]
                    dst = y2hh[:, SEQ * ch + TOK * half:SEQ * ch + TOK * (half + 1)]
                    bld.wait("vector", SX[0])
                    if half == 0:
                        vop("tensor_scalar", mgs[:, 0:TOK], own, msk[:, 1:2], 1.0,
                            op0=OP.mult, op1=OP.mult, sem=S["sDV"])
                        vop("scalar_tensor_tensor", dst, mrg[:, 0:TOK], msk[:, 0:1],
                            mgs[:, 0:TOK], op0=OP.mult, op1=OP.add, sem=S["sDV"])
                    else:
                        vop("tensor_scalar", mgs[:, 0:TOK], mrg[:, 0:TOK], msk[:, 1:2], 1.0,
                            op0=OP.mult, op1=OP.mult, sem=S["sDV"])
                        vop("scalar_tensor_tensor", dst, own, msk[:, 0:1],
                            mgs[:, 0:TOK], op0=OP.mult, op1=OP.add, sem=S["sDV"])
            barrier()

        if upto == "ln2":
            bld.em("gpsimd", lambda q: q.dma_start(
                out=xout[0:1024, :].bitcast(BF16), in_=y2go[:, :]), S["sBC"], 16)
            bld.wait("gpsimd", S["sBC"])
        if upto in ("full", "attn", "wr"):
            pav = ps[0:65, 3072:3584]
            pzb = ps[0:64, 3584:3584 + 512]
            ptmp = ps[0:1, 3584:3584 + 65]
            rot_expdone = [0, 0]
            fills_done = [0, 0]
            qkdone = [0, 0]
            av_guard = 0
            for k in range(nheads):
                ch4 = k // 2
                pbase = 64 * (k % 2)

                def yh(c0, w_, ch4=ch4, pbase=pbase):
                    return y2hh[pbase:pbase + 64, SEQ * ch4 + c0:SEQ * ch4 + c0 + w_]

                for which, (wsl0, wsl1, bcol, scl) in enumerate(
                        [(0, 64, 0, 0.125), (64, 128, 1, None)]):
                    for cq in range(8):
                        s = (which * 8 + cq) % 2
                        pqk = ps[0:64, 1536 * s:1536 * s + 512]
                        bld.wait("tensor", S["sAC"], qkdone[s])
                        if which == 0 and cq == 0:
                            bld.wait("tensor", S["sDV"], av_guard)
                            bld.wait("tensor", S["sAC"], max(rot_expdone))
                        mm(pqk, wqkb[pbase:pbase + 64, wsl0:wsl1], yh(512 * cq, 512),
                           True, True, sem=S["sPE"])
                        bld.wait("scalar", S["sPE"])
                        act(qk2[0:64, SEQ * which + 512 * cq:SEQ * which + 512 * (cq + 1)], pqk,
                            AF.Identity, bias=bq8[pbase:pbase + 64, bcol:bcol + 1], scale=scl,
                            sem=S["sAC"])
                        qkdone[s] = S["sAC"].n
                vext_u = vext
                Ebuf_u = Ebuf
                vdone = [0, 0]
                for kb in range(32):
                    s = kb % 2
                    pv = ps[:, 512 + 512 * s:512 + 512 * s + 65]
                    bld.wait("tensor", S["sDV"], vdone[s])
                    mm(pv, yh(128 * kb, 128), wv65[pbase:pbase + 64, :], True, True,
                       sem=S["sPE"])
                    bld.wait("vector", S["sPE"])
                    vop("tensor_tensor", vext_u[:, 65 * kb:65 * (kb + 1)], pv, bvb[:, :],
                        OP.add, sem=S["sDV"])
                    vdone[s] = S["sDV"].n
                bld.wait("tensor", S["sDV"])
                bld.wait("tensor", S["sAC"])
                chunk_range = [] if attn_level == 0 else list(range(7, -1, -1))
                for ci, c in enumerate(chunk_range):
                    nblk = 4 * c + 4
                    glist = [(g0, min(3, nblk - g0)) for g0 in range(0, nblk, 3)]
                    first_av = True
                    for gi, (g0, gn) in enumerate(glist):
                        r = gi % 2
                        rot = r * 1536
                        bld.wait("tensor", S["sAC"], rot_expdone[r])
                        for j in range(gn):
                            b = g0 + j
                            mm(ps[0:128, rot + 512 * j:rot + 512 * j + 512],
                               qk2[0:64, SEQ + 128 * b:SEQ + 128 * (b + 1)],
                               qk2[0:64, 512 * c:512 * (c + 1)], True, True,
                               sem=(S["sPE"] if j == gn - 1 else None))
                        bld.wait("scalar", S["sPE"])
                        bld.wait("scalar", S["sPL"], fills_done[r])
                        act(Ebuf_u[:, 1536 * r:1536 * r + 512 * gn],
                            ps[:, rot:rot + 512 * gn], AF.Exp, sem=S["sAC"])
                        rot_expdone[r] = S["sAC"].n
                        nfills = 0
                        for j in range(gn):
                            b = g0 + j
                            if b >= 4 * c:
                                if nfills == 0:
                                    bld.wait("gpsimd", S["sAC"], rot_expdone[r])
                                eap = Ebuf_u[:, 1536 * r + 512 * j:1536 * r + 512 * (j + 1)]
                                bld.em("gpsimd", lambda q, eap=eap, bb=b, cc=c:
                                       q.affine_select(eap, eap, [[1, 512]], OP.is_ge,
                                                       fillreg["one"],
                                                       base=512 * cc - 128 * bb - 1,
                                                       channel_multiplier=-1),
                                       S["sPL"], 1)
                                nfills += 1
                        if nfills:
                            fills_done[r] = S["sPL"].n
                            bld.wait("tensor", S["sPL"], fills_done[r])
                        if attn_level < 2:
                            continue
                        if first_av:
                            bld.wait("tensor", S["sDV"], av_guard)
                            first_av = False
                        for j in range(gn):
                            b = g0 + j
                            stop = (c == 7 and b == nblk - 1)
                            mm(pav, vext_u[:, 65 * b:65 * (b + 1)],
                               Ebuf_u[:, 1536 * r + 512 * j:1536 * r + 512 * (j + 1)],
                               start=(b == 0), stop=stop,
                               sem=(S["sPE"] if stop else None))
                    if attn_level < 2:
                        av_guard = S["sDV"].n
                        continue
                    if c < 7:
                        mm(pav, vsufB[0:1, 0:65], onesb[0:1, 0:512], start=False, stop=True,
                           sem=S["sPE"])
                    if attn_level < 3:
                        bld.wait("vector", S["sPE"])
                        vop("tensor_copy", aosb[0:64, 0:512], ps[0:64, 3072:3584], sem=S["sDV"])
                        av_guard = S["sDV"].n
                        continue
                    bld.wait("vector", S["sPE"])
                    vop("reciprocal", rcpz[0:1, 0:512], ps[64:65, 3072:3584], sem=S["sDV"])
                    vop("tensor_copy", rcpb[0:1, 0:512], rcpz[0:1, 0:512], sem=S["sDV"])
                    vop("scalar_tensor_tensor", rcpb[0:1, 512:1024], rcpb[0:1, 0:512], -1.0,
                        rcpz[0:1, 0:512], op0=OP.mult, op1=OP.add, sem=S["sDV"])
                    bld.wait("tensor", S["sDV"])
                    mm(pzb, onesb[0:1, 0:64], rcpb[0:1, 0:512], True, False)
                    mm(pzb, onesb[0:1, 0:64], rcpb[0:1, 512:1024], False, True, sem=S["sPE"])
                    bld.wait("vector", S["sPE"])
                    aslot = aosb[0:64, 512 * (ci % 2):512 * (ci % 2) + 512]
                    bld.wait("vector", S["sST"])
                    vop("tensor_copy", aslot, ps[0:64, 3072:3584], sem=S["sDV"])
                    bld.wait("vector", S["sDV"])
                    vop("tensor_tensor", aslot, aslot, pzb, OP.mult, sem=S["sDV"])
                    zb_read = S["sDV"].n
                    bld.wait("scalar", S["sDV"])
                    sdma(aod[64 * k:64 * (k + 1), 512 * c:512 * (c + 1)], aslot)
                    if c > 0:
                        bld.wait("tensor", S["sDV"], zb_read)
                        for j in range(4):
                            b = 4 * c + j
                            mm(ptmp, onesb[:, 0:1], vext_u[:, 65 * b:65 * (b + 1)],
                               start=(j == 0), stop=(j == 3), sem=(S["sPE"] if j == 3 else None))
                        bld.wait("vector", S["sPE"])
                        if c == 7:
                            vop("tensor_copy", vsufA[0:1, 0:65], ptmp, sem=S["sDV"])
                        else:
                            vop("tensor_tensor", vsufA[0:1, 0:65], vsufA[0:1, 0:65], ptmp,
                                OP.add, sem=S["sDV"])
                        vop("tensor_copy", vsufB[0:1, 0:65], vsufA[0:1, 0:65], sem=S["sDV"])
                    av_guard = S["sDV"].n
            bld.wait("tensor", S["sDV"], av_guard)
            bld.wait("scalar", S["sST"])
            barrier()

        if upto == "attn":
            bld.em("gpsimd", lambda q: q.dma_start(
                out=xout[0:1024, :].bitcast(BF16),
                in_=aod[:, :].rearrange("p (a n) -> (p a) n", a=2)), S["sBC"], 16)
            bld.wait("gpsimd", S["sBC"])
        if upto in ("full", "wr"):
            cA = lambda ch: candA[:, 2050 * ch:2050 * (ch + 1)]
            cB = lambda ch: candB[:, 2050 * ch:2050 * (ch + 1)]
            ow = lambda ch: ownw[:, 2050 * ch:2050 * (ch + 1)]
            for ch in range(4):
                vop("memset", cA(ch)[:, 0:1], 0.0, sem=S["sDV"])
                vop("memset", cB(ch)[:, 2049:2050], 0.0, sem=S["sDV"])
            bld.wait("sync", S["sDV"])
            for ch in range(4):
                bld.dma("sync", cA(ch)[:, 1:2050], aod[128 * ch:128 * (ch + 1), 0:2049], SX[0])
                bld.dma("sync", cB(ch)[:, 0:2049], aod[128 * ch:128 * (ch + 1), 2047:4096], SX[0])
            bld.wait("vector", SX[0])
            for ch in range(4):
                vop("tensor_scalar", ow(ch), cB(ch), msk[:, 1:2], 1.0,
                    op0=OP.mult, op1=OP.mult, sem=S["sDV"])
                vop("scalar_tensor_tensor", ow(ch), cA(ch), msk[:, 0:1], ow(ch),
                    op0=OP.mult, op1=OP.add, sem=S["sDV"])
                bld.wait("scalar", S["sDV"])
                sdma(aoci[128 * ch:128 * (ch + 1), :], ow(ch))
            bld.dma("sync", hbf[0:1, 0:2048].bitcast(F32), x1d[1:2, :], SX[0])
            bld.dma("sync", hbf[0:1, 2048:4096].bitcast(F32), x1d[2048:2049, :], SX[0])
            bld.wait("scalar", SX[0])
            sdma(aoci[512:513, 0:2048], hbf[0:1, 0:2048])
            sdma(aoci[513:514, 0:2048], hbf[0:1, 2048:4096])
            bld.wait("gpsimd", S["sST"])
            bld.em("gpsimd", lambda q: q.collective_compute(
                "AllGather", OP.bypass, replica_groups=[[0, 1], [2, 3], [4, 5], [6, 7]],
                ins=[aoci[:, :]], outs=[aogo[:, :]]), S["sCC"], 1)
            bld.wait("vector", S["sST"])
            for ch in range(4):
                vop("tensor_scalar", ow(ch), cA(ch), msk[:, 1:2], 1.0,
                    op0=OP.mult, op1=OP.mult, sem=S["sDV"])
                vop("scalar_tensor_tensor", ow(ch), cB(ch), msk[:, 0:1], ow(ch),
                    op0=OP.mult, op1=OP.add, sem=S["sDV"])
            bld.wait("sync", S["sCC"])
            bld.wait("sync", S["sDV"])
            for ch in range(4):
                bld.dma("sync", cA(ch), aogo[128 * ch:128 * (ch + 1), 0:2050], SX[0])
                bld.dma("sync", cB(ch), aogo[514 + 128 * ch:514 + 128 * (ch + 1), 0:2050], SX[0])
            bld.wait("vector", SX[0])
            for ch in range(4):
                vop("tensor_scalar", cB(ch), cB(ch), msk[:, 1:2], 1.0,
                    op0=OP.mult, op1=OP.mult, sem=S["sDV"])
                vop("scalar_tensor_tensor", cB(ch), cA(ch), msk[:, 0:1], cB(ch),
                    op0=OP.mult, op1=OP.add, sem=S["sDV"])
            for ch in range(4):
                vop("tensor_scalar", cA(ch), ow(ch), msk[:, 1:2], 1.0,
                    op0=OP.mult, op1=OP.mult, sem=S["sDV"])
                vop("scalar_tensor_tensor", awin[:, 2050 * ch:2050 * (ch + 1)], cB(ch),
                    msk[:, 0:1], cA(ch), op0=OP.mult, op1=OP.add, sem=S["sDV"])
            for ch in range(4):
                vop("tensor_scalar", cA(ch), cB(ch), msk[:, 1:2], 1.0,
                    op0=OP.mult, op1=OP.mult, sem=S["sDV"])
                vop("scalar_tensor_tensor", awin[:, 2050 * (ch + 4):2050 * (ch + 5)],
                    ow(ch), msk[:, 0:1], cA(ch), op0=OP.mult, op1=OP.add, sem=S["sDV"])
            bld.wait("sync", S["sDV"])
            bld.dma("sync", hbf[0:1, 0:2048], aogo[1026:1027, 0:2048], SX[0])
            bld.dma("sync", hbf[0:1, 2048:4096], aogo[513:514, 0:2048], SX[0])
            bld.wait("vector", SX[0])
            vop("tensor_scalar", halo[0:1, 0:1024], hbf[0:1, 0:2048].bitcast(F32),
                msk[0:1, 1:2], 1.0, op0=OP.mult, op1=OP.mult, sem=S["sDV"])
            vop("scalar_tensor_tensor", halo[0:1, 0:1024], hbf[0:1, 2048:4096].bitcast(F32),
                msk[0:1, 0:1], halo[0:1, 0:1024], op0=OP.mult, op1=OP.add, sem=S["sDV"])
            wrdone = []
            for si, (o, width) in enumerate(WR_SPANS):
                nu = (width + 127) // 128
                for oc in range(8):
                    fps = bank(2 + oc % 2)[:, 0:width]
                    if len(wrdone) >= 2:
                        bld.wait("tensor", S["sAC"], wrdone[-2])
                    if si == 0 and oc == 0:
                        bld.wait("tensor", S["sDV"])
                    for kc in range(8):
                        mm(fps, wrB[:, 1024 * kc + 128 * oc:1024 * kc + 128 * (oc + 1)],
                           awin[:, 2050 * kc + o:2050 * kc + o + width],
                           start=(kc == 0), stop=False)
                    mm(fps, brb[0:1, 128 * oc:128 * (oc + 1)], onesb[0:1, 0:width],
                       start=False, stop=True, sem=S["sPE"])
                    bld.wait("scalar", S["sPE"])
                    act(wfs[:, 384 * oc:384 * oc + width], fps, AF.Copy, sem=S["sAC"])
                    wrdone.append(S["sAC"].n)
                bld.wait("tensor", S["sAC"])
                for u in range(nu):
                    blk = min(128, width - 128 * u)
                    w0 = o + 128 * u
                    bld.wait("sync", S["sDV"])
                    bld.dma("sync", wxt[0:blk, 0:1024], x1d[w0:w0 + blk, :], SX[0])
                    pT = ps[:, 2048:3072]
                    bld.wait("tensor", S["sDV"])
                    for oc in range(8):
                        tr(pT[0:blk, 128 * oc:128 * (oc + 1)],
                           wfs[:, 384 * oc + 128 * u:384 * oc + 128 * u + blk],
                           identf[:], sem=(S["sPE"] if oc == 7 else None))
                    bld.wait("vector", S["sPE"])
                    bld.wait("vector", SX[0])
                    bld.wait("vector", S["sST"])
                    vop("tensor_tensor", wxm[0:blk, 0:1024], pT[0:blk, 0:1024],
                        wxt[0:blk, 0:1024], OP.add, sem=S["sDV"])
                    bld.wait("scalar", S["sDV"])
                    sdma(x2d[w0:w0 + blk, :], wxm[0:blk, 0:1024])
            for wrow, mcol in [(0, 0), (2049, 1)]:
                bld.wait("sync", S["sST"])
                bld.dma("sync", wxm[0:1, 0:1024], x2d[wrow:wrow + 1, :], SX[0])
                bld.wait("vector", SX[0])
                bld.wait("vector", S["sST"])
                vop("tensor_tensor", wxm[0:1, 0:1024], wxm[0:1, 0:1024], halo[0:1, :],
                    OP.add, sem=S["sDV"])
                vop("tensor_scalar", wxm[0:1, 0:1024], wxm[0:1, 0:1024], msk[0:1, mcol:mcol + 1],
                    1.0, op0=OP.mult, op1=OP.mult, sem=S["sDV"])
                bld.wait("scalar", S["sDV"])
                sdma(x2d[wrow:wrow + 1, :], wxm[0:1, 0:1024])
            bld.wait("scalar", S["sST"])
            barrier()

        if upto == "wr":
            finish_debug(x2d[0:2048, :])
        if upto == "full":
            ffn_call(x2d, xout, -1)
        bld.wait("gpsimd", S["sST"])

        @block.sync
        def _(q):
            for f in bld.ev["sync"]:
                f(q)

        @block.tensor
        def _(q):
            for f in bld.ev["tensor"]:
                f(q)

        @block.vector
        def _(q):
            for f in bld.ev["vector"]:
                f(q)

        @block.scalar
        def _(q):
            for f in bld.ev["scalar"]:
                f(q)

        @block.gpsimd
        def _(q):
            for f in bld.ev["gpsimd"]:
                f(q)

    return nc


_NC_CACHE = None


def _get_nc():
    global _NC_CACHE
    if _NC_CACHE is None:
        _NC_CACHE = build_program()
    return _NC_CACHE


def kernel(L0_attn_x, L0_attn_Wqkv, L0_attn_bqkv, L0_attn_Wr, L0_attn_br,
           L0_ffn_w1, L0_ffn_conv_w, L0_ffn_conv_b, L0_ffn_w2):
    x = np.asarray(L0_attn_x, dtype=np.float32)
    w1 = np.asarray(L0_ffn_w1, dtype=np.float32)
    w2 = np.asarray(L0_ffn_w2, dtype=np.float32)
    conv_w = np.asarray(L0_ffn_conv_w, dtype=np.float32)
    conv_b = np.asarray(L0_ffn_conv_b, dtype=np.float32)
    wqkv = np.asarray(L0_attn_Wqkv, dtype=np.float32)
    bqkv = np.asarray(L0_attn_bqkv, dtype=np.float32)
    wr = np.asarray(L0_attn_Wr, dtype=np.float32)
    br = np.asarray(L0_attn_br, dtype=np.float32)

    w1tiles = np.ascontiguousarray(
        w1.reshape(32, 128, 8, 128).transpose(0, 3, 2, 1).reshape(32, 128, 1024))
    w2tiles = np.ascontiguousarray(
        w2.reshape(8, 128, 32, 128).transpose(0, 3, 2, 1).reshape(8, 128, 4096))
    cwh = np.zeros((128, 128), np.float32)
    cwh.reshape(128, 32, 4)[:, :, 0:3] = conv_w[:, 0, :].reshape(32, 128, 3).transpose(1, 0, 2)
    cbh = np.ascontiguousarray(conv_b.reshape(32, 128).T)
    bqkvh = np.ascontiguousarray(np.tile(bqkv.reshape(3, 64).T, (2, 1)))
    wqkvh = np.ascontiguousarray(np.tile(wqkv, (2, 1)))
    bvbh = np.ones((128, 65), np.float32)
    bvbh[:, 0:64] = bqkv[128:192][None, :]
    brh = np.ascontiguousarray(br.reshape(1, 1024))
    iden = np.eye(128, dtype=np.float32)

    in_maps = []
    for c in range(8):
        b, p = c // 2, c % 2
        xw = np.zeros((WP, D), np.float32)
        xw[1:2049] = x[b, 2048 * p:2048 * p + 2048]
        if p == 0:
            xw[2049] = x[b, 2048]
        else:
            xw[0] = x[b, 2047]
        mskh = np.zeros((128, 4), np.float32)
        mskh[:, 0] = float(p)
        mskh[:, 1] = 1.0 - float(p)
        in_maps.append({
            "xin": xw, "w1t": w1tiles, "w2t": w2tiles, "cwi": cwh, "cbi": cbh,
            "wqkvi": wqkvh, "bqkvi": bqkvh, "bvbi": bvbh, "wri": wr, "bri": brh,
            "ideni": iden, "mski": mskh,
        })

    nc = _get_nc()
    res = run_bass_kernel_spmd(nc, in_maps, core_ids=list(range(8)))
    out = np.empty((4, 4096, 1024), np.float32)
    for c in range(8):
        b, p = c // 2, c % 2
        out[b, 2048 * p:2048 * p + 2048] = res.results[c]["xout"]
    return out


# version 07
# speedup vs baseline: 1.1894x; 1.1247x over previous
import numpy as np
import ml_dtypes
from contextlib import ExitStack

import concourse.bass as bass
import concourse.mybir as mybir
from concourse.alu_op_type import AluOpType as OP
from concourse.bass_utils import run_bass_kernel_spmd

F32 = mybir.dt.float32
F32R = mybir.dt.float32r
BF16 = mybir.dt.bfloat16
I8 = mybir.dt.int8
AF = mybir.ActivationFunctionType
AX = mybir.AxisListType

D = 1024
DH = 4096
SEQ = 4096
TOK = 2048
W = 2050
WP = 2176
NT = 17
SPW = 256
NSP = 8
WR_SPANS = [(0, 384), (384, 384), (768, 384), (1152, 384), (1536, 384), (1920, 130)]
ENGS = ("sync", "tensor", "vector", "scalar", "gpsimd")


class Sem:
    def __init__(self, h):
        self.h = h
        self.n = 0


class B:
    def __init__(self):
        self.ev = {e: [] for e in ENGS}

    def em(self, eng, fn, sem=None, inc=1, selfwait=False):
        if sem is not None:
            sem.n += inc
            def wrapped(q, fn=fn, s=sem.h, k=inc):
                fn(q).then_inc(s, k)
            self.ev[eng].append(wrapped)
            if selfwait:
                v = sem.n
                self.ev[eng].append(lambda q, s=sem.h, v=v: q.wait_ge(s, v))
        else:
            self.ev[eng].append(lambda q, fn=fn: fn(q))

    def wait(self, eng, sem, val=None):
        v = sem.n if val is None else val
        if v > 0:
            self.ev[eng].append(lambda q, s=sem.h, v=v: q.wait_ge(s, v))

    def dma(self, eng, out, in_, sem):
        self.em(eng, lambda q, o=out, i=in_: q.dma_start(out=o, in_=i), sem, 16)


def build_program(upto="full", nheads=8, attn_level=3, fpart=4):
    nc = bass.Bass("TRN2", target_bir_lowering=False)

    xin = nc.dram_tensor("xin", [WP, D], F32, kind="ExternalInput")
    w1t = nc.dram_tensor("w1t", [32, 128, 1024], F32, kind="ExternalInput")
    w2t = nc.dram_tensor("w2t", [8, 128, 4096], F32, kind="ExternalInput")
    cwi = nc.dram_tensor("cwi", [128, 128], F32, kind="ExternalInput")
    cbi = nc.dram_tensor("cbi", [128, 32], F32, kind="ExternalInput")
    wqkvi = nc.dram_tensor("wqkvi", [128, 192], F32, kind="ExternalInput")
    bqkvi = nc.dram_tensor("bqkvi", [128, 3], F32, kind="ExternalInput")
    bvbi = nc.dram_tensor("bvbi", [128, 65], F32, kind="ExternalInput")
    wri = nc.dram_tensor("wri", [1024, 1024], F32, kind="ExternalInput")
    bri = nc.dram_tensor("bri", [1, 1024], F32, kind="ExternalInput")
    ideni = nc.dram_tensor("ideni", [128, 128], F32, kind="ExternalInput")
    mski = nc.dram_tensor("mski", [128, 4], F32, kind="ExternalInput")

    xout = nc.dram_tensor("xout", [TOK, D], F32, kind="ExternalOutput")

    m1d = nc.dram_tensor("m1d", [32, 128, 1024], BF16)
    m2d = nc.dram_tensor("m2d", [8, 128, 4096], BF16)
    x1d = nc.dram_tensor("x1d", [WP, D], F32)
    x2d = nc.dram_tensor("x2d", [WP, D], F32)
    aod = nc.dram_tensor("aod", [512, 4096], BF16)
    rowd = nc.dram_tensor("rowd", [17, 128], F32)
    y2ci = nc.dram_tensor("y2ci", [512, TOK], BF16)
    y2go = nc.dram_tensor("y2go", [1024, TOK], BF16)
    aoci = nc.dram_tensor("aoci", [514, 2050], BF16)
    aogo = nc.dram_tensor("aogo", [1028, 2050], BF16)

    bld = B()

    with ExitStack() as ctx:
        e = ctx.enter_context

        identf = e(nc.sbuf_tensor("identf", [128, 128], F32))
        identb = e(nc.sbuf_tensor("identb", [128, 128], BF16))
        cw = e(nc.sbuf_tensor("cw", [128, 128], F32))
        cb = e(nc.sbuf_tensor("cb", [128, 32], F32))
        bqk = e(nc.sbuf_tensor("bqk", [128, 3], F32))
        bq8 = e(nc.sbuf_tensor("bq8", [128, 2], F32))
        wqkf = e(nc.sbuf_tensor("wqkf", [128, 192], F32))
        wqkb = e(nc.sbuf_tensor("wqkb", [128, 192], BF16))
        wv65 = e(nc.sbuf_tensor("wv65", [128, 65], BF16))
        bvb = e(nc.sbuf_tensor("bvb", [128, 65], F32))
        brrow = e(nc.sbuf_tensor("brrow", [1, 1024], F32))
        brb = e(nc.sbuf_tensor("brb", [1, 1024], BF16))
        msk = e(nc.sbuf_tensor("msk", [128, 4], F32))
        onesf = e(nc.sbuf_tensor("onesf", [1, 128], F32))
        onescol = e(nc.sbuf_tensor("onescol", [128, 1], F32))
        onesb = e(nc.sbuf_tensor("onesb", [128, 512], BF16))
        epsv = e(nc.sbuf_tensor("epsv", [128, 1], F32))
        wsc = e(nc.sbuf_tensor("wsc", [128, 4], F32))
        scr1 = e(nc.sbuf_tensor("scr1", [1, 8], F32))
        stA = e(nc.sbuf_tensor("stA", [128, 8 * NT], F32))
        o17 = e(nc.sbuf_tensor("o17", [17, 128], F32))
        oscrow = e(nc.sbuf_tensor("oscrow", [1, WP], F32))
        s2row = e(nc.sbuf_tensor("s2row", [1, 256], F32))
        o2c = e(nc.sbuf_tensor("o2c", [128, 4], F32))
        xmsc = e(nc.sbuf_tensor("xmsc", [128, 1024], F32))
        sqsc = e(nc.sbuf_tensor("sqsc", [128, 1024], BF16))
        ntmb = e(nc.sbuf_tensor("ntmb", [128, 1024], BF16))
        oscb = e(nc.sbuf_tensor("oscb", [128, 288], F32))
        s2b = e(nc.sbuf_tensor("s2b", [128, 256], F32))
        vsufA = e(nc.sbuf_tensor("vsufA", [1, 80], F32))
        vsufB = e(nc.sbuf_tensor("vsufB", [1, 80], BF16))
        rcpz = e(nc.sbuf_tensor("rcpz", [1, 512], F32))
        rcpb = e(nc.sbuf_tensor("rcpb", [1, 1024], BF16))
        wabs = e(nc.sbuf_tensor("wabs", [128, 1024], BF16))
        halo = e(nc.sbuf_tensor("halo", [1, 1024], F32))
        hbf = e(nc.sbuf_tensor("hbf", [1, 4096], BF16))
        wrB = e(nc.sbuf_tensor("wrB", [128, 8 * 1024], BF16))
        vext2 = wrB[:, 0:32 * 65]
        Ebuf2 = wrB[:, 2080:2080 + 2 * 1536]

        AR = 132 * 1024
        MAN_TOP = 229344
        MAN_BASE = MAN_TOP - AR
        _vn = [0]

        def V(off, nelem, dt, parts=128):
            esz = {F32: 4, F32R: 4, BF16: 2, I8: 1}[dt]
            _vn[0] += 1
            h = nc.alloc_sbuf_tensor_at(f"ov{_vn[0]}", [128, nelem], dt,
                                        offset=MAN_BASE + off)
            ap = h[:, :] if parts == 128 else h[0:parts, :]
            return ap, off + ((nelem * esz + 31) // 32) * 32

        off = 0
        nfm, off = V(off, 8 * WP, I8)
        PERM_END = off
        LIM = AR

        off = PERM_END
        silub, off = V(off, 32 * SPW, F32)
        n2b, off = V(off, 32 * SPW, BF16)
        fsb, off = V(off, 8 * SPW, F32)
        m1sb, off = V(off, 4 * 1024, BF16)
        m2w, off = V(off, 2 * 4096, BF16)
        xt, off = V(off, 2 * 1024, F32)
        xtm, off = V(off, 2 * 1024, F32)
        nbr, off = V(off, 8 * 258, BF16)
        gb, off = V(off, 2 * 258, F32)
        c2b, off = V(off, 2 * 256, F32)
        c3b, off = V(off, 2 * 256, F32)
        rmx4, off = V(off, 4 * 256, F32)
        rmxf, off = V(off, 256, F32)
        n28, off = V(off, 8 * 256, I8)
        assert off <= LIM, off

        off = PERM_END
        y2hh, off = V(off, 4 * SEQ, BF16)
        ATT2 = off
        y2fm, off = V(off, 4 * TOK, BF16)
        y2own, off = V(off, 4 * TOK, BF16)
        mrg, off = V(off, 4 * TOK, BF16)
        mgs, off = V(off, 2048, BF16)
        assert off <= LIM, off
        off = ATT2
        qk2, off = V(off, 2 * SEQ, BF16)
        vext, off = V(off, 32 * 65, BF16)
        Ebuf, off = V(off, 2 * 1536, BF16)
        aosb, off = V(off, 2 * 512, BF16)
        assert off <= LIM, off

        off = PERM_END
        candA, off = V(off, 4 * 2050, BF16)
        candB, off = V(off, 4 * 2050, BF16)
        ownw, off = V(off, 4 * 2050, BF16)
        awin, off = V(off, 8 * 2050, BF16)
        wfs, off = V(off, 8 * 384, F32)
        wxt, off = V(off, 1024, F32)
        wxm, off = V(off, 1024, F32)
        assert off <= LIM, off

        off = PERM_END
        wqa, off = V(off, 4096, F32)
        wqb, off = V(off, 4096, F32)
        wq8, off = V(off, 4096, I8)
        wqm, off = V(off, 4096, BF16)
        wrstg, off = V(off, 1024, F32)
        assert off <= LIM, off

        ps = e(nc.psum_tensor("ps", [128, 4096], F32))

        def bank(i, w=512):
            return ps[:, 512 * i:512 * i + w]

        S = {}
        for name in ["sLD", "sPE", "sDV", "sAC", "sPL", "sST", "sCC",
                     "sB1", "sB2", "sB3", "sB4", "sB5",
                     "sW0", "sW1", "sX0", "sX1", "sXG", "sBC", "sT0", "sT1",
                     "sM1a", "sM1b", "sM1c", "sM1d", "sM2a", "sM2b"]:
            S[name] = Sem(e(nc.semaphore(name)))
        SW = [S["sW0"], S["sW1"]]
        SX = [S["sX0"], S["sX1"]]
        SM1 = [S["sM1a"], S["sM1b"], S["sM1c"], S["sM1d"]]
        SM2 = [S["sM2a"], S["sM2b"]]

        block = e(nc.Block())
        fillreg = {}
        bld.ev["gpsimd"].append(lambda q: fillreg.__setitem__("one", q.to_reg(1.0)))

        def barrier():
            bs = [S["sB1"], S["sB2"], S["sB3"], S["sB4"], S["sB5"]]
            for eng, s in zip(ENGS, bs):
                if eng in ("vector", "gpsimd"):
                    bld.em(eng, lambda q: q.engine_nop(), s, 1)
                else:
                    bld.em(eng, lambda q: q.nop(), s, 1)
            for eng in ENGS:
                for s in bs:
                    bld.wait(eng, s)

        def mm(out, lhsT, rhs, start, stop, sem=None):
            bld.em("tensor", lambda q, o=out, l=lhsT, r=rhs, a=start, b=stop:
                   q.matmul(o, l, r, start=a, stop=b), sem, 1)

        def tr(out, in_, ident, sem=None):
            bld.em("tensor", lambda q, o=out, i=in_, d=ident:
                   q.transpose(o, i, d), sem, 1)

        def _narrow(out):
            try:
                return out.free_size < 256
            except Exception:
                return True

        def act(out, in_, func, bias=None, scale=None, accum=None, sem=None):
            def f(q, o=out, i=in_, fn=func, b=bias, s=scale, a=accum):
                kw = {}
                if b is not None:
                    kw["bias"] = b
                if s is not None:
                    kw["scale"] = s
                if a is not None:
                    kw["accum_out"] = a
                return q.activation(out=o, in_=i, func=fn, **kw)
            bld.em("scalar", f, sem, 1, selfwait=_narrow(out) or accum is not None)

        def vop(kind, *a, sem=None, eng="vector", **kw):
            out = a[0] if a else None
            bld.em(eng, lambda q, k=kind, aa=a, kk=kw: getattr(q, k)(*aa, **kk), sem, 1,
                   selfwait=_narrow(out))

        def sdma(out, in_, eng="scalar", sem=None):
            bld.em(eng, lambda q, o=out, i=in_: q.dma_start(out=o, in_=i),
                   S["sST"] if sem is None else sem, 16)

        for out, src in [(identf[:], ideni[:, :]), (cw[:], cwi[:, :]), (cb[:], cbi[:, :]),
                         (bvb[:], bvbi[:, :]), (brrow[:], bri[:, :]), (msk[:], mski[:, :]),
                         (bqk[:], bqkvi[:, :]), (wqkf[:], wqkvi[:, :])]:
            bld.dma("sync", out, src, S["sLD"])
        bld.wait("vector", S["sLD"])
        vop("tensor_copy", identb[:], identf[:], sem=S["sDV"])
        vop("memset", onesf[:], 1.0, sem=S["sDV"])
        vop("memset", onescol[:], 1.0, sem=S["sDV"])


        vop("memset", onesb[:], 1.0, sem=S["sDV"])
        vop("memset", epsv[:], 1e-5, sem=S["sDV"])
        vop("tensor_scalar", bq8[:, 0:1], bqk[:, 0:1], 0.125, 1.0,
            op0=OP.mult, op1=OP.mult, sem=S["sDV"])
        vop("tensor_copy", bq8[:, 1:2], bqk[:, 1:2], sem=S["sDV"])
        vop("tensor_copy", wqkb[:], wqkf[:], sem=S["sDV"])
        vop("memset", wv65[:], 0.0, sem=S["sDV"])
        vop("tensor_copy", brb[:], brrow[:], sem=S["sDV"])
        bld.wait("vector", S["sDV"])
        vop("tensor_copy", wv65[:, 0:64], wqkb[:, 128:192], sem=S["sDV"])

        for kc in range(8):
            bld.wait("sync", S["sDV"])
            bld.dma("sync", wrstg[:, :], wri[128 * kc:128 * (kc + 1), :], S["sLD"])
            bld.wait("vector", S["sLD"])
            vop("tensor_copy", wrB[:, 1024 * kc:1024 * (kc + 1)], wrstg[:, :], sem=S["sDV"])

        def wmean(wt, ntiles, wd, bank_i):
            absdone = []
            for qn in ("sync",):
                bld.wait(qn, S["sAC"])
                bld.wait(qn, S["sDV"])
            for i in range(ntiles):
                buf = wqa if i % 2 == 0 else wqb
                qn = "sync"
                if i >= 2:
                    bld.wait(qn, S["sAC"], absdone[i - 2])
                bld.dma(qn, buf[:, 0:wd], wt[i, :, :], SW[i % 2])
                bld.wait("scalar", SW[i % 2])
                for g in range(0, wd, 1024):
                    bld.wait("scalar", S["sPE"])
                    act(wabs[:, 0:1024], buf[:, g:g + 1024], AF.Abs, sem=S["sAC"])
                    bld.wait("tensor", S["sAC"])
                    for g2 in range(0, 1024, 512):
                        first = (i == 0 and g == 0 and g2 == 0)
                        last = (i == ntiles - 1 and g + 1024 == wd and g2 == 512)
                        mm(bank(bank_i)[0:1, :], onesb[:, 0:1],
                           wabs[:, g2:g2 + 512], first, last,
                           sem=S["sPE"])
                absdone.append(S["sAC"].n)
        wmean(w1t, 32, 1024, 0)
        wmean(w2t, 8, 4096, 1)

        bld.wait("vector", S["sPE"])
        vop("reduce_sum", scr1[0:1, 0:1], bank(0)[0:1, :], AX.X, sem=S["sDV"])
        vop("reduce_sum", scr1[0:1, 1:2], bank(1)[0:1, :], AX.X, sem=S["sDV"])
        vop("tensor_scalar", scr1[0:1, 2:3], scr1[0:1, 0:1], 1.0 / (DH * D), 1e-5,
            op0=OP.mult, op1=OP.max, sem=S["sDV"])
        vop("tensor_scalar", scr1[0:1, 3:4], scr1[0:1, 1:2], 1.0 / (DH * D), 1e-5,
            op0=OP.mult, op1=OP.max, sem=S["sDV"])
        vop("reciprocal", scr1[0:1, 4:5], scr1[0:1, 2:3], sem=S["sDV"])
        vop("reciprocal", scr1[0:1, 5:6], scr1[0:1, 3:4], sem=S["sDV"])
        bld.wait("tensor", S["sDV"])
        for j, col in enumerate([4, 2, 5, 3]):
            mm(bank(2)[:, j:j + 1], onesf[:, :], scr1[0:1, col:col + 1], True, True,
               sem=(S["sPE"] if j == 3 else None))
        bld.wait("vector", S["sPE"])
        vop("tensor_copy", wsc[:, 0:4], bank(2)[:, 0:4], sem=S["sDV"])

        def wquant(wt, md, ntiles, wd, wscol):
            qdone = []
            for qn in ("sync",):
                bld.wait(qn, S["sAC"])
                bld.wait(qn, S["sDV"])
                bld.wait(qn, S["sPL"])
            for i in range(ntiles):
                buf = wqa if i % 2 == 0 else wqb
                qn = "sync"
                if i >= 2:
                    bld.wait(qn, S["sDV"], qdone[i - 2])
                bld.dma(qn, buf[:, 0:wd], wt[i, :, :], SW[i % 2])
                bld.wait("vector", SW[i % 2])
                vop("tensor_scalar", wq8[:, 0:wd], buf[:, 0:wd], wsc[:, wscol:wscol + 1],
                    1.0, op0=OP.mult, op1=OP.mult, sem=S["sDV"])
                qdone.append(S["sDV"].n)
                bld.wait("vector", S["sST"])
                vop("tensor_scalar", wqm[:, 0:wd], wq8[:, 0:wd], -1.0, 1.0,
                    op0=OP.max, op1=OP.min, sem=S["sDV"])
                bld.wait("scalar", S["sDV"])
                sdma(md[i, :, :], wqm[:, 0:wd])
        bld.wait("vector", S["sDV"])
        wquant(w1t, m1d, 32, 1024, 0)
        wquant(w2t, m2d, 8, 4096, 2)

        vop("memset", xmsc[:, 0:1024], 0.0, sem=S["sDV"])
        bld.wait("scalar", S["sDV"])
        r = 2050
        while r < WP:
            nr = min(126, WP - r)
            sdma(x2d[r:r + nr, :], xmsc[0:nr, 0:1024])
            r += nr
        sdma(x1d[0:1, :], xmsc[0:1, 0:1024])
        sdma(x1d[2049:2050, :], xmsc[0:1, 0:1024])
        bld.wait("scalar", S["sST"])
        barrier()

        def ln_quant_entry(xsrc):
            P = NT
            SXE = [SX[0], S["sXG"]]
            a_dv = []
            a_ac = []
            a_pl = []
            for t in range(NT):
                slot = t % 2
                qn = "sync"
                if t >= 2:
                    bld.wait(qn, S["sDV"], a_dv[t - 2])
                    bld.wait(qn, S["sAC"], a_ac[t - 2])
                bld.dma(qn, xt[:, 1024 * slot:1024 * (slot + 1)],
                        xsrc[128 * t:128 * (t + 1), :], SXE[slot])
                xtile = xt[:, 1024 * slot:1024 * (slot + 1)]
                bld.wait("vector", SXE[slot])
                vop("reduce_sum", stA[:, 0 * P + t:0 * P + t + 1], xtile, AX.X, sem=S["sDV"])
                vop("tensor_scalar", stA[:, 2 * P + t:2 * P + t + 1],
                    stA[:, 0 * P + t:0 * P + t + 1], 1.0 / D, 1.0,
                    op0=OP.mult, op1=OP.mult, sem=S["sDV"])
                bld.wait("scalar", SXE[slot])
                act(sqsc[:, 0:1024], xtile, AF.Square,
                    accum=stA[:, 1 * P + t:1 * P + t + 1], sem=S["sAC"])
                a_ac.append(S["sAC"].n)
                vop("tensor_scalar", silub[:, 1024 * slot:1024 * (slot + 1)], xtile,
                    stA[:, 2 * P + t:2 * P + t + 1], 1.0,
                    op0=OP.subtract, op1=OP.mult, sem=S["sDV"])
                vop("reduce_max", stA[:, 5 * P + t:5 * P + t + 1],
                    silub[:, 1024 * slot:1024 * (slot + 1)], AX.X,
                    apply_absolute_value=True, sem=S["sDV"])
                a_dv.append(S["sDV"].n)
                a_pl.append(0)
            bld.wait("vector", S["sAC"])
            pl = lambda i: stA[:, i * P:(i + 1) * P]
            vop("tensor_tensor", pl(3), pl(2), pl(2), OP.mult, sem=S["sDV"])
            vop("tensor_scalar", pl(4), pl(1), 1.0 / D, 1.0, op0=OP.mult, op1=OP.mult, sem=S["sDV"])
            vop("tensor_tensor", pl(3), pl(4), pl(3), OP.subtract, sem=S["sDV"])
            bld.wait("scalar", S["sDV"])
            act(xmsc[:, 0:P], pl(3), AF.Ln, bias=epsv[:, 0:1], sem=S["sAC"])
            act(pl(4), xmsc[:, 0:P], AF.Exp, scale=-0.5, sem=S["sAC"])
            bld.wait("vector", S["sAC"])
            vop("tensor_tensor", pl(6), pl(4), pl(5), OP.mult, sem=S["sDV"])
            vop("tensor_scalar", pl(5), pl(6), 1e-5, 1.0, op0=OP.max, op1=OP.mult, sem=S["sDV"])
            vop("reciprocal", pl(6), pl(5), sem=S["sDV"])
            vop("tensor_tensor", pl(6), pl(6), pl(4), OP.mult, sem=S["sDV"])
            vop("tensor_scalar", pl(6), pl(6), 127.0, 1.0, op0=OP.mult, op1=OP.mult, sem=S["sDV"])
            vop("tensor_scalar", pl(7), pl(5), wsc[:, 1:2], 1.0 / 127.0,
                op0=OP.mult, op1=OP.mult, sem=S["sDV"])
            bld.wait("tensor", S["sDV"])
            bld.wait("tensor", S["sAC"])
            tr(ps[0:NT, 3584:3584 + 128], pl(7), identf[:], sem=S["sPE"])
            bld.wait("scalar", S["sPE"])
            act(o17[:, :], ps[0:NT, 3584:3584 + 128], AF.Copy, sem=S["sAC"])
            bld.wait("gpsimd", S["sAC"])
            bld.em("gpsimd", lambda q: q.dma_start(out=rowd[:, :], in_=o17[:, :]), S["sBC"], 16)
            bld.wait("gpsimd", S["sBC"])
            bld.em("gpsimd", lambda q: q.dma_start(
                out=oscrow[0:1, :], in_=rowd[:, :].rearrange("(o p) n -> o (p n)", o=1)),
                S["sBC"], 16)
            oscrow_ready = S["sBC"].n
            bld.wait("tensor", S["sAC"])
            trd = []
            b_pl = []
            b_cast = []
            b_trl = []
            for t in range(NT):
                slot = t % 2
                qn = "sync"
                if t >= 2:
                    bld.wait(qn, S["sDV"], b_pl[t - 2])
                else:
                    bld.wait(qn, S["sDV"], a_dv[16 - t])
                    bld.wait(qn, S["sAC"], a_ac[16 - t])
                bld.dma(qn, xt[:, 1024 * slot:1024 * (slot + 1)],
                        xsrc[128 * t:128 * (t + 1), :], SXE[slot])
                xtile = xt[:, 1024 * slot:1024 * (slot + 1)]
                bld.wait("vector", SXE[slot])
                if t >= 2:
                    bld.wait("vector", S["sAC"], b_cast[t - 2])
                vop("tensor_scalar", n28[:, 1024 * slot:1024 * (slot + 1)], xtile,
                    stA[:, 2 * P + t:2 * P + t + 1], stA[:, 6 * P + t:6 * P + t + 1],
                    op0=OP.subtract, op1=OP.mult, sem=S["sDV"])
                b_pl.append(S["sDV"].n)
                bld.wait("scalar", S["sDV"], b_pl[-1])
                if t >= 2:
                    bld.wait("scalar", S["sPE"], b_trl[t - 2])
                act(n2b[:, 1024 * slot:1024 * (slot + 1)],
                    n28[:, 1024 * slot:1024 * (slot + 1)], AF.Copy, sem=S["sAC"])
                b_cast.append(S["sAC"].n)
                bld.wait("tensor", S["sAC"], b_cast[-1])
                for k in range(8):
                    pslot = (ps[:, 3072:3072 + 256] if k % 2 == 0
                             else ps[:, 3584:3584 + 256]).bitcast(BF16)[:, 0:128]
                    if len(trd) >= 2:
                        bld.wait("tensor", trd[-2][0], trd[-2][1])
                    tr(pslot, n2b[:, 1024 * slot + 128 * k:1024 * slot + 128 * (k + 1)],
                       identb[:], sem=S["sPE"])
                    dst = nfm[:, WP * k + 128 * t:WP * k + 128 * (t + 1)]
                    if k % 2 == 0:
                        bld.wait("vector", S["sPE"])
                        vop("tensor_copy", dst, pslot, sem=S["sDV"])
                        trd.append((S["sDV"], S["sDV"].n))
                    else:
                        bld.wait("scalar", S["sPE"])
                        act(dst, pslot, AF.Copy, sem=S["sAC"])
                        trd.append((S["sAC"], S["sAC"].n))
                b_trl.append(S["sPE"].n)
            bld.wait("tensor", S["sDV"])
            bld.wait("tensor", S["sAC"])
            return oscrow_ready, S["sDV"].n, S["sAC"].n

        m1done = []
        m2done = []
        FPS = {"fd": [0] * 8, "resid": 0, "wave": [0] * 8, "hlast": 0, "xtst": [0, 0]}

        def ffn_call(xsrc, resdst, res_off):
            oscrow_ready, nfm_dv, nfm_ac = ln_quant_entry(xsrc)
            for si in range(NSP):
                o = 1 + SPW * si
                hw = SPW + 2
                bld.wait("tensor", S["sBC"], oscrow_ready)
                bld.wait("tensor", S["sAC"], FPS["fd"][6])
                mm(bank(6)[:, 0:hw], onesf[:, :], oscrow[0:1, o - 1:o - 1 + hw],
                   True, True, sem=S["sPE"])
                bld.wait("scalar", S["sPE"])
                act(oscb[:, 0:hw], bank(6)[:, 0:hw], AF.Copy, sem=S["sAC"])
                oscb_ready = S["sAC"].n
                bld.wait("gpsimd", S["sPE"], FPS["hlast"])
                if si == 0:
                    bld.wait("gpsimd", S["sDV"], nfm_dv)
                    bld.wait("gpsimd", S["sAC"], nfm_ac)
                for kk in range(8):
                    vop("tensor_copy", nbr[:, 258 * kk:258 * kk + hw],
                        nfm[:, WP * kk + o - 1:WP * kk + o - 1 + hw],
                        sem=S["sPL"], eng="gpsimd")
                nbr_ready = S["sPL"].n
                h_sem = []
                evg = []
                evA = []
                evC = []
                evS = []

                def emit_BC_silu(hc):
                    slot = hc % 2
                    gs = gb[:, 258 * slot:258 * slot + hw]
                    cs = c2b[:, 256 * slot:256 * (slot + 1)]
                    c3s = c3b[:, 256 * slot:256 * (slot + 1)]
                    bld.wait("vector", S["sPL"], evA[hc])
                    vop("scalar_tensor_tensor", cs, gs[:, 0:SPW],
                        cw[:, 4 * hc:4 * hc + 1], cs,
                        op0=OP.mult, op1=OP.add, sem=S["sDV"])
                    if hc >= 2:
                        bld.wait("vector", S["sAC"], evS[hc - 2])
                    vop("scalar_tensor_tensor", c3s, gs[:, 2:2 + SPW],
                        cw[:, 4 * hc + 2:4 * hc + 3], cs,
                        op0=OP.mult, op1=OP.add, sem=S["sDV"])
                    evC.append(S["sDV"].n)
                    bld.wait("scalar", S["sDV"], evC[hc])
                    act(silub[:, SPW * hc:SPW * hc + SPW], c3s, AF.Silu,
                        bias=cb[:, hc:hc + 1], sem=S["sAC"])
                    evS.append(S["sAC"].n)

                def emit_rmx(p):
                    bld.wait("vector", S["sAC"], evS[8 * p + 7])
                    vop("tensor_reduce", rmx4[:, 256 * p:256 * (p + 1)],
                        silub[:, 2048 * p:2048 * (p + 1)].rearrange(
                            "p (hc t) -> p t hc", hc=8),
                        AX.X, OP.max, apply_absolute_value=True, sem=S["sDV"])

                for hc in range(32):
                    mslot = len(m1done) % 4
                    if len(m1done) >= 4:
                        bld.wait("sync", S["sPE"], m1done[-4])
                    bld.dma("sync", m1sb[:, 1024 * mslot:1024 * (mslot + 1)],
                            m1d[hc, :, :], SM1[mslot])
                    hps = bank(hc % 2)[:, 0:hw]
                    bld.wait("tensor", SM1[mslot])
                    if hc == 0:
                        bld.wait("tensor", S["sPL"], nbr_ready)
                        bld.wait("tensor", S["sAC"], FPS["fd"][1])
                    if hc >= 2:
                        bld.wait("tensor", S["sDV"], evg[hc - 2])
                    for k in range(8):
                        mm(hps, m1sb[:, 1024 * mslot + 128 * k:1024 * mslot + 128 * (k + 1)],
                           nbr[:, 258 * k:258 * k + hw],
                           start=(k == 0), stop=(k == 7), sem=(S["sPE"] if k == 7 else None))
                    m1done.append(S["sPE"].n)
                    h_sem.append(S["sPE"].n)
                    bld.wait("vector", S["sPE"], h_sem[hc])
                    if hc == 0:
                        bld.wait("vector", S["sAC"], oscb_ready)
                    if hc >= 2:
                        bld.wait("vector", S["sPL"], evA[hc - 2])
                    vop("tensor_tensor", gb[:, 258 * (hc % 2):258 * (hc % 2) + hw],
                        hps, oscb[:, 0:hw], OP.mult, sem=S["sDV"])
                    evg.append(S["sDV"].n)
                    bld.wait("gpsimd", S["sDV"], evg[hc])
                    if hc >= 2:
                        bld.wait("gpsimd", S["sDV"], evC[hc - 2])
                    vop("tensor_scalar", c2b[:, 256 * (hc % 2):256 * (hc % 2 + 1)],
                        gb[:, 258 * (hc % 2) + 1:258 * (hc % 2) + 1 + SPW],
                        cw[:, 4 * hc + 1:4 * hc + 2], 1.0,
                        op0=OP.mult, op1=OP.mult, sem=S["sPL"], eng="gpsimd")
                    evA.append(S["sPL"].n)
                    if hc >= 1:
                        emit_BC_silu(hc - 1)
                    if hc >= 2 and (hc - 2) % 8 == 7:
                        emit_rmx((hc - 2) // 8)
                emit_BC_silu(31)
                emit_rmx(3)
                FPS["hlast"] = h_sem[-1]
                if fpart < 2:
                    if fpart < 1.2:
                        continue
                    pass
                vop("tensor_reduce", rmxf[:, 0:SPW],
                    rmx4[:, :].rearrange("p (g t) -> p t g", g=4),
                    AX.X, OP.max, sem=S["sDV"])
                rmx_dv = S["sDV"].n
                if fpart < 1.3:
                    continue
                for u in range(2):
                    cb0 = 16 * u
                    bld.wait("tensor", S["sDV"])
                    bld.wait("tensor", S["sAC"])
                    tr(ps[0:128, 3584:3584 + 128], rmxf[:, 128 * u:128 * (u + 1)],
                       identf[:, :], sem=S["sPE"])
                    bld.wait("vector", S["sPE"])
                    vop("reduce_max", xmsc[0:128, cb0:cb0 + 1],
                        ps[0:128, 3584:3584 + 128], AX.X, sem=S["sDV"])
                    if fpart < 1.5:
                        continue
                    vop("tensor_scalar", xmsc[0:128, cb0 + 1:cb0 + 2],
                        xmsc[0:128, cb0:cb0 + 1], 1e-5, 1.0,
                        op0=OP.max, op1=OP.mult, sem=S["sDV"])
                    vop("reciprocal", xmsc[0:128, cb0 + 2:cb0 + 3],
                        xmsc[0:128, cb0 + 1:cb0 + 2], sem=S["sDV"])
                    vop("tensor_scalar", xmsc[0:128, cb0 + 3:cb0 + 4],
                        xmsc[0:128, cb0 + 2:cb0 + 3], 127.0, 1.0,
                        op0=OP.mult, op1=OP.mult, sem=S["sDV"])
                    vop("tensor_scalar", o2c[0:128, u:u + 1],
                        xmsc[0:128, cb0 + 1:cb0 + 2], wsc[0:128, 3:4], 0.5 / 127.0,
                        op0=OP.mult, op1=OP.mult, sem=S["sDV"])
                    if fpart < 1.6:
                        continue
                    bld.wait("tensor", S["sDV"])
                    tr(ps[0:1, 3840:3840 + 128],
                       xmsc[0:128, cb0 + 3:cb0 + 4], identf[:], sem=S["sPE"])
                    bld.wait("scalar", S["sPE"])
                    act(s2row[0:1, 128 * u:128 * (u + 1)],
                        ps[0:1, 3840:3840 + 128], AF.Copy, sem=S["sAC"])
                bld.wait("tensor", S["sAC"])
                mm(bank(6)[:, 0:SPW], onesf[:, :], s2row[0:1, 0:SPW], True, True,
                   sem=S["sPE"])
                bld.wait("scalar", S["sPE"])
                act(s2b[:, 0:SPW], bank(6)[:, 0:SPW], AF.Copy, sem=S["sAC"])
                s2b_ready = S["sAC"].n
                if fpart < 3:
                    continue
                qv = []
                castv = []
                ocstop = [0] * 8
                for w in range(8):
                    m2slot = len(m2done) % 2
                    if len(m2done) >= 2:
                        bld.wait("gpsimd", S["sPE"], m2done[-2])
                    for oc in range(8):
                        bld.dma("gpsimd",
                                m2w[:, 4096 * m2slot + 512 * oc:4096 * m2slot + 512 * (oc + 1)],
                                m2d[oc, :, 512 * w:512 * (w + 1)], SM2[m2slot])
                    for j in range(4):
                        hc = 4 * w + j
                        if hc == 0:
                            bld.wait("vector", S["sAC"], s2b_ready)
                        if w >= 2 and j == 0:
                            bld.wait("vector", S["sAC"], castv[w - 2])
                        vop("tensor_tensor", n28[:, 256 * (hc % 8):256 * (hc % 8) + 256],
                            silub[:, SPW * hc:SPW * (hc + 1)], s2b[:, 0:SPW], OP.mult,
                            sem=S["sDV"])
                        qv.append(S["sDV"].n)
                    bld.wait("scalar", S["sDV"], qv[-1])
                    bld.wait("scalar", S["sPE"], FPS["wave"][w])
                    act(n2b[:, 1024 * w:1024 * (w + 1)],
                        n28[:, 1024 * (w % 2):1024 * (w % 2 + 1)], AF.Copy, sem=S["sAC"])
                    castv.append(S["sAC"].n)
                    if fpart < 4:
                        m2done.append(S["sPE"].n)
                        continue
                    bld.wait("tensor", SM2[m2slot])
                    bld.wait("tensor", S["sAC"], castv[w])
                    if w == 0:
                        bld.wait("tensor", S["sDV"], FPS["resid"])
                    if w < 7:
                        for j in range(4):
                            for oc in range(8):
                                mm(bank(oc)[:, 0:SPW],
                                   m2w[:, 4096 * m2slot + 512 * oc + 128 * j:
                                       4096 * m2slot + 512 * oc + 128 * (j + 1)],
                                   n2b[:, 256 * (4 * w + j):256 * (4 * w + j + 1)],
                                   start=(w == 0 and j == 0), stop=False,
                                   sem=(S["sPE"] if (j == 3 and oc == 7) else None))
                    else:
                        for oc in range(8):
                            for j in range(4):
                                mm(bank(oc)[:, 0:SPW],
                                   m2w[:, 4096 * m2slot + 512 * oc + 128 * j:
                                       4096 * m2slot + 512 * oc + 128 * (j + 1)],
                                   n2b[:, 256 * (28 + j):256 * (29 + j)],
                                   start=False, stop=(j == 3),
                                   sem=(S["sPE"] if j == 3 else None))
                            ocstop[oc] = S["sPE"].n
                    m2done.append(S["sPE"].n)
                    FPS["wave"][w] = S["sPE"].n
                if fpart < 4:
                    continue
                for oc in range(8):
                    bld.wait("scalar", S["sPE"], ocstop[oc])
                    act(fsb[:, 256 * oc:256 * (oc + 1)], bank(oc)[:, 0:SPW],
                        AF.Copy, sem=S["sAC"])
                    FPS["fd"][oc] = S["sAC"].n
                bld.wait("tensor", S["sAC"], FPS["fd"][7])
                for u in range(2):
                    xslot = u % 2
                    bld.wait("sync", S["sDV"])
                    bld.dma("sync", xt[0:128, 1024 * xslot:1024 * (xslot + 1)],
                            xsrc[o + 128 * u:o + 128 * u + 128, :], SX[xslot])
                    if u == 1:
                        bld.wait("tensor", S["sDV"], FPS["resid"])
                    for oc in range(8):
                        tr(ps[0:128, 2048 + 128 * oc:2048 + 128 * (oc + 1)],
                           fsb[:, 256 * oc + 128 * u:256 * oc + 128 * u + 128],
                           identf[:], sem=(S["sPE"] if oc == 7 else None))
                    bld.wait("vector", S["sPE"])
                    bld.wait("vector", SX[xslot])
                    bld.wait("vector", S["sT%d" % u], FPS["xtst"][u])
                    vop("scalar_tensor_tensor", xtm[0:128, 1024 * u:1024 * (u + 1)],
                        ps[0:128, 2048:3072], o2c[0:128, u:u + 1],
                        xt[0:128, 1024 * xslot:1024 * (xslot + 1)],
                        op0=OP.mult, op1=OP.add, sem=S["sDV"])
                    FPS["resid"] = S["sDV"].n
                    bld.wait("scalar", S["sDV"], FPS["resid"])
                    sdma(resdst[o + res_off + 128 * u:o + res_off + 128 * u + 128, :],
                         xtm[0:128, 1024 * u:1024 * (u + 1)], sem=S["sT%d" % u])
                    FPS["xtst"][u] = S["sT%d" % u].n
            bld.wait("scalar", S["sST"])
            bld.wait("scalar", S["sT0"])
            bld.wait("scalar", S["sT1"])

        def finish_debug(srcap):
            bld.em("gpsimd", lambda q, s=srcap: q.dma_start(out=xout[:, :], in_=s),
                   S["sBC"], 16)
            bld.wait("gpsimd", S["sBC"])

        if upto == "pre":
            finish_debug(x2d[0:2048, :])
        if upto in ("full", "ffn1", "ln2", "attn", "wr"):
            ffn_call(xin, x1d, 0)
            barrier()
        if upto == "ffn1":
            finish_debug(x1d[1:2049, :])

        RUN_LN2 = upto in ("full", "ln2", "attn", "wr")
        if RUN_LN2:
            P = NT
            for t in range(16):
                slot = t % 2
                if t >= 2:
                    bld.wait("sync", S["sDV"])
                    bld.wait("sync", S["sAC"])
                bld.dma("sync", xt[:, 1024 * slot:1024 * (slot + 1)],
                        x1d[1 + 128 * t:1 + 128 * (t + 1), :], SX[slot])
                xtile = xt[:, 1024 * slot:1024 * (slot + 1)]
                bld.wait("vector", SX[slot])
                vop("reduce_sum", stA[:, t:t + 1], xtile, AX.X, sem=S["sDV"])
                bld.wait("scalar", SX[slot])
                act(sqsc[:, 0:1024], xtile, AF.Square, accum=stA[:, P + t:P + t + 1], sem=S["sAC"])
                vop("tensor_scalar", stA[:, 2 * P + t:2 * P + t + 1], stA[:, t:t + 1],
                    1.0 / D, 1.0, op0=OP.mult, op1=OP.mult, sem=S["sDV"])
            bld.wait("vector", S["sAC"])
            pl = lambda i: stA[:, i * P:i * P + 16]
            vop("tensor_tensor", pl(3), pl(2), pl(2), OP.mult, sem=S["sDV"])
            vop("tensor_scalar", pl(4), pl(1), 1.0 / D, 1.0, op0=OP.mult, op1=OP.mult, sem=S["sDV"])
            vop("tensor_tensor", pl(3), pl(4), pl(3), OP.subtract, sem=S["sDV"])
            bld.wait("scalar", S["sDV"])
            act(xmsc[:, 0:16], pl(3), AF.Ln, bias=epsv[:, 0:1], sem=S["sAC"])
            act(pl(4), xmsc[:, 0:16], AF.Exp, scale=-0.5, sem=S["sAC"])
            bld.wait("vector", S["sAC"])
            y2done = []
            for t in range(16):
                slot = t % 2
                bld.wait("sync", S["sDV"])
                bld.wait("sync", S["sAC"])
                bld.dma("sync", xt[:, 1024 * slot:1024 * (slot + 1)],
                        x1d[1 + 128 * t:1 + 128 * (t + 1), :], SX[slot])
                xtile = xt[:, 1024 * slot:1024 * (slot + 1)]
                bld.wait("vector", SX[slot])
                if t >= 1:
                    bld.wait("vector", S["sPE"])
                vop("tensor_scalar", ntmb[:, 0:1024], xtile, stA[:, 2 * P + t:2 * P + t + 1],
                    stA[:, 4 * P + t:4 * P + t + 1], op0=OP.subtract, op1=OP.mult, sem=S["sDV"])
                bld.wait("tensor", S["sDV"])
                for k in range(8):
                    pslot = (ps[:, 3072:3072 + 256] if k % 2 == 0
                             else ps[:, 3584:3584 + 256]).bitcast(BF16)[:, 0:128]
                    if len(y2done) >= 2:
                        bld.wait("tensor", S["sDV"], y2done[-2])
                    tr(pslot, ntmb[:, 128 * k:128 * (k + 1)], identb[:], sem=S["sPE"])
                    bld.wait("vector", S["sPE"])
                    dst = y2fm if k < 4 else y2own
                    kk = k % 4
                    vop("tensor_copy", dst[:, TOK * kk + 128 * t:TOK * kk + 128 * (t + 1)],
                        pslot, sem=S["sDV"])
                    y2done.append(S["sDV"].n)
            bld.wait("vector", S["sDV"])
            for ch in range(4):
                lo = y2fm[:, TOK * ch:TOK * (ch + 1)]
                hi = y2own[:, TOK * ch:TOK * (ch + 1)]
                vop("tensor_scalar", mrg[:, 0:TOK], hi, msk[:, 1:2], 1.0,
                    op0=OP.mult, op1=OP.mult, sem=S["sDV"])
                vop("scalar_tensor_tensor", mrg[:, 0:TOK], lo, msk[:, 0:1], mrg[:, 0:TOK],
                    op0=OP.mult, op1=OP.add, sem=S["sDV"])
                bld.wait("scalar", S["sDV"])
                sdma(y2ci[128 * ch:128 * (ch + 1), :], mrg[:, 0:TOK])
                vop("tensor_scalar", mrg[:, TOK:2 * TOK], lo, msk[:, 1:2], 1.0,
                    op0=OP.mult, op1=OP.mult, sem=S["sDV"])
                vop("scalar_tensor_tensor", mrg[:, TOK:2 * TOK], hi, msk[:, 0:1],
                    mrg[:, TOK:2 * TOK], op0=OP.mult, op1=OP.add, sem=S["sDV"])
            bld.wait("vector", S["sST"])
            for ch in range(4):
                lo = y2fm[:, TOK * ch:TOK * (ch + 1)]
                hi = y2own[:, TOK * ch:TOK * (ch + 1)]
                vop("tensor_scalar", sqsc[:, 0:1024].rearrange("p n -> p n") if False else mrg[:, 0:TOK],
                    lo, msk[:, 1:2], 1.0, op0=OP.mult, op1=OP.mult, sem=S["sDV"])
                vop("scalar_tensor_tensor", y2own[:, TOK * ch:TOK * (ch + 1)], hi, msk[:, 0:1],
                    mrg[:, 0:TOK], op0=OP.mult, op1=OP.add, sem=S["sDV"])
            bld.wait("gpsimd", S["sST"])
            bld.wait("gpsimd", S["sDV"])
            bld.em("gpsimd", lambda q: q.collective_compute(
                "AllGather", OP.bypass, replica_groups=[[0, 1], [2, 3], [4, 5], [6, 7]],
                ins=[y2ci[:, :]], outs=[y2go[:, :]]), S["sCC"], 1)
            bld.wait("sync", S["sCC"])
            for half in range(2):
                row0 = 0 if half == 0 else 512
                for ch in range(4):
                    bld.wait("sync", S["sDV"])
                    bld.dma("sync", mrg[:, 0:TOK],
                            y2go[row0 + 128 * ch:row0 + 128 * (ch + 1), :], SX[0])
                    own = y2own[:, TOK * ch:TOK * (ch + 1)]
                    dst = y2hh[:, SEQ * ch + TOK * half:SEQ * ch + TOK * (half + 1)]
                    bld.wait("vector", SX[0])
                    if half == 0:
                        vop("tensor_scalar", mgs[:, 0:TOK], own, msk[:, 1:2], 1.0,
                            op0=OP.mult, op1=OP.mult, sem=S["sDV"])
                        vop("scalar_tensor_tensor", dst, mrg[:, 0:TOK], msk[:, 0:1],
                            mgs[:, 0:TOK], op0=OP.mult, op1=OP.add, sem=S["sDV"])
                    else:
                        vop("tensor_scalar", mgs[:, 0:TOK], mrg[:, 0:TOK], msk[:, 1:2], 1.0,
                            op0=OP.mult, op1=OP.mult, sem=S["sDV"])
                        vop("scalar_tensor_tensor", dst, own, msk[:, 0:1],
                            mgs[:, 0:TOK], op0=OP.mult, op1=OP.add, sem=S["sDV"])
            barrier()

        if upto == "ln2":
            bld.em("gpsimd", lambda q: q.dma_start(
                out=xout[0:1024, :].bitcast(BF16), in_=y2go[:, :]), S["sBC"], 16)
            bld.wait("gpsimd", S["sBC"])
        if upto in ("full", "attn", "wr"):
            pav = ps[0:65, 3072:3584]
            pzb = ps[0:64, 3584:3584 + 512]
            ptmp = ps[0:1, 3584:3584 + 65]
            rot_expdone = [0, 0]
            fills_done = [0, 0]
            qkdone = [0, 0]
            av_guard = 0
            for k in range(nheads):
                ch4 = k // 2
                pbase = 64 * (k % 2)

                def yh(c0, w_, ch4=ch4, pbase=pbase):
                    return y2hh[pbase:pbase + 64, SEQ * ch4 + c0:SEQ * ch4 + c0 + w_]

                for which, (wsl0, wsl1, bcol, scl) in enumerate(
                        [(0, 64, 0, 0.125), (64, 128, 1, None)]):
                    for cq in range(8):
                        s = (which * 8 + cq) % 2
                        pqk = ps[0:64, 1536 * s:1536 * s + 512]
                        bld.wait("tensor", S["sAC"], qkdone[s])
                        if which == 0 and cq == 0:
                            bld.wait("tensor", S["sDV"], av_guard)
                            bld.wait("tensor", S["sAC"], max(rot_expdone))
                        mm(pqk, wqkb[pbase:pbase + 64, wsl0:wsl1], yh(512 * cq, 512),
                           True, True, sem=S["sPE"])
                        bld.wait("scalar", S["sPE"])
                        act(qk2[0:64, SEQ * which + 512 * cq:SEQ * which + 512 * (cq + 1)], pqk,
                            AF.Identity, bias=bq8[pbase:pbase + 64, bcol:bcol + 1], scale=scl,
                            sem=S["sAC"])
                        qkdone[s] = S["sAC"].n
                vext_u = vext
                Ebuf_u = Ebuf
                vdone = [0, 0]
                for kb in range(32):
                    s = kb % 2
                    pv = ps[:, 512 + 512 * s:512 + 512 * s + 65]
                    bld.wait("tensor", S["sDV"], vdone[s])
                    mm(pv, yh(128 * kb, 128), wv65[pbase:pbase + 64, :], True, True,
                       sem=S["sPE"])
                    bld.wait("vector", S["sPE"])
                    vop("tensor_tensor", vext_u[:, 65 * kb:65 * (kb + 1)], pv, bvb[:, :],
                        OP.add, sem=S["sDV"])
                    vdone[s] = S["sDV"].n
                bld.wait("tensor", S["sDV"])
                bld.wait("tensor", S["sAC"])
                chunk_range = [] if attn_level == 0 else list(range(7, -1, -1))
                for ci, c in enumerate(chunk_range):
                    nblk = 4 * c + 4
                    glist = [(g0, min(3, nblk - g0)) for g0 in range(0, nblk, 3)]
                    first_av = True
                    for gi, (g0, gn) in enumerate(glist):
                        r = gi % 2
                        rot = r * 1536
                        bld.wait("tensor", S["sAC"], rot_expdone[r])
                        for j in range(gn):
                            b = g0 + j
                            mm(ps[0:128, rot + 512 * j:rot + 512 * j + 512],
                               qk2[0:64, SEQ + 128 * b:SEQ + 128 * (b + 1)],
                               qk2[0:64, 512 * c:512 * (c + 1)], True, True,
                               sem=(S["sPE"] if j == gn - 1 else None))
                        bld.wait("scalar", S["sPE"])
                        bld.wait("scalar", S["sPL"], fills_done[r])
                        act(Ebuf_u[:, 1536 * r:1536 * r + 512 * gn],
                            ps[:, rot:rot + 512 * gn], AF.Exp, sem=S["sAC"])
                        rot_expdone[r] = S["sAC"].n
                        nfills = 0
                        for j in range(gn):
                            b = g0 + j
                            if b >= 4 * c:
                                if nfills == 0:
                                    bld.wait("gpsimd", S["sAC"], rot_expdone[r])
                                eap = Ebuf_u[:, 1536 * r + 512 * j:1536 * r + 512 * (j + 1)]
                                bld.em("gpsimd", lambda q, eap=eap, bb=b, cc=c:
                                       q.affine_select(eap, eap, [[1, 512]], OP.is_ge,
                                                       fillreg["one"],
                                                       base=512 * cc - 128 * bb - 1,
                                                       channel_multiplier=-1),
                                       S["sPL"], 1)
                                nfills += 1
                        if nfills:
                            fills_done[r] = S["sPL"].n
                            bld.wait("tensor", S["sPL"], fills_done[r])
                        if attn_level < 2:
                            continue
                        if first_av:
                            bld.wait("tensor", S["sDV"], av_guard)
                            first_av = False
                        for j in range(gn):
                            b = g0 + j
                            stop = (c == 7 and b == nblk - 1)
                            mm(pav, vext_u[:, 65 * b:65 * (b + 1)],
                               Ebuf_u[:, 1536 * r + 512 * j:1536 * r + 512 * (j + 1)],
                               start=(b == 0), stop=stop,
                               sem=(S["sPE"] if stop else None))
                    if attn_level < 2:
                        av_guard = S["sDV"].n
                        continue
                    if c < 7:
                        mm(pav, vsufB[0:1, 0:65], onesb[0:1, 0:512], start=False, stop=True,
                           sem=S["sPE"])
                    if attn_level < 3:
                        bld.wait("vector", S["sPE"])
                        vop("tensor_copy", aosb[0:64, 0:512], ps[0:64, 3072:3584], sem=S["sDV"])
                        av_guard = S["sDV"].n
                        continue
                    bld.wait("vector", S["sPE"])
                    vop("reciprocal", rcpz[0:1, 0:512], ps[64:65, 3072:3584], sem=S["sDV"])
                    vop("tensor_copy", rcpb[0:1, 0:512], rcpz[0:1, 0:512], sem=S["sDV"])
                    vop("scalar_tensor_tensor", rcpb[0:1, 512:1024], rcpb[0:1, 0:512], -1.0,
                        rcpz[0:1, 0:512], op0=OP.mult, op1=OP.add, sem=S["sDV"])
                    bld.wait("tensor", S["sDV"])
                    mm(pzb, onesb[0:1, 0:64], rcpb[0:1, 0:512], True, False)
                    mm(pzb, onesb[0:1, 0:64], rcpb[0:1, 512:1024], False, True, sem=S["sPE"])
                    bld.wait("vector", S["sPE"])
                    aslot = aosb[0:64, 512 * (ci % 2):512 * (ci % 2) + 512]
                    bld.wait("vector", S["sST"])
                    vop("tensor_copy", aslot, ps[0:64, 3072:3584], sem=S["sDV"])
                    bld.wait("vector", S["sDV"])
                    vop("tensor_tensor", aslot, aslot, pzb, OP.mult, sem=S["sDV"])
                    zb_read = S["sDV"].n
                    bld.wait("scalar", S["sDV"])
                    sdma(aod[64 * k:64 * (k + 1), 512 * c:512 * (c + 1)], aslot)
                    if c > 0:
                        bld.wait("tensor", S["sDV"], zb_read)
                        for j in range(4):
                            b = 4 * c + j
                            mm(ptmp, onesb[:, 0:1], vext_u[:, 65 * b:65 * (b + 1)],
                               start=(j == 0), stop=(j == 3), sem=(S["sPE"] if j == 3 else None))
                        bld.wait("vector", S["sPE"])
                        if c == 7:
                            vop("tensor_copy", vsufA[0:1, 0:65], ptmp, sem=S["sDV"])
                        else:
                            vop("tensor_tensor", vsufA[0:1, 0:65], vsufA[0:1, 0:65], ptmp,
                                OP.add, sem=S["sDV"])
                        vop("tensor_copy", vsufB[0:1, 0:65], vsufA[0:1, 0:65], sem=S["sDV"])
                    av_guard = S["sDV"].n
            bld.wait("tensor", S["sDV"], av_guard)
            bld.wait("scalar", S["sST"])
            barrier()

        if upto == "attn":
            bld.em("gpsimd", lambda q: q.dma_start(
                out=xout[0:1024, :].bitcast(BF16),
                in_=aod[:, :].rearrange("p (a n) -> (p a) n", a=2)), S["sBC"], 16)
            bld.wait("gpsimd", S["sBC"])
        if upto in ("full", "wr"):
            cA = lambda ch: candA[:, 2050 * ch:2050 * (ch + 1)]
            cB = lambda ch: candB[:, 2050 * ch:2050 * (ch + 1)]
            ow = lambda ch: ownw[:, 2050 * ch:2050 * (ch + 1)]
            for ch in range(4):
                vop("memset", cA(ch)[:, 0:1], 0.0, sem=S["sDV"])
                vop("memset", cB(ch)[:, 2049:2050], 0.0, sem=S["sDV"])
            bld.wait("sync", S["sDV"])
            for ch in range(4):
                bld.dma("sync", cA(ch)[:, 1:2050], aod[128 * ch:128 * (ch + 1), 0:2049], SX[0])
                bld.dma("sync", cB(ch)[:, 0:2049], aod[128 * ch:128 * (ch + 1), 2047:4096], SX[0])
            bld.wait("vector", SX[0])
            for ch in range(4):
                vop("tensor_scalar", ow(ch), cB(ch), msk[:, 1:2], 1.0,
                    op0=OP.mult, op1=OP.mult, sem=S["sDV"])
                vop("scalar_tensor_tensor", ow(ch), cA(ch), msk[:, 0:1], ow(ch),
                    op0=OP.mult, op1=OP.add, sem=S["sDV"])
                bld.wait("scalar", S["sDV"])
                sdma(aoci[128 * ch:128 * (ch + 1), :], ow(ch))
            bld.dma("sync", hbf[0:1, 0:2048].bitcast(F32), x1d[1:2, :], SX[0])
            bld.dma("sync", hbf[0:1, 2048:4096].bitcast(F32), x1d[2048:2049, :], SX[0])
            bld.wait("scalar", SX[0])
            sdma(aoci[512:513, 0:2048], hbf[0:1, 0:2048])
            sdma(aoci[513:514, 0:2048], hbf[0:1, 2048:4096])
            bld.wait("gpsimd", S["sST"])
            bld.em("gpsimd", lambda q: q.collective_compute(
                "AllGather", OP.bypass, replica_groups=[[0, 1], [2, 3], [4, 5], [6, 7]],
                ins=[aoci[:, :]], outs=[aogo[:, :]]), S["sCC"], 1)
            bld.wait("vector", S["sST"])
            for ch in range(4):
                vop("tensor_scalar", ow(ch), cA(ch), msk[:, 1:2], 1.0,
                    op0=OP.mult, op1=OP.mult, sem=S["sDV"])
                vop("scalar_tensor_tensor", ow(ch), cB(ch), msk[:, 0:1], ow(ch),
                    op0=OP.mult, op1=OP.add, sem=S["sDV"])
            bld.wait("sync", S["sCC"])
            bld.wait("sync", S["sDV"])
            for ch in range(4):
                bld.dma("sync", cA(ch), aogo[128 * ch:128 * (ch + 1), 0:2050], SX[0])
                bld.dma("sync", cB(ch), aogo[514 + 128 * ch:514 + 128 * (ch + 1), 0:2050], SX[0])
            bld.wait("vector", SX[0])
            for ch in range(4):
                vop("tensor_scalar", cB(ch), cB(ch), msk[:, 1:2], 1.0,
                    op0=OP.mult, op1=OP.mult, sem=S["sDV"])
                vop("scalar_tensor_tensor", cB(ch), cA(ch), msk[:, 0:1], cB(ch),
                    op0=OP.mult, op1=OP.add, sem=S["sDV"])
            for ch in range(4):
                vop("tensor_scalar", cA(ch), ow(ch), msk[:, 1:2], 1.0,
                    op0=OP.mult, op1=OP.mult, sem=S["sDV"])
                vop("scalar_tensor_tensor", awin[:, 2050 * ch:2050 * (ch + 1)], cB(ch),
                    msk[:, 0:1], cA(ch), op0=OP.mult, op1=OP.add, sem=S["sDV"])
            for ch in range(4):
                vop("tensor_scalar", cA(ch), cB(ch), msk[:, 1:2], 1.0,
                    op0=OP.mult, op1=OP.mult, sem=S["sDV"])
                vop("scalar_tensor_tensor", awin[:, 2050 * (ch + 4):2050 * (ch + 5)],
                    ow(ch), msk[:, 0:1], cA(ch), op0=OP.mult, op1=OP.add, sem=S["sDV"])
            bld.wait("sync", S["sDV"])
            bld.dma("sync", hbf[0:1, 0:2048], aogo[1026:1027, 0:2048], SX[0])
            bld.dma("sync", hbf[0:1, 2048:4096], aogo[513:514, 0:2048], SX[0])
            bld.wait("vector", SX[0])
            vop("tensor_scalar", halo[0:1, 0:1024], hbf[0:1, 0:2048].bitcast(F32),
                msk[0:1, 1:2], 1.0, op0=OP.mult, op1=OP.mult, sem=S["sDV"])
            vop("scalar_tensor_tensor", halo[0:1, 0:1024], hbf[0:1, 2048:4096].bitcast(F32),
                msk[0:1, 0:1], halo[0:1, 0:1024], op0=OP.mult, op1=OP.add, sem=S["sDV"])
            wrdone = []
            for si, (o, width) in enumerate(WR_SPANS):
                nu = (width + 127) // 128
                for oc in range(8):
                    fps = bank(2 + oc % 2)[:, 0:width]
                    if len(wrdone) >= 2:
                        bld.wait("tensor", S["sAC"], wrdone[-2])
                    if si == 0 and oc == 0:
                        bld.wait("tensor", S["sDV"])
                    for kc in range(8):
                        mm(fps, wrB[:, 1024 * kc + 128 * oc:1024 * kc + 128 * (oc + 1)],
                           awin[:, 2050 * kc + o:2050 * kc + o + width],
                           start=(kc == 0), stop=False)
                    mm(fps, brb[0:1, 128 * oc:128 * (oc + 1)], onesb[0:1, 0:width],
                       start=False, stop=True, sem=S["sPE"])
                    bld.wait("scalar", S["sPE"])
                    act(wfs[:, 384 * oc:384 * oc + width], fps, AF.Copy, sem=S["sAC"])
                    wrdone.append(S["sAC"].n)
                bld.wait("tensor", S["sAC"])
                for u in range(nu):
                    blk = min(128, width - 128 * u)
                    w0 = o + 128 * u
                    bld.wait("sync", S["sDV"])
                    bld.dma("sync", wxt[0:blk, 0:1024], x1d[w0:w0 + blk, :], SX[0])
                    pT = ps[:, 2048:3072]
                    bld.wait("tensor", S["sDV"])
                    for oc in range(8):
                        tr(pT[0:blk, 128 * oc:128 * (oc + 1)],
                           wfs[:, 384 * oc + 128 * u:384 * oc + 128 * u + blk],
                           identf[:], sem=(S["sPE"] if oc == 7 else None))
                    bld.wait("vector", S["sPE"])
                    bld.wait("vector", SX[0])
                    bld.wait("vector", S["sST"])
                    vop("tensor_tensor", wxm[0:blk, 0:1024], pT[0:blk, 0:1024],
                        wxt[0:blk, 0:1024], OP.add, sem=S["sDV"])
                    bld.wait("scalar", S["sDV"])
                    sdma(x2d[w0:w0 + blk, :], wxm[0:blk, 0:1024])
            for wrow, mcol in [(0, 0), (2049, 1)]:
                bld.wait("sync", S["sST"])
                bld.dma("sync", wxm[0:1, 0:1024], x2d[wrow:wrow + 1, :], SX[0])
                bld.wait("vector", SX[0])
                bld.wait("vector", S["sST"])
                vop("tensor_tensor", wxm[0:1, 0:1024], wxm[0:1, 0:1024], halo[0:1, :],
                    OP.add, sem=S["sDV"])
                vop("tensor_scalar", wxm[0:1, 0:1024], wxm[0:1, 0:1024], msk[0:1, mcol:mcol + 1],
                    1.0, op0=OP.mult, op1=OP.mult, sem=S["sDV"])
                bld.wait("scalar", S["sDV"])
                sdma(x2d[wrow:wrow + 1, :], wxm[0:1, 0:1024])
            bld.wait("scalar", S["sST"])
            barrier()

        if upto == "wr":
            finish_debug(x2d[0:2048, :])
        if upto == "full":
            ffn_call(x2d, xout, -1)
        bld.wait("gpsimd", S["sST"])

        @block.sync
        def _(q):
            for f in bld.ev["sync"]:
                f(q)

        @block.tensor
        def _(q):
            for f in bld.ev["tensor"]:
                f(q)

        @block.vector
        def _(q):
            for f in bld.ev["vector"]:
                f(q)

        @block.scalar
        def _(q):
            for f in bld.ev["scalar"]:
                f(q)

        @block.gpsimd
        def _(q):
            for f in bld.ev["gpsimd"]:
                f(q)

    return nc


_NC_CACHE = None


def _get_nc():
    global _NC_CACHE
    if _NC_CACHE is None:
        _NC_CACHE = build_program()
    return _NC_CACHE


def kernel(L0_attn_x, L0_attn_Wqkv, L0_attn_bqkv, L0_attn_Wr, L0_attn_br,
           L0_ffn_w1, L0_ffn_conv_w, L0_ffn_conv_b, L0_ffn_w2):
    x = np.asarray(L0_attn_x, dtype=np.float32)
    w1 = np.asarray(L0_ffn_w1, dtype=np.float32)
    w2 = np.asarray(L0_ffn_w2, dtype=np.float32)
    conv_w = np.asarray(L0_ffn_conv_w, dtype=np.float32)
    conv_b = np.asarray(L0_ffn_conv_b, dtype=np.float32)
    wqkv = np.asarray(L0_attn_Wqkv, dtype=np.float32)
    bqkv = np.asarray(L0_attn_bqkv, dtype=np.float32)
    wr = np.asarray(L0_attn_Wr, dtype=np.float32)
    br = np.asarray(L0_attn_br, dtype=np.float32)

    w1tiles = np.ascontiguousarray(
        w1.reshape(32, 128, 8, 128).transpose(0, 3, 2, 1).reshape(32, 128, 1024))
    w2tiles = np.ascontiguousarray(
        w2.reshape(8, 128, 32, 128).transpose(0, 3, 2, 1).reshape(8, 128, 4096))
    cwh = np.zeros((128, 128), np.float32)
    cwh.reshape(128, 32, 4)[:, :, 0:3] = conv_w[:, 0, :].reshape(32, 128, 3).transpose(1, 0, 2)
    cbh = np.ascontiguousarray(conv_b.reshape(32, 128).T)
    bqkvh = np.ascontiguousarray(np.tile(bqkv.reshape(3, 64).T, (2, 1)))
    wqkvh = np.ascontiguousarray(np.tile(wqkv, (2, 1)))
    bvbh = np.ones((128, 65), np.float32)
    bvbh[:, 0:64] = bqkv[128:192][None, :]
    brh = np.ascontiguousarray(br.reshape(1, 1024))
    iden = np.eye(128, dtype=np.float32)

    in_maps = []
    for c in range(8):
        b, p = c // 2, c % 2
        xw = np.zeros((WP, D), np.float32)
        xw[1:2049] = x[b, 2048 * p:2048 * p + 2048]
        if p == 0:
            xw[2049] = x[b, 2048]
        else:
            xw[0] = x[b, 2047]
        mskh = np.zeros((128, 4), np.float32)
        mskh[:, 0] = float(p)
        mskh[:, 1] = 1.0 - float(p)
        in_maps.append({
            "xin": xw, "w1t": w1tiles, "w2t": w2tiles, "cwi": cwh, "cbi": cbh,
            "wqkvi": wqkvh, "bqkvi": bqkvh, "bvbi": bvbh, "wri": wr, "bri": brh,
            "ideni": iden, "mski": mskh,
        })

    nc = _get_nc()
    res = run_bass_kernel_spmd(nc, in_maps, core_ids=list(range(8)))
    out = np.empty((4, 4096, 1024), np.float32)
    for c in range(8):
        b, p = c // 2, c % 2
        out[b, 2048 * p:2048 * p + 2048] = res.results[c]["xout"]
    return out


# version 09
# speedup vs baseline: 1.2286x; 1.0329x over previous
import numpy as np
import ml_dtypes
from contextlib import ExitStack

import concourse.bass as bass
import concourse.mybir as mybir
from concourse.alu_op_type import AluOpType as OP
from concourse.bass_utils import run_bass_kernel_spmd

F32 = mybir.dt.float32
F32R = mybir.dt.float32r
BF16 = mybir.dt.bfloat16
I8 = mybir.dt.int8
AF = mybir.ActivationFunctionType
AX = mybir.AxisListType

D = 1024
DH = 4096
SEQ = 4096
TOK = 2048
W = 2050
WP = 2176
NT = 17
SPW = 256
NSP = 8
WR_SPANS = [(0, 384), (384, 384), (768, 384), (1152, 384), (1536, 384), (1920, 130)]
ENGS = ("sync", "tensor", "vector", "scalar", "gpsimd")


class Sem:
    def __init__(self, h):
        self.h = h
        self.n = 0


class B:
    def __init__(self):
        self.ev = {e: [] for e in ENGS}

    def em(self, eng, fn, sem=None, inc=1, selfwait=False):
        if sem is not None:
            sem.n += inc
            def wrapped(q, fn=fn, s=sem.h, k=inc):
                fn(q).then_inc(s, k)
            self.ev[eng].append(wrapped)
            if selfwait:
                v = sem.n
                self.ev[eng].append(lambda q, s=sem.h, v=v: q.wait_ge(s, v))
        else:
            self.ev[eng].append(lambda q, fn=fn: fn(q))

    def wait(self, eng, sem, val=None):
        v = sem.n if val is None else val
        if v > 0:
            self.ev[eng].append(lambda q, s=sem.h, v=v: q.wait_ge(s, v))

    def dma(self, eng, out, in_, sem):
        self.em(eng, lambda q, o=out, i=in_: q.dma_start(out=o, in_=i), sem, 16)


def build_program(upto="full", nheads=8, attn_level=3, fpart=4):
    nc = bass.Bass("TRN2", target_bir_lowering=False)

    xin = nc.dram_tensor("xin", [WP, D], F32, kind="ExternalInput")
    w1t = nc.dram_tensor("w1t", [32, 128, 1024], F32, kind="ExternalInput")
    w2t = nc.dram_tensor("w2t", [8, 128, 4096], F32, kind="ExternalInput")
    cwi = nc.dram_tensor("cwi", [128, 128], F32, kind="ExternalInput")
    cbi = nc.dram_tensor("cbi", [128, 32], F32, kind="ExternalInput")
    wqkvi = nc.dram_tensor("wqkvi", [128, 192], F32, kind="ExternalInput")
    bqkvi = nc.dram_tensor("bqkvi", [128, 3], F32, kind="ExternalInput")
    bvbi = nc.dram_tensor("bvbi", [128, 65], F32, kind="ExternalInput")
    wri = nc.dram_tensor("wri", [1024, 1024], F32, kind="ExternalInput")
    bri = nc.dram_tensor("bri", [1, 1024], F32, kind="ExternalInput")
    ideni = nc.dram_tensor("ideni", [128, 128], F32, kind="ExternalInput")
    mski = nc.dram_tensor("mski", [128, 4], F32, kind="ExternalInput")

    xout = nc.dram_tensor("xout", [TOK, D], F32, kind="ExternalOutput")

    m1d = nc.dram_tensor("m1d", [32, 128, 1024], BF16)
    m2d = nc.dram_tensor("m2d", [8, 128, 4096], BF16)
    x1d = nc.dram_tensor("x1d", [WP, D], F32)
    x2d = nc.dram_tensor("x2d", [WP, D], F32)
    aod = nc.dram_tensor("aod", [512, 4096], BF16)
    rowd = nc.dram_tensor("rowd", [17, 128], F32)
    y2ci = nc.dram_tensor("y2ci", [512, TOK], BF16)
    y2go = nc.dram_tensor("y2go", [1024, TOK], BF16)
    aoci = nc.dram_tensor("aoci", [514, 2050], BF16)
    aogo = nc.dram_tensor("aogo", [1028, 2050], BF16)

    bld = B()

    with ExitStack() as ctx:
        e = ctx.enter_context

        identf = e(nc.sbuf_tensor("identf", [128, 128], F32))
        identb = e(nc.sbuf_tensor("identb", [128, 128], BF16))
        cw = e(nc.sbuf_tensor("cw", [128, 128], F32))
        cb = e(nc.sbuf_tensor("cb", [128, 32], F32))
        bqk = e(nc.sbuf_tensor("bqk", [128, 3], F32))
        bq8 = e(nc.sbuf_tensor("bq8", [128, 2], F32))
        wqkf = e(nc.sbuf_tensor("wqkf", [128, 192], F32))
        wqkb = e(nc.sbuf_tensor("wqkb", [128, 192], BF16))
        wv65 = e(nc.sbuf_tensor("wv65", [128, 65], BF16))
        bvb = e(nc.sbuf_tensor("bvb", [128, 65], F32))
        brrow = e(nc.sbuf_tensor("brrow", [1, 1024], F32))
        brb = e(nc.sbuf_tensor("brb", [1, 1024], BF16))
        msk = e(nc.sbuf_tensor("msk", [128, 4], F32))
        onesf = e(nc.sbuf_tensor("onesf", [1, 128], F32))
        onescol = e(nc.sbuf_tensor("onescol", [128, 1], F32))
        onesb = e(nc.sbuf_tensor("onesb", [128, 512], BF16))
        epsv = e(nc.sbuf_tensor("epsv", [128, 1], F32))
        wsc = e(nc.sbuf_tensor("wsc", [128, 4], F32))
        scr1 = e(nc.sbuf_tensor("scr1", [1, 8], F32))
        stA = e(nc.sbuf_tensor("stA", [128, 8 * NT], F32))
        o17 = e(nc.sbuf_tensor("o17", [17, 128], F32))
        oscrow = e(nc.sbuf_tensor("oscrow", [1, WP], F32))
        s2row = e(nc.sbuf_tensor("s2row", [1, 256], F32))
        o2c = e(nc.sbuf_tensor("o2c", [128, 4], F32))
        xmsc = e(nc.sbuf_tensor("xmsc", [128, 1024], F32))
        sqsc = e(nc.sbuf_tensor("sqsc", [128, 1024], BF16))
        ntmb = e(nc.sbuf_tensor("ntmb", [128, 1024], BF16))
        oscb = e(nc.sbuf_tensor("oscb", [128, 288], F32))
        s2b = e(nc.sbuf_tensor("s2b", [128, 256], F32))
        vsufA = e(nc.sbuf_tensor("vsufA", [1, 80], F32))
        vsufB = e(nc.sbuf_tensor("vsufB", [1, 80], BF16))
        rcpz = e(nc.sbuf_tensor("rcpz", [1, 512], F32))
        rcpb = e(nc.sbuf_tensor("rcpb", [1, 1024], BF16))
        wabs = e(nc.sbuf_tensor("wabs", [128, 1024], BF16))
        halo = e(nc.sbuf_tensor("halo", [1, 1024], F32))
        hbf = e(nc.sbuf_tensor("hbf", [1, 4096], BF16))
        wrB = e(nc.sbuf_tensor("wrB", [128, 8 * 1024], BF16))
        vext2 = wrB[:, 0:32 * 65]
        Ebuf2 = wrB[:, 2080:2080 + 2 * 1536]

        AR = 136 * 1024 + 3104
        MAN_TOP = 229344
        MAN_BASE = MAN_TOP - AR
        _vn = [0]

        def V(off, nelem, dt, parts=128):
            esz = {F32: 4, F32R: 4, BF16: 2, I8: 1}[dt]
            _vn[0] += 1
            h = nc.alloc_sbuf_tensor_at(f"ov{_vn[0]}", [128, nelem], dt,
                                        offset=MAN_BASE + off)
            ap = h[:, :] if parts == 128 else h[0:parts, :]
            return ap, off + ((nelem * esz + 31) // 32) * 32

        off = 0
        nfm, off = V(off, 8 * WP, I8)
        PERM_END = off
        LIM = AR

        off = PERM_END
        silub, off = V(off, 32 * SPW, F32)
        n2b, off = V(off, 32 * SPW, BF16)
        fsb, off = V(off, 8 * SPW, F32)
        m1sb, off = V(off, 4 * 1024, BF16)
        m2w, off = V(off, 3 * 4096, BF16)
        xt, off = V(off, 2 * 1024, F32)
        xtm, off = V(off, 2 * 1024, F32)
        nbr, off = V(off, 8 * 258, BF16)
        gb, off = V(off, 2 * 258, F32)
        c2b, off = V(off, 2 * 256, F32)
        c3b, off = V(off, 2 * 256, F32)
        rmx4, off = V(off, 4 * 256, F32)
        rmxf, off = V(off, 256, F32)
        n28, off = V(off, 8 * 256, I8)
        assert off <= LIM, off

        off = PERM_END
        y2hh, off = V(off, 4 * SEQ, BF16)
        ATT2 = off
        y2fm, off = V(off, 4 * TOK, BF16)
        y2own, off = V(off, 4 * TOK, BF16)
        mrg, off = V(off, 4 * TOK, BF16)
        mgs, off = V(off, 2048, BF16)
        assert off <= LIM, off
        off = ATT2
        qk2, off = V(off, 2 * SEQ, BF16)
        vext, off = V(off, 32 * 65, BF16)
        Ebuf, off = V(off, 2 * 1536, BF16)
        aosb, off = V(off, 2 * 512, BF16)
        assert off <= LIM, off

        off = PERM_END
        candA, off = V(off, 4 * 2050, BF16)
        candB, off = V(off, 4 * 2050, BF16)
        ownw, off = V(off, 4 * 2050, BF16)
        awin, off = V(off, 8 * 2050, BF16)
        wfs, off = V(off, 8 * 384, F32)
        wxt, off = V(off, 1024, F32)
        wxm, off = V(off, 1024, F32)
        assert off <= LIM, off

        off = PERM_END
        wqa, off = V(off, 4096, F32)
        wqb, off = V(off, 4096, F32)
        wq8, off = V(off, 4096, I8)
        wqm, off = V(off, 4096, BF16)
        wrstg, off = V(off, 1024, F32)
        assert off <= LIM, off

        ps = e(nc.psum_tensor("ps", [128, 4096], F32))

        def bank(i, w=512):
            return ps[:, 512 * i:512 * i + w]

        S = {}
        for name in ["sLD", "sPE", "sDV", "sAC", "sPL", "sST", "sCC",
                     "sB1", "sB2", "sB3", "sB4", "sB5",
                     "sW0", "sW1", "sX0", "sX1", "sXG", "sBC", "sT0", "sT1",
                     "sM1a", "sM1b", "sM1c", "sM1d", "sM1e", "sM1f", "sM1g", "sM1h",
                     "sM2a", "sM2b", "sM2c"]:
            S[name] = Sem(e(nc.semaphore(name)))
        SW = [S["sW0"], S["sW1"]]
        SX = [S["sX0"], S["sX1"]]
        SM1 = [S["sM1a"], S["sM1b"], S["sM1c"], S["sM1d"],
               S["sM1e"], S["sM1f"], S["sM1g"], S["sM1h"]]
        SM2 = [S["sM2a"], S["sM2b"], S["sM2c"]]

        block = e(nc.Block())
        fillreg = {}
        bld.ev["gpsimd"].append(lambda q: fillreg.__setitem__("one", q.to_reg(1.0)))

        def barrier():
            bs = [S["sB1"], S["sB2"], S["sB3"], S["sB4"], S["sB5"]]
            for eng, s in zip(ENGS, bs):
                if eng in ("vector", "gpsimd"):
                    bld.em(eng, lambda q: q.engine_nop(), s, 1)
                else:
                    bld.em(eng, lambda q: q.nop(), s, 1)
            for eng in ENGS:
                for s in bs:
                    bld.wait(eng, s)

        def mm(out, lhsT, rhs, start, stop, sem=None):
            bld.em("tensor", lambda q, o=out, l=lhsT, r=rhs, a=start, b=stop:
                   q.matmul(o, l, r, start=a, stop=b), sem, 1)

        def tr(out, in_, ident, sem=None):
            bld.em("tensor", lambda q, o=out, i=in_, d=ident:
                   q.transpose(o, i, d), sem, 1)

        def _narrow(out):
            try:
                return out.free_size < 256
            except Exception:
                return True

        def act(out, in_, func, bias=None, scale=None, accum=None, sem=None):
            def f(q, o=out, i=in_, fn=func, b=bias, s=scale, a=accum):
                kw = {}
                if b is not None:
                    kw["bias"] = b
                if s is not None:
                    kw["scale"] = s
                if a is not None:
                    kw["accum_out"] = a
                return q.activation(out=o, in_=i, func=fn, **kw)
            bld.em("scalar", f, sem, 1, selfwait=_narrow(out) or accum is not None)

        def vop(kind, *a, sem=None, eng="vector", **kw):
            out = a[0] if a else None
            bld.em(eng, lambda q, k=kind, aa=a, kk=kw: getattr(q, k)(*aa, **kk), sem, 1,
                   selfwait=_narrow(out))

        def sdma(out, in_, eng="scalar", sem=None):
            bld.em(eng, lambda q, o=out, i=in_: q.dma_start(out=o, in_=i),
                   S["sST"] if sem is None else sem, 16)

        for out, src in [(identf[:], ideni[:, :]), (cw[:], cwi[:, :]), (cb[:], cbi[:, :]),
                         (bvb[:], bvbi[:, :]), (brrow[:], bri[:, :]), (msk[:], mski[:, :]),
                         (bqk[:], bqkvi[:, :]), (wqkf[:], wqkvi[:, :])]:
            bld.dma("sync", out, src, S["sLD"])
        bld.wait("vector", S["sLD"])
        vop("tensor_copy", identb[:], identf[:], sem=S["sDV"])
        vop("memset", onesf[:], 1.0, sem=S["sDV"])
        vop("memset", onescol[:], 1.0, sem=S["sDV"])


        vop("memset", onesb[:], 1.0, sem=S["sDV"])
        vop("memset", epsv[:], 1e-5, sem=S["sDV"])
        vop("tensor_scalar", bq8[:, 0:1], bqk[:, 0:1], 0.125, 1.0,
            op0=OP.mult, op1=OP.mult, sem=S["sDV"])
        vop("tensor_copy", bq8[:, 1:2], bqk[:, 1:2], sem=S["sDV"])
        vop("tensor_copy", wqkb[:], wqkf[:], sem=S["sDV"])
        vop("memset", wv65[:], 0.0, sem=S["sDV"])
        vop("tensor_copy", brb[:], brrow[:], sem=S["sDV"])
        bld.wait("vector", S["sDV"])
        vop("tensor_copy", wv65[:, 0:64], wqkb[:, 128:192], sem=S["sDV"])

        for kc in range(8):
            bld.wait("sync", S["sDV"])
            bld.dma("sync", wrstg[:, :], wri[128 * kc:128 * (kc + 1), :], S["sLD"])
            bld.wait("vector", S["sLD"])
            vop("tensor_copy", wrB[:, 1024 * kc:1024 * (kc + 1)], wrstg[:, :], sem=S["sDV"])

        def wmean(wt, ntiles, wd, bank_i):
            absdone = []
            for qn in ("sync",):
                bld.wait(qn, S["sAC"])
                bld.wait(qn, S["sDV"])
            for i in range(ntiles):
                buf = wqa if i % 2 == 0 else wqb
                qn = "sync"
                if i >= 2:
                    bld.wait(qn, S["sAC"], absdone[i - 2])
                bld.dma(qn, buf[:, 0:wd], wt[i, :, :], SW[i % 2])
                bld.wait("scalar", SW[i % 2])
                for g in range(0, wd, 1024):
                    bld.wait("scalar", S["sPE"])
                    act(wabs[:, 0:1024], buf[:, g:g + 1024], AF.Abs, sem=S["sAC"])
                    bld.wait("tensor", S["sAC"])
                    for g2 in range(0, 1024, 512):
                        first = (i == 0 and g == 0 and g2 == 0)
                        last = (i == ntiles - 1 and g + 1024 == wd and g2 == 512)
                        mm(bank(bank_i)[0:1, :], onesb[:, 0:1],
                           wabs[:, g2:g2 + 512], first, last,
                           sem=S["sPE"])
                absdone.append(S["sAC"].n)
        wmean(w1t, 32, 1024, 0)
        wmean(w2t, 8, 4096, 1)

        bld.wait("vector", S["sPE"])
        vop("reduce_sum", scr1[0:1, 0:1], bank(0)[0:1, :], AX.X, sem=S["sDV"])
        vop("reduce_sum", scr1[0:1, 1:2], bank(1)[0:1, :], AX.X, sem=S["sDV"])
        vop("tensor_scalar", scr1[0:1, 2:3], scr1[0:1, 0:1], 1.0 / (DH * D), 1e-5,
            op0=OP.mult, op1=OP.max, sem=S["sDV"])
        vop("tensor_scalar", scr1[0:1, 3:4], scr1[0:1, 1:2], 1.0 / (DH * D), 1e-5,
            op0=OP.mult, op1=OP.max, sem=S["sDV"])
        vop("reciprocal", scr1[0:1, 4:5], scr1[0:1, 2:3], sem=S["sDV"])
        vop("reciprocal", scr1[0:1, 5:6], scr1[0:1, 3:4], sem=S["sDV"])
        bld.wait("tensor", S["sDV"])
        for j, col in enumerate([4, 2, 5, 3]):
            mm(bank(2)[:, j:j + 1], onesf[:, :], scr1[0:1, col:col + 1], True, True,
               sem=(S["sPE"] if j == 3 else None))
        bld.wait("vector", S["sPE"])
        vop("tensor_copy", wsc[:, 0:4], bank(2)[:, 0:4], sem=S["sDV"])

        def wquant(wt, md, ntiles, wd, wscol):
            qdone = []
            for qn in ("sync",):
                bld.wait(qn, S["sAC"])
                bld.wait(qn, S["sDV"])
                bld.wait(qn, S["sPL"])
            for i in range(ntiles):
                buf = wqa if i % 2 == 0 else wqb
                qn = "sync"
                if i >= 2:
                    bld.wait(qn, S["sDV"], qdone[i - 2])
                bld.dma(qn, buf[:, 0:wd], wt[i, :, :], SW[i % 2])
                bld.wait("vector", SW[i % 2])
                vop("tensor_scalar", wq8[:, 0:wd], buf[:, 0:wd], wsc[:, wscol:wscol + 1],
                    1.0, op0=OP.mult, op1=OP.mult, sem=S["sDV"])
                qdone.append(S["sDV"].n)
                bld.wait("vector", S["sST"])
                vop("tensor_scalar", wqm[:, 0:wd], wq8[:, 0:wd], -1.0, 1.0,
                    op0=OP.max, op1=OP.min, sem=S["sDV"])
                bld.wait("scalar", S["sDV"])
                sdma(md[i, :, :], wqm[:, 0:wd])
        bld.wait("vector", S["sDV"])
        wquant(w1t, m1d, 32, 1024, 0)
        wquant(w2t, m2d, 8, 4096, 2)

        vop("memset", xmsc[:, 0:1024], 0.0, sem=S["sDV"])
        bld.wait("scalar", S["sDV"])
        r = 2050
        while r < WP:
            nr = min(126, WP - r)
            sdma(x2d[r:r + nr, :], xmsc[0:nr, 0:1024])
            r += nr
        sdma(x1d[0:1, :], xmsc[0:1, 0:1024])
        sdma(x1d[2049:2050, :], xmsc[0:1, 0:1024])
        bld.wait("scalar", S["sST"])
        barrier()

        def ln_quant_entry(xsrc):
            P = NT
            SXE = [SX[0], S["sXG"]]
            a_dv = []
            a_ac = []
            a_pl = []
            for t in range(NT):
                slot = t % 2
                qn = "sync"
                if t >= 2:
                    bld.wait(qn, S["sDV"], a_dv[t - 2])
                    bld.wait(qn, S["sAC"], a_ac[t - 2])
                bld.dma(qn, xt[:, 1024 * slot:1024 * (slot + 1)],
                        xsrc[128 * t:128 * (t + 1), :], SXE[slot])
                xtile = xt[:, 1024 * slot:1024 * (slot + 1)]
                bld.wait("vector", SXE[slot])
                vop("reduce_sum", stA[:, 0 * P + t:0 * P + t + 1], xtile, AX.X, sem=S["sDV"])
                vop("tensor_scalar", stA[:, 2 * P + t:2 * P + t + 1],
                    stA[:, 0 * P + t:0 * P + t + 1], 1.0 / D, 1.0,
                    op0=OP.mult, op1=OP.mult, sem=S["sDV"])
                bld.wait("scalar", SXE[slot])
                act(sqsc[:, 0:1024], xtile, AF.Square,
                    accum=stA[:, 1 * P + t:1 * P + t + 1], sem=S["sAC"])
                a_ac.append(S["sAC"].n)
                vop("tensor_scalar", silub[:, 1024 * slot:1024 * (slot + 1)], xtile,
                    stA[:, 2 * P + t:2 * P + t + 1], 1.0,
                    op0=OP.subtract, op1=OP.mult, sem=S["sDV"])
                vop("reduce_max", stA[:, 5 * P + t:5 * P + t + 1],
                    silub[:, 1024 * slot:1024 * (slot + 1)], AX.X,
                    apply_absolute_value=True, sem=S["sDV"])
                a_dv.append(S["sDV"].n)
                a_pl.append(0)
            bld.wait("vector", S["sAC"])
            pl = lambda i: stA[:, i * P:(i + 1) * P]
            vop("tensor_tensor", pl(3), pl(2), pl(2), OP.mult, sem=S["sDV"])
            vop("tensor_scalar", pl(4), pl(1), 1.0 / D, 1.0, op0=OP.mult, op1=OP.mult, sem=S["sDV"])
            vop("tensor_tensor", pl(3), pl(4), pl(3), OP.subtract, sem=S["sDV"])
            bld.wait("scalar", S["sDV"])
            act(xmsc[:, 0:P], pl(3), AF.Ln, bias=epsv[:, 0:1], sem=S["sAC"])
            act(pl(4), xmsc[:, 0:P], AF.Exp, scale=-0.5, sem=S["sAC"])
            bld.wait("vector", S["sAC"])
            vop("tensor_tensor", pl(6), pl(4), pl(5), OP.mult, sem=S["sDV"])
            vop("tensor_scalar", pl(5), pl(6), 1e-5, 1.0, op0=OP.max, op1=OP.mult, sem=S["sDV"])
            vop("reciprocal", pl(6), pl(5), sem=S["sDV"])
            vop("tensor_tensor", pl(6), pl(6), pl(4), OP.mult, sem=S["sDV"])
            vop("tensor_scalar", pl(6), pl(6), 127.0, 1.0, op0=OP.mult, op1=OP.mult, sem=S["sDV"])
            vop("tensor_scalar", pl(7), pl(5), wsc[:, 1:2], 1.0 / 127.0,
                op0=OP.mult, op1=OP.mult, sem=S["sDV"])
            bld.wait("tensor", S["sDV"])
            bld.wait("tensor", S["sAC"])
            tr(ps[0:NT, 3584:3584 + 128], pl(7), identf[:], sem=S["sPE"])
            bld.wait("scalar", S["sPE"])
            act(o17[:, :], ps[0:NT, 3584:3584 + 128], AF.Copy, sem=S["sAC"])
            bld.wait("gpsimd", S["sAC"])
            bld.em("gpsimd", lambda q: q.dma_start(out=rowd[:, :], in_=o17[:, :]), S["sBC"], 16)
            bld.wait("gpsimd", S["sBC"])
            bld.em("gpsimd", lambda q: q.dma_start(
                out=oscrow[0:1, :], in_=rowd[:, :].rearrange("(o p) n -> o (p n)", o=1)),
                S["sBC"], 16)
            oscrow_ready = S["sBC"].n
            bld.wait("tensor", S["sAC"])
            trd = []
            b_pl = []
            b_cast = []
            b_trl = []
            for t in range(NT):
                slot = t % 2
                qn = "sync"
                if t >= 2:
                    bld.wait(qn, S["sDV"], b_pl[t - 2])
                else:
                    bld.wait(qn, S["sDV"], a_dv[16 - t])
                    bld.wait(qn, S["sAC"], a_ac[16 - t])
                bld.dma(qn, xt[:, 1024 * slot:1024 * (slot + 1)],
                        xsrc[128 * t:128 * (t + 1), :], SXE[slot])
                xtile = xt[:, 1024 * slot:1024 * (slot + 1)]
                bld.wait("vector", SXE[slot])
                if t >= 2:
                    bld.wait("vector", S["sAC"], b_cast[t - 2])
                vop("tensor_scalar", n28[:, 1024 * slot:1024 * (slot + 1)], xtile,
                    stA[:, 2 * P + t:2 * P + t + 1], stA[:, 6 * P + t:6 * P + t + 1],
                    op0=OP.subtract, op1=OP.mult, sem=S["sDV"])
                b_pl.append(S["sDV"].n)
                bld.wait("scalar", S["sDV"], b_pl[-1])
                if t >= 2:
                    bld.wait("scalar", S["sPE"], b_trl[t - 2])
                act(n2b[:, 1024 * slot:1024 * (slot + 1)],
                    n28[:, 1024 * slot:1024 * (slot + 1)], AF.Copy, sem=S["sAC"])
                b_cast.append(S["sAC"].n)
                bld.wait("tensor", S["sAC"], b_cast[-1])
                for k in range(8):
                    pslot = (ps[:, 3072:3072 + 256] if k % 2 == 0
                             else ps[:, 3584:3584 + 256]).bitcast(BF16)[:, 0:128]
                    if len(trd) >= 2:
                        bld.wait("tensor", trd[-2][0], trd[-2][1])
                    tr(pslot, n2b[:, 1024 * slot + 128 * k:1024 * slot + 128 * (k + 1)],
                       identb[:], sem=S["sPE"])
                    dst = nfm[:, WP * k + 128 * t:WP * k + 128 * (t + 1)]
                    if k % 2 == 0:
                        bld.wait("vector", S["sPE"])
                        vop("tensor_copy", dst, pslot, sem=S["sDV"])
                        trd.append((S["sDV"], S["sDV"].n))
                    else:
                        bld.wait("scalar", S["sPE"])
                        act(dst, pslot, AF.Copy, sem=S["sAC"])
                        trd.append((S["sAC"], S["sAC"].n))
                b_trl.append(S["sPE"].n)
            bld.wait("tensor", S["sDV"])
            bld.wait("tensor", S["sAC"])
            return oscrow_ready, S["sDV"].n, S["sAC"].n

        m1done = []
        m2done = []
        FPS = {"fd": [0] * 8, "resid": 0, "wave": [0] * 8, "hlast": 0, "xtst": [0, 0]}

        def ffn_call(xsrc, resdst, res_off):
            oscrow_ready, nfm_dv, nfm_ac = ln_quant_entry(xsrc)
            for si in range(NSP):
                o = 1 + SPW * si
                hw = SPW + 2
                bld.wait("tensor", S["sBC"], oscrow_ready)
                bld.wait("tensor", S["sAC"], FPS["fd"][6])
                mm(bank(6)[:, 0:hw], onesf[:, :], oscrow[0:1, o - 1:o - 1 + hw],
                   True, True, sem=S["sPE"])
                bld.wait("scalar", S["sPE"])
                act(oscb[:, 0:hw], bank(6)[:, 0:hw], AF.Copy, sem=S["sAC"])
                oscb_ready = S["sAC"].n
                bld.wait("gpsimd", S["sPE"], FPS["hlast"])
                if si == 0:
                    bld.wait("gpsimd", S["sDV"], nfm_dv)
                    bld.wait("gpsimd", S["sAC"], nfm_ac)
                for kk in range(8):
                    vop("tensor_copy", nbr[:, 258 * kk:258 * kk + hw],
                        nfm[:, WP * kk + o - 1:WP * kk + o - 1 + hw],
                        sem=S["sPL"], eng="gpsimd")
                nbr_ready = S["sPL"].n
                h_sem = []
                evg = []
                evA = []
                evC = []
                evS = []

                def emit_BC_silu(hc):
                    slot = hc % 2
                    gs = gb[:, 258 * slot:258 * slot + hw]
                    cs = c2b[:, 256 * slot:256 * (slot + 1)]
                    c3s = c3b[:, 256 * slot:256 * (slot + 1)]
                    bld.wait("vector", S["sPL"], evA[hc])
                    vop("scalar_tensor_tensor", cs, gs[:, 0:SPW],
                        cw[:, 4 * hc:4 * hc + 1], cs,
                        op0=OP.mult, op1=OP.add, sem=S["sDV"])
                    if hc >= 2:
                        bld.wait("vector", S["sAC"], evS[hc - 2])
                    vop("scalar_tensor_tensor", c3s, gs[:, 2:2 + SPW],
                        cw[:, 4 * hc + 2:4 * hc + 3], cs,
                        op0=OP.mult, op1=OP.add, sem=S["sDV"])
                    evC.append(S["sDV"].n)
                    bld.wait("scalar", S["sDV"], evC[hc])
                    act(silub[:, SPW * hc:SPW * hc + SPW], c3s, AF.Silu,
                        bias=cb[:, hc:hc + 1], sem=S["sAC"])
                    evS.append(S["sAC"].n)

                def emit_rmx(p):
                    bld.wait("vector", S["sAC"], evS[8 * p + 7])
                    vop("tensor_reduce", rmx4[:, 256 * p:256 * (p + 1)],
                        silub[:, 2048 * p:2048 * (p + 1)].rearrange(
                            "p (hc t) -> p t hc", hc=8),
                        AX.X, OP.max, apply_absolute_value=True, sem=S["sDV"])

                for hc in range(32):
                    mslot = len(m1done) % 4
                    if len(m1done) >= 4:
                        bld.wait("sync", S["sPE"], m1done[-4])
                    bld.dma("sync", m1sb[:, 1024 * mslot:1024 * (mslot + 1)],
                            m1d[hc, :, :], SM1[mslot])
                    hps = bank(hc % 2)[:, 0:hw]
                    bld.wait("tensor", SM1[mslot])
                    if hc == 0:
                        bld.wait("tensor", S["sPL"], nbr_ready)
                        bld.wait("tensor", S["sAC"], FPS["fd"][1])
                    if hc >= 2:
                        bld.wait("tensor", S["sDV"], evg[hc - 2])
                    for k in range(8):
                        mm(hps, m1sb[:, 1024 * mslot + 128 * k:1024 * mslot + 128 * (k + 1)],
                           nbr[:, 258 * k:258 * k + hw],
                           start=(k == 0), stop=(k == 7), sem=(S["sPE"] if k == 7 else None))
                    m1done.append(S["sPE"].n)
                    h_sem.append(S["sPE"].n)
                    bld.wait("vector", S["sPE"], h_sem[hc])
                    if hc == 0:
                        bld.wait("vector", S["sAC"], oscb_ready)
                    if hc >= 2:
                        bld.wait("vector", S["sPL"], evA[hc - 2])
                    vop("tensor_tensor", gb[:, 258 * (hc % 2):258 * (hc % 2) + hw],
                        hps, oscb[:, 0:hw], OP.mult, sem=S["sDV"])
                    evg.append(S["sDV"].n)
                    bld.wait("gpsimd", S["sDV"], evg[hc])
                    if hc >= 2:
                        bld.wait("gpsimd", S["sDV"], evC[hc - 2])
                    vop("tensor_scalar", c2b[:, 256 * (hc % 2):256 * (hc % 2 + 1)],
                        gb[:, 258 * (hc % 2) + 1:258 * (hc % 2) + 1 + SPW],
                        cw[:, 4 * hc + 1:4 * hc + 2], 1.0,
                        op0=OP.mult, op1=OP.mult, sem=S["sPL"], eng="gpsimd")
                    evA.append(S["sPL"].n)
                    if hc >= 1:
                        emit_BC_silu(hc - 1)
                    if hc >= 2 and (hc - 2) % 8 == 7:
                        emit_rmx((hc - 2) // 8)
                emit_BC_silu(31)
                emit_rmx(3)
                FPS["hlast"] = h_sem[-1]
                if fpart < 2:
                    if fpart < 1.2:
                        continue
                    pass
                vop("tensor_reduce", rmxf[:, 0:SPW],
                    rmx4[:, :].rearrange("p (g t) -> p t g", g=4),
                    AX.X, OP.max, sem=S["sDV"])
                rmx_dv = S["sDV"].n
                if fpart < 1.3:
                    continue
                for u in range(2):
                    cb0 = 16 * u
                    bld.wait("tensor", S["sDV"])
                    bld.wait("tensor", S["sAC"])
                    tr(ps[0:128, 3584:3584 + 128], rmxf[:, 128 * u:128 * (u + 1)],
                       identf[:, :], sem=S["sPE"])
                    bld.wait("vector", S["sPE"])
                    vop("reduce_max", xmsc[0:128, cb0:cb0 + 1],
                        ps[0:128, 3584:3584 + 128], AX.X, sem=S["sDV"])
                    if fpart < 1.5:
                        continue
                    vop("tensor_scalar", xmsc[0:128, cb0 + 1:cb0 + 2],
                        xmsc[0:128, cb0:cb0 + 1], 1e-5, 1.0,
                        op0=OP.max, op1=OP.mult, sem=S["sDV"])
                    vop("reciprocal", xmsc[0:128, cb0 + 2:cb0 + 3],
                        xmsc[0:128, cb0 + 1:cb0 + 2], sem=S["sDV"])
                    vop("tensor_scalar", xmsc[0:128, cb0 + 3:cb0 + 4],
                        xmsc[0:128, cb0 + 2:cb0 + 3], 127.0, 1.0,
                        op0=OP.mult, op1=OP.mult, sem=S["sDV"])
                    vop("tensor_scalar", o2c[0:128, u:u + 1],
                        xmsc[0:128, cb0 + 1:cb0 + 2], wsc[0:128, 3:4], 0.5 / 127.0,
                        op0=OP.mult, op1=OP.mult, sem=S["sDV"])
                    if fpart < 1.6:
                        continue
                    bld.wait("tensor", S["sDV"])
                    tr(ps[0:1, 3840:3840 + 128],
                       xmsc[0:128, cb0 + 3:cb0 + 4], identf[:], sem=S["sPE"])
                    bld.wait("scalar", S["sPE"])
                    act(s2row[0:1, 128 * u:128 * (u + 1)],
                        ps[0:1, 3840:3840 + 128], AF.Copy, sem=S["sAC"])
                bld.wait("tensor", S["sAC"])
                mm(bank(6)[:, 0:SPW], onesf[:, :], s2row[0:1, 0:SPW], True, True,
                   sem=S["sPE"])
                bld.wait("scalar", S["sPE"])
                act(s2b[:, 0:SPW], bank(6)[:, 0:SPW], AF.Copy, sem=S["sAC"])
                s2b_ready = S["sAC"].n
                if fpart < 3:
                    continue
                qv = []
                castv = []
                ocstop = [0] * 8
                for w in range(8):
                    m2slot = len(m2done) % 3
                    if len(m2done) >= 3:
                        bld.wait("gpsimd", S["sPE"], m2done[-3])
                    for oc in range(8):
                        bld.dma("gpsimd",
                                m2w[:, 4096 * m2slot + 512 * oc:4096 * m2slot + 512 * (oc + 1)],
                                m2d[oc, :, 512 * w:512 * (w + 1)], SM2[m2slot])
                    for j in range(4):
                        hc = 4 * w + j
                        if hc == 0:
                            bld.wait("vector", S["sAC"], s2b_ready)
                        if w >= 2 and j == 0:
                            bld.wait("vector", S["sAC"], castv[w - 2])
                        vop("tensor_tensor", n28[:, 256 * (hc % 8):256 * (hc % 8) + 256],
                            silub[:, SPW * hc:SPW * (hc + 1)], s2b[:, 0:SPW], OP.mult,
                            sem=S["sDV"])
                        qv.append(S["sDV"].n)
                    bld.wait("scalar", S["sDV"], qv[-1])
                    bld.wait("scalar", S["sPE"], FPS["wave"][w])
                    act(n2b[:, 1024 * w:1024 * (w + 1)],
                        n28[:, 1024 * (w % 2):1024 * (w % 2 + 1)], AF.Copy, sem=S["sAC"])
                    castv.append(S["sAC"].n)
                    if fpart < 4:
                        m2done.append(S["sPE"].n)
                        continue
                    bld.wait("tensor", SM2[m2slot])
                    bld.wait("tensor", S["sAC"], castv[w])
                    if w == 0:
                        bld.wait("tensor", S["sDV"], FPS["resid"])
                    if w < 7:
                        for j in range(4):
                            for oc in range(8):
                                mm(bank(oc)[:, 0:SPW],
                                   m2w[:, 4096 * m2slot + 512 * oc + 128 * j:
                                       4096 * m2slot + 512 * oc + 128 * (j + 1)],
                                   n2b[:, 256 * (4 * w + j):256 * (4 * w + j + 1)],
                                   start=(w == 0 and j == 0), stop=False,
                                   sem=(S["sPE"] if (j == 3 and oc == 7) else None))
                    else:
                        for oc in range(8):
                            for j in range(4):
                                mm(bank(oc)[:, 0:SPW],
                                   m2w[:, 4096 * m2slot + 512 * oc + 128 * j:
                                       4096 * m2slot + 512 * oc + 128 * (j + 1)],
                                   n2b[:, 256 * (28 + j):256 * (29 + j)],
                                   start=False, stop=(j == 3),
                                   sem=(S["sPE"] if j == 3 else None))
                            ocstop[oc] = S["sPE"].n
                    m2done.append(S["sPE"].n)
                    FPS["wave"][w] = S["sPE"].n
                if fpart < 4:
                    continue
                for oc in range(8):
                    bld.wait("scalar", S["sPE"], ocstop[oc])
                    act(fsb[:, 256 * oc:256 * (oc + 1)], bank(oc)[:, 0:SPW],
                        AF.Copy, sem=S["sAC"])
                    FPS["fd"][oc] = S["sAC"].n
                bld.wait("tensor", S["sAC"], FPS["fd"][7])
                for u in range(2):
                    xslot = u % 2
                    bld.wait("sync", S["sDV"])
                    bld.dma("sync", xt[0:128, 1024 * xslot:1024 * (xslot + 1)],
                            xsrc[o + 128 * u:o + 128 * u + 128, :], SX[xslot])
                    if u == 1:
                        bld.wait("tensor", S["sDV"], FPS["resid"])
                    for oc in range(8):
                        tr(ps[0:128, 2048 + 128 * oc:2048 + 128 * (oc + 1)],
                           fsb[:, 256 * oc + 128 * u:256 * oc + 128 * u + 128],
                           identf[:], sem=(S["sPE"] if oc == 7 else None))
                    bld.wait("vector", S["sPE"])
                    bld.wait("vector", SX[xslot])
                    bld.wait("vector", S["sT%d" % u], FPS["xtst"][u])
                    vop("scalar_tensor_tensor", xtm[0:128, 1024 * u:1024 * (u + 1)],
                        ps[0:128, 2048:3072], o2c[0:128, u:u + 1],
                        xt[0:128, 1024 * xslot:1024 * (xslot + 1)],
                        op0=OP.mult, op1=OP.add, sem=S["sDV"])
                    FPS["resid"] = S["sDV"].n
                    bld.wait("scalar", S["sDV"], FPS["resid"])
                    sdma(resdst[o + res_off + 128 * u:o + res_off + 128 * u + 128, :],
                         xtm[0:128, 1024 * u:1024 * (u + 1)], sem=S["sT%d" % u])
                    FPS["xtst"][u] = S["sT%d" % u].n
            bld.wait("scalar", S["sST"])
            bld.wait("scalar", S["sT0"])
            bld.wait("scalar", S["sT1"])

        def finish_debug(srcap):
            bld.em("gpsimd", lambda q, s=srcap: q.dma_start(out=xout[:, :], in_=s),
                   S["sBC"], 16)
            bld.wait("gpsimd", S["sBC"])

        if upto == "pre":
            finish_debug(x2d[0:2048, :])
        if upto in ("full", "ffn1", "ln2", "attn", "wr"):
            ffn_call(xin, x1d, 0)
            barrier()
        if upto == "ffn1":
            finish_debug(x1d[1:2049, :])

        RUN_LN2 = upto in ("full", "ln2", "attn", "wr")
        if RUN_LN2:
            P = NT
            for t in range(16):
                slot = t % 2
                if t >= 2:
                    bld.wait("sync", S["sDV"])
                    bld.wait("sync", S["sAC"])
                bld.dma("sync", xt[:, 1024 * slot:1024 * (slot + 1)],
                        x1d[1 + 128 * t:1 + 128 * (t + 1), :], SX[slot])
                xtile = xt[:, 1024 * slot:1024 * (slot + 1)]
                bld.wait("vector", SX[slot])
                vop("reduce_sum", stA[:, t:t + 1], xtile, AX.X, sem=S["sDV"])
                bld.wait("scalar", SX[slot])
                act(sqsc[:, 0:1024], xtile, AF.Square, accum=stA[:, P + t:P + t + 1], sem=S["sAC"])
                vop("tensor_scalar", stA[:, 2 * P + t:2 * P + t + 1], stA[:, t:t + 1],
                    1.0 / D, 1.0, op0=OP.mult, op1=OP.mult, sem=S["sDV"])
            bld.wait("vector", S["sAC"])
            pl = lambda i: stA[:, i * P:i * P + 16]
            vop("tensor_tensor", pl(3), pl(2), pl(2), OP.mult, sem=S["sDV"])
            vop("tensor_scalar", pl(4), pl(1), 1.0 / D, 1.0, op0=OP.mult, op1=OP.mult, sem=S["sDV"])
            vop("tensor_tensor", pl(3), pl(4), pl(3), OP.subtract, sem=S["sDV"])
            bld.wait("scalar", S["sDV"])
            act(xmsc[:, 0:16], pl(3), AF.Ln, bias=epsv[:, 0:1], sem=S["sAC"])
            act(pl(4), xmsc[:, 0:16], AF.Exp, scale=-0.5, sem=S["sAC"])
            bld.wait("vector", S["sAC"])
            y2done = []
            for t in range(16):
                slot = t % 2
                bld.wait("sync", S["sDV"])
                bld.wait("sync", S["sAC"])
                bld.dma("sync", xt[:, 1024 * slot:1024 * (slot + 1)],
                        x1d[1 + 128 * t:1 + 128 * (t + 1), :], SX[slot])
                xtile = xt[:, 1024 * slot:1024 * (slot + 1)]
                bld.wait("vector", SX[slot])
                if t >= 1:
                    bld.wait("vector", S["sPE"])
                vop("tensor_scalar", ntmb[:, 0:1024], xtile, stA[:, 2 * P + t:2 * P + t + 1],
                    stA[:, 4 * P + t:4 * P + t + 1], op0=OP.subtract, op1=OP.mult, sem=S["sDV"])
                bld.wait("tensor", S["sDV"])
                for k in range(8):
                    pslot = (ps[:, 3072:3072 + 256] if k % 2 == 0
                             else ps[:, 3584:3584 + 256]).bitcast(BF16)[:, 0:128]
                    if len(y2done) >= 2:
                        bld.wait("tensor", S["sDV"], y2done[-2])
                    tr(pslot, ntmb[:, 128 * k:128 * (k + 1)], identb[:], sem=S["sPE"])
                    bld.wait("vector", S["sPE"])
                    dst = y2fm if k < 4 else y2own
                    kk = k % 4
                    vop("tensor_copy", dst[:, TOK * kk + 128 * t:TOK * kk + 128 * (t + 1)],
                        pslot, sem=S["sDV"])
                    y2done.append(S["sDV"].n)
            bld.wait("vector", S["sDV"])
            for ch in range(4):
                lo = y2fm[:, TOK * ch:TOK * (ch + 1)]
                hi = y2own[:, TOK * ch:TOK * (ch + 1)]
                vop("tensor_scalar", mrg[:, 0:TOK], hi, msk[:, 1:2], 1.0,
                    op0=OP.mult, op1=OP.mult, sem=S["sDV"])
                vop("scalar_tensor_tensor", mrg[:, 0:TOK], lo, msk[:, 0:1], mrg[:, 0:TOK],
                    op0=OP.mult, op1=OP.add, sem=S["sDV"])
                bld.wait("scalar", S["sDV"])
                sdma(y2ci[128 * ch:128 * (ch + 1), :], mrg[:, 0:TOK])
                vop("tensor_scalar", mrg[:, TOK:2 * TOK], lo, msk[:, 1:2], 1.0,
                    op0=OP.mult, op1=OP.mult, sem=S["sDV"])
                vop("scalar_tensor_tensor", mrg[:, TOK:2 * TOK], hi, msk[:, 0:1],
                    mrg[:, TOK:2 * TOK], op0=OP.mult, op1=OP.add, sem=S["sDV"])
            bld.wait("vector", S["sST"])
            for ch in range(4):
                lo = y2fm[:, TOK * ch:TOK * (ch + 1)]
                hi = y2own[:, TOK * ch:TOK * (ch + 1)]
                vop("tensor_scalar", sqsc[:, 0:1024].rearrange("p n -> p n") if False else mrg[:, 0:TOK],
                    lo, msk[:, 1:2], 1.0, op0=OP.mult, op1=OP.mult, sem=S["sDV"])
                vop("scalar_tensor_tensor", y2own[:, TOK * ch:TOK * (ch + 1)], hi, msk[:, 0:1],
                    mrg[:, 0:TOK], op0=OP.mult, op1=OP.add, sem=S["sDV"])
            bld.wait("gpsimd", S["sST"])
            bld.wait("gpsimd", S["sDV"])
            bld.em("gpsimd", lambda q: q.collective_compute(
                "AllGather", OP.bypass, replica_groups=[[0, 1], [2, 3], [4, 5], [6, 7]],
                ins=[y2ci[:, :]], outs=[y2go[:, :]]), S["sCC"], 1)
            bld.wait("sync", S["sCC"])
            for half in range(2):
                row0 = 0 if half == 0 else 512
                for ch in range(4):
                    bld.wait("sync", S["sDV"])
                    bld.dma("sync", mrg[:, 0:TOK],
                            y2go[row0 + 128 * ch:row0 + 128 * (ch + 1), :], SX[0])
                    own = y2own[:, TOK * ch:TOK * (ch + 1)]
                    dst = y2hh[:, SEQ * ch + TOK * half:SEQ * ch + TOK * (half + 1)]
                    bld.wait("vector", SX[0])
                    if half == 0:
                        vop("tensor_scalar", mgs[:, 0:TOK], own, msk[:, 1:2], 1.0,
                            op0=OP.mult, op1=OP.mult, sem=S["sDV"])
                        vop("scalar_tensor_tensor", dst, mrg[:, 0:TOK], msk[:, 0:1],
                            mgs[:, 0:TOK], op0=OP.mult, op1=OP.add, sem=S["sDV"])
                    else:
                        vop("tensor_scalar", mgs[:, 0:TOK], mrg[:, 0:TOK], msk[:, 1:2], 1.0,
                            op0=OP.mult, op1=OP.mult, sem=S["sDV"])
                        vop("scalar_tensor_tensor", dst, own, msk[:, 0:1],
                            mgs[:, 0:TOK], op0=OP.mult, op1=OP.add, sem=S["sDV"])
            barrier()

        if upto == "ln2":
            bld.em("gpsimd", lambda q: q.dma_start(
                out=xout[0:1024, :].bitcast(BF16), in_=y2go[:, :]), S["sBC"], 16)
            bld.wait("gpsimd", S["sBC"])
        if upto in ("full", "attn", "wr"):
            pav = ps[0:65, 3072:3584]
            pzb = ps[0:64, 3584:3584 + 512]
            ptmp = ps[0:1, 3584:3584 + 65]
            rot_expdone = [0, 0]
            fills_done = [0, 0]
            qkdone = [0, 0]
            av_guard = 0
            for k in range(nheads):
                ch4 = k // 2
                pbase = 64 * (k % 2)

                def yh(c0, w_, ch4=ch4, pbase=pbase):
                    return y2hh[pbase:pbase + 64, SEQ * ch4 + c0:SEQ * ch4 + c0 + w_]

                for which, (wsl0, wsl1, bcol, scl) in enumerate(
                        [(0, 64, 0, 0.125), (64, 128, 1, None)]):
                    for cq in range(8):
                        s = (which * 8 + cq) % 2
                        pqk = ps[0:64, 1536 * s:1536 * s + 512]
                        bld.wait("tensor", S["sAC"], qkdone[s])
                        if which == 0 and cq == 0:
                            bld.wait("tensor", S["sDV"], av_guard)
                            bld.wait("tensor", S["sAC"], max(rot_expdone))
                        mm(pqk, wqkb[pbase:pbase + 64, wsl0:wsl1], yh(512 * cq, 512),
                           True, True, sem=S["sPE"])
                        bld.wait("scalar", S["sPE"])
                        act(qk2[0:64, SEQ * which + 512 * cq:SEQ * which + 512 * (cq + 1)], pqk,
                            AF.Identity, bias=bq8[pbase:pbase + 64, bcol:bcol + 1], scale=scl,
                            sem=S["sAC"])
                        qkdone[s] = S["sAC"].n
                vext_u = vext
                Ebuf_u = Ebuf
                vdone = [0, 0]
                for kb in range(32):
                    s = kb % 2
                    pv = ps[:, 512 + 512 * s:512 + 512 * s + 65]
                    bld.wait("tensor", S["sDV"], vdone[s])
                    mm(pv, yh(128 * kb, 128), wv65[pbase:pbase + 64, :], True, True,
                       sem=S["sPE"])
                    bld.wait("vector", S["sPE"])
                    vop("tensor_tensor", vext_u[:, 65 * kb:65 * (kb + 1)], pv, bvb[:, :],
                        OP.add, sem=S["sDV"])
                    vdone[s] = S["sDV"].n
                bld.wait("tensor", S["sDV"])
                bld.wait("tensor", S["sAC"])
                chunk_range = [] if attn_level == 0 else list(range(7, -1, -1))
                for ci, c in enumerate(chunk_range):
                    nblk = 4 * c + 4
                    glist = [(g0, min(3, nblk - g0)) for g0 in range(0, nblk, 3)]
                    first_av = True
                    for gi, (g0, gn) in enumerate(glist):
                        r = gi % 2
                        rot = r * 1536
                        bld.wait("tensor", S["sAC"], rot_expdone[r])
                        for j in range(gn):
                            b = g0 + j
                            mm(ps[0:128, rot + 512 * j:rot + 512 * j + 512],
                               qk2[0:64, SEQ + 128 * b:SEQ + 128 * (b + 1)],
                               qk2[0:64, 512 * c:512 * (c + 1)], True, True,
                               sem=(S["sPE"] if j == gn - 1 else None))
                        bld.wait("scalar", S["sPE"])
                        bld.wait("scalar", S["sPL"], fills_done[r])
                        act(Ebuf_u[:, 1536 * r:1536 * r + 512 * gn],
                            ps[:, rot:rot + 512 * gn], AF.Exp, sem=S["sAC"])
                        rot_expdone[r] = S["sAC"].n
                        nfills = 0
                        for j in range(gn):
                            b = g0 + j
                            if b >= 4 * c:
                                if nfills == 0:
                                    bld.wait("gpsimd", S["sAC"], rot_expdone[r])
                                eap = Ebuf_u[:, 1536 * r + 512 * j:1536 * r + 512 * (j + 1)]
                                bld.em("gpsimd", lambda q, eap=eap, bb=b, cc=c:
                                       q.affine_select(eap, eap, [[1, 512]], OP.is_ge,
                                                       fillreg["one"],
                                                       base=512 * cc - 128 * bb - 1,
                                                       channel_multiplier=-1),
                                       S["sPL"], 1)
                                nfills += 1
                        if nfills:
                            fills_done[r] = S["sPL"].n
                            bld.wait("tensor", S["sPL"], fills_done[r])
                        if attn_level < 2:
                            continue
                        if first_av:
                            bld.wait("tensor", S["sDV"], av_guard)
                            first_av = False
                        for j in range(gn):
                            b = g0 + j
                            stop = (c == 7 and b == nblk - 1)
                            mm(pav, vext_u[:, 65 * b:65 * (b + 1)],
                               Ebuf_u[:, 1536 * r + 512 * j:1536 * r + 512 * (j + 1)],
                               start=(b == 0), stop=stop,
                               sem=(S["sPE"] if stop else None))
                    if attn_level < 2:
                        av_guard = S["sDV"].n
                        continue
                    if c < 7:
                        mm(pav, vsufB[0:1, 0:65], onesb[0:1, 0:512], start=False, stop=True,
                           sem=S["sPE"])
                    if attn_level < 3:
                        bld.wait("vector", S["sPE"])
                        vop("tensor_copy", aosb[0:64, 0:512], ps[0:64, 3072:3584], sem=S["sDV"])
                        av_guard = S["sDV"].n
                        continue
                    bld.wait("vector", S["sPE"])
                    vop("reciprocal", rcpz[0:1, 0:512], ps[64:65, 3072:3584], sem=S["sDV"])
                    vop("tensor_copy", rcpb[0:1, 0:512], rcpz[0:1, 0:512], sem=S["sDV"])
                    vop("scalar_tensor_tensor", rcpb[0:1, 512:1024], rcpb[0:1, 0:512], -1.0,
                        rcpz[0:1, 0:512], op0=OP.mult, op1=OP.add, sem=S["sDV"])
                    bld.wait("tensor", S["sDV"])
                    mm(pzb, onesb[0:1, 0:64], rcpb[0:1, 0:512], True, False)
                    mm(pzb, onesb[0:1, 0:64], rcpb[0:1, 512:1024], False, True, sem=S["sPE"])
                    bld.wait("vector", S["sPE"])
                    aslot = aosb[0:64, 512 * (ci % 2):512 * (ci % 2) + 512]
                    bld.wait("vector", S["sST"])
                    vop("tensor_copy", aslot, ps[0:64, 3072:3584], sem=S["sDV"])
                    bld.wait("vector", S["sDV"])
                    vop("tensor_tensor", aslot, aslot, pzb, OP.mult, sem=S["sDV"])
                    zb_read = S["sDV"].n
                    bld.wait("scalar", S["sDV"])
                    sdma(aod[64 * k:64 * (k + 1), 512 * c:512 * (c + 1)], aslot)
                    if c > 0:
                        bld.wait("tensor", S["sDV"], zb_read)
                        for j in range(4):
                            b = 4 * c + j
                            mm(ptmp, onesb[:, 0:1], vext_u[:, 65 * b:65 * (b + 1)],
                               start=(j == 0), stop=(j == 3), sem=(S["sPE"] if j == 3 else None))
                        bld.wait("vector", S["sPE"])
                        if c == 7:
                            vop("tensor_copy", vsufA[0:1, 0:65], ptmp, sem=S["sDV"])
                        else:
                            vop("tensor_tensor", vsufA[0:1, 0:65], vsufA[0:1, 0:65], ptmp,
                                OP.add, sem=S["sDV"])
                        vop("tensor_copy", vsufB[0:1, 0:65], vsufA[0:1, 0:65], sem=S["sDV"])
                    av_guard = S["sDV"].n
            bld.wait("tensor", S["sDV"], av_guard)
            bld.wait("scalar", S["sST"])
            barrier()

        if upto == "attn":
            bld.em("gpsimd", lambda q: q.dma_start(
                out=xout[0:1024, :].bitcast(BF16),
                in_=aod[:, :].rearrange("p (a n) -> (p a) n", a=2)), S["sBC"], 16)
            bld.wait("gpsimd", S["sBC"])
        if upto in ("full", "wr"):
            cA = lambda ch: candA[:, 2050 * ch:2050 * (ch + 1)]
            cB = lambda ch: candB[:, 2050 * ch:2050 * (ch + 1)]
            ow = lambda ch: ownw[:, 2050 * ch:2050 * (ch + 1)]
            for ch in range(4):
                vop("memset", cA(ch)[:, 0:1], 0.0, sem=S["sDV"])
                vop("memset", cB(ch)[:, 2049:2050], 0.0, sem=S["sDV"])
            bld.wait("sync", S["sDV"])
            for ch in range(4):
                bld.dma("sync", cA(ch)[:, 1:2050], aod[128 * ch:128 * (ch + 1), 0:2049], SX[0])
                bld.dma("sync", cB(ch)[:, 0:2049], aod[128 * ch:128 * (ch + 1), 2047:4096], SX[0])
            bld.wait("vector", SX[0])
            for ch in range(4):
                vop("tensor_scalar", ow(ch), cB(ch), msk[:, 1:2], 1.0,
                    op0=OP.mult, op1=OP.mult, sem=S["sDV"])
                vop("scalar_tensor_tensor", ow(ch), cA(ch), msk[:, 0:1], ow(ch),
                    op0=OP.mult, op1=OP.add, sem=S["sDV"])
                bld.wait("scalar", S["sDV"])
                sdma(aoci[128 * ch:128 * (ch + 1), :], ow(ch))
            bld.dma("sync", hbf[0:1, 0:2048].bitcast(F32), x1d[1:2, :], SX[0])
            bld.dma("sync", hbf[0:1, 2048:4096].bitcast(F32), x1d[2048:2049, :], SX[0])
            bld.wait("scalar", SX[0])
            sdma(aoci[512:513, 0:2048], hbf[0:1, 0:2048])
            sdma(aoci[513:514, 0:2048], hbf[0:1, 2048:4096])
            bld.wait("gpsimd", S["sST"])
            bld.em("gpsimd", lambda q: q.collective_compute(
                "AllGather", OP.bypass, replica_groups=[[0, 1], [2, 3], [4, 5], [6, 7]],
                ins=[aoci[:, :]], outs=[aogo[:, :]]), S["sCC"], 1)
            bld.wait("vector", S["sST"])
            for ch in range(4):
                vop("tensor_scalar", ow(ch), cA(ch), msk[:, 1:2], 1.0,
                    op0=OP.mult, op1=OP.mult, sem=S["sDV"])
                vop("scalar_tensor_tensor", ow(ch), cB(ch), msk[:, 0:1], ow(ch),
                    op0=OP.mult, op1=OP.add, sem=S["sDV"])
            bld.wait("sync", S["sCC"])
            bld.wait("sync", S["sDV"])
            for ch in range(4):
                bld.dma("sync", cA(ch), aogo[128 * ch:128 * (ch + 1), 0:2050], SX[0])
                bld.dma("sync", cB(ch), aogo[514 + 128 * ch:514 + 128 * (ch + 1), 0:2050], SX[0])
            bld.wait("vector", SX[0])
            for ch in range(4):
                vop("tensor_scalar", cB(ch), cB(ch), msk[:, 1:2], 1.0,
                    op0=OP.mult, op1=OP.mult, sem=S["sDV"])
                vop("scalar_tensor_tensor", cB(ch), cA(ch), msk[:, 0:1], cB(ch),
                    op0=OP.mult, op1=OP.add, sem=S["sDV"])
            for ch in range(4):
                vop("tensor_scalar", cA(ch), ow(ch), msk[:, 1:2], 1.0,
                    op0=OP.mult, op1=OP.mult, sem=S["sDV"])
                vop("scalar_tensor_tensor", awin[:, 2050 * ch:2050 * (ch + 1)], cB(ch),
                    msk[:, 0:1], cA(ch), op0=OP.mult, op1=OP.add, sem=S["sDV"])
            for ch in range(4):
                vop("tensor_scalar", cA(ch), cB(ch), msk[:, 1:2], 1.0,
                    op0=OP.mult, op1=OP.mult, sem=S["sDV"])
                vop("scalar_tensor_tensor", awin[:, 2050 * (ch + 4):2050 * (ch + 5)],
                    ow(ch), msk[:, 0:1], cA(ch), op0=OP.mult, op1=OP.add, sem=S["sDV"])
            bld.wait("sync", S["sDV"])
            bld.dma("sync", hbf[0:1, 0:2048], aogo[1026:1027, 0:2048], SX[0])
            bld.dma("sync", hbf[0:1, 2048:4096], aogo[513:514, 0:2048], SX[0])
            bld.wait("vector", SX[0])
            vop("tensor_scalar", halo[0:1, 0:1024], hbf[0:1, 0:2048].bitcast(F32),
                msk[0:1, 1:2], 1.0, op0=OP.mult, op1=OP.mult, sem=S["sDV"])
            vop("scalar_tensor_tensor", halo[0:1, 0:1024], hbf[0:1, 2048:4096].bitcast(F32),
                msk[0:1, 0:1], halo[0:1, 0:1024], op0=OP.mult, op1=OP.add, sem=S["sDV"])
            wrdone = []
            for si, (o, width) in enumerate(WR_SPANS):
                nu = (width + 127) // 128
                for oc in range(8):
                    fps = bank(2 + oc % 2)[:, 0:width]
                    if len(wrdone) >= 2:
                        bld.wait("tensor", S["sAC"], wrdone[-2])
                    if si == 0 and oc == 0:
                        bld.wait("tensor", S["sDV"])
                    for kc in range(8):
                        mm(fps, wrB[:, 1024 * kc + 128 * oc:1024 * kc + 128 * (oc + 1)],
                           awin[:, 2050 * kc + o:2050 * kc + o + width],
                           start=(kc == 0), stop=False)
                    mm(fps, brb[0:1, 128 * oc:128 * (oc + 1)], onesb[0:1, 0:width],
                       start=False, stop=True, sem=S["sPE"])
                    bld.wait("scalar", S["sPE"])
                    act(wfs[:, 384 * oc:384 * oc + width], fps, AF.Copy, sem=S["sAC"])
                    wrdone.append(S["sAC"].n)
                bld.wait("tensor", S["sAC"])
                for u in range(nu):
                    blk = min(128, width - 128 * u)
                    w0 = o + 128 * u
                    bld.wait("sync", S["sDV"])
                    bld.dma("sync", wxt[0:blk, 0:1024], x1d[w0:w0 + blk, :], SX[0])
                    pT = ps[:, 2048:3072]
                    bld.wait("tensor", S["sDV"])
                    for oc in range(8):
                        tr(pT[0:blk, 128 * oc:128 * (oc + 1)],
                           wfs[:, 384 * oc + 128 * u:384 * oc + 128 * u + blk],
                           identf[:], sem=(S["sPE"] if oc == 7 else None))
                    bld.wait("vector", S["sPE"])
                    bld.wait("vector", SX[0])
                    bld.wait("vector", S["sST"])
                    vop("tensor_tensor", wxm[0:blk, 0:1024], pT[0:blk, 0:1024],
                        wxt[0:blk, 0:1024], OP.add, sem=S["sDV"])
                    bld.wait("scalar", S["sDV"])
                    sdma(x2d[w0:w0 + blk, :], wxm[0:blk, 0:1024])
            for wrow, mcol in [(0, 0), (2049, 1)]:
                bld.wait("sync", S["sST"])
                bld.dma("sync", wxm[0:1, 0:1024], x2d[wrow:wrow + 1, :], SX[0])
                bld.wait("vector", SX[0])
                bld.wait("vector", S["sST"])
                vop("tensor_tensor", wxm[0:1, 0:1024], wxm[0:1, 0:1024], halo[0:1, :],
                    OP.add, sem=S["sDV"])
                vop("tensor_scalar", wxm[0:1, 0:1024], wxm[0:1, 0:1024], msk[0:1, mcol:mcol + 1],
                    1.0, op0=OP.mult, op1=OP.mult, sem=S["sDV"])
                bld.wait("scalar", S["sDV"])
                sdma(x2d[wrow:wrow + 1, :], wxm[0:1, 0:1024])
            bld.wait("scalar", S["sST"])
            barrier()

        if upto == "wr":
            finish_debug(x2d[0:2048, :])
        if upto == "full":
            ffn_call(x2d, xout, -1)
        bld.wait("gpsimd", S["sST"])

        @block.sync
        def _(q):
            for f in bld.ev["sync"]:
                f(q)

        @block.tensor
        def _(q):
            for f in bld.ev["tensor"]:
                f(q)

        @block.vector
        def _(q):
            for f in bld.ev["vector"]:
                f(q)

        @block.scalar
        def _(q):
            for f in bld.ev["scalar"]:
                f(q)

        @block.gpsimd
        def _(q):
            for f in bld.ev["gpsimd"]:
                f(q)

    return nc


_NC_CACHE = None


def _get_nc():
    global _NC_CACHE
    if _NC_CACHE is None:
        _NC_CACHE = build_program()
    return _NC_CACHE


def kernel(L0_attn_x, L0_attn_Wqkv, L0_attn_bqkv, L0_attn_Wr, L0_attn_br,
           L0_ffn_w1, L0_ffn_conv_w, L0_ffn_conv_b, L0_ffn_w2):
    x = np.asarray(L0_attn_x, dtype=np.float32)
    w1 = np.asarray(L0_ffn_w1, dtype=np.float32)
    w2 = np.asarray(L0_ffn_w2, dtype=np.float32)
    conv_w = np.asarray(L0_ffn_conv_w, dtype=np.float32)
    conv_b = np.asarray(L0_ffn_conv_b, dtype=np.float32)
    wqkv = np.asarray(L0_attn_Wqkv, dtype=np.float32)
    bqkv = np.asarray(L0_attn_bqkv, dtype=np.float32)
    wr = np.asarray(L0_attn_Wr, dtype=np.float32)
    br = np.asarray(L0_attn_br, dtype=np.float32)

    w1tiles = np.ascontiguousarray(
        w1.reshape(32, 128, 8, 128).transpose(0, 3, 2, 1).reshape(32, 128, 1024))
    w2tiles = np.ascontiguousarray(
        w2.reshape(8, 128, 32, 128).transpose(0, 3, 2, 1).reshape(8, 128, 4096))
    cwh = np.zeros((128, 128), np.float32)
    cwh.reshape(128, 32, 4)[:, :, 0:3] = conv_w[:, 0, :].reshape(32, 128, 3).transpose(1, 0, 2)
    cbh = np.ascontiguousarray(conv_b.reshape(32, 128).T)
    bqkvh = np.ascontiguousarray(np.tile(bqkv.reshape(3, 64).T, (2, 1)))
    wqkvh = np.ascontiguousarray(np.tile(wqkv, (2, 1)))
    bvbh = np.ones((128, 65), np.float32)
    bvbh[:, 0:64] = bqkv[128:192][None, :]
    brh = np.ascontiguousarray(br.reshape(1, 1024))
    iden = np.eye(128, dtype=np.float32)

    in_maps = []
    for c in range(8):
        b, p = c // 2, c % 2
        xw = np.zeros((WP, D), np.float32)
        xw[1:2049] = x[b, 2048 * p:2048 * p + 2048]
        if p == 0:
            xw[2049] = x[b, 2048]
        else:
            xw[0] = x[b, 2047]
        mskh = np.zeros((128, 4), np.float32)
        mskh[:, 0] = float(p)
        mskh[:, 1] = 1.0 - float(p)
        in_maps.append({
            "xin": xw, "w1t": w1tiles, "w2t": w2tiles, "cwi": cwh, "cbi": cbh,
            "wqkvi": wqkvh, "bqkvi": bqkvh, "bvbi": bvbh, "wri": wr, "bri": brh,
            "ideni": iden, "mski": mskh,
        })

    nc = _get_nc()
    res = run_bass_kernel_spmd(nc, in_maps, core_ids=list(range(8)))
    out = np.empty((4, 4096, 1024), np.float32)
    for c in range(8):
        b, p = c // 2, c % 2
        out[b, 2048 * p:2048 * p + 2048] = res.results[c]["xout"]
    return out
